# version 16
# baseline (speedup 1.0000x reference)
import os
import numpy as np
from contextlib import ExitStack

import concourse.bass as bass
import concourse.mybir as mybir
from concourse.bass_utils import run_bass_kernel_spmd

F32 = mybir.dt.float32
BF16 = mybir.dt.bfloat16
I32 = mybir.dt.int32
FP16 = mybir.dt.float16
AF = mybir.ActivationFunctionType
ALU = mybir.AluOpType
AX = mybir.AxisListType

T, E, H, HD, FF = 2048, 1024, 16, 64, 3072
NEC = E // 128
NTB = T // 128
TWOPI = float(2.0 * np.pi)
KSTOP = os.environ.get("KSTOP", "full")


class Sem:
    def __init__(self, ctx, nc, name):
        self.h = ctx.enter_context(nc.semaphore(name))
        self.n = 0


class Emitter:
    ENGS = ("pe", "dve", "act", "pool", "sp")

    def __init__(self):
        self.ops = {e: [] for e in self.ENGS}

    def em(self, eng, fn, waits=(), inc=None, drain=None, is_dma=False):
        if drain is None:
            drain = eng in ("dve", "act", "pool")
        w = tuple((s, v) for (s, v) in waits if v > 0)
        r = None
        if inc is not None:
            s, k = inc
            s.n += k
            r = s.n
        self.ops[eng].append((w, fn, inc, r, drain, is_dma))
        return r

    def run(self, eng, engobj):
        compute = eng in ("dve", "act", "pool")
        for w, fn, inc, _r, drain, is_dma in self.ops[eng]:
            for s, v in w:
                engobj.wait_ge(s.h, v)
            if drain:
                engobj.drain()
            inst = fn(engobj)
            if inc is not None:
                if compute and not is_dma:
                    engobj.drain().then_inc(inc[0].h, inc[1])
                else:
                    inst.then_inc(inc[0].h, inc[1])


def build_nc():
    nc = bass.Bass("TRN2", target_bir_lowering=False)

    xT = nc.dram_tensor("x", [E, T], F32, kind="ExternalInput").ap()
    wT = {nm: nc.dram_tensor(nm + "T", [E, E], F32, kind="ExternalInput").ap()
          for nm in ("wq", "wk", "wv", "wg", "wo")}
    wdT = nc.dram_tensor("wdT", [E, H], F32, kind="ExternalInput").ap()
    bfm_names = ("bq", "bk", "bg", "bo", "gamma", "beta")
    bfm = {nm: nc.dram_tensor(nm, [128, NEC], F32, kind="ExternalInput").ap()
           for nm in bfm_names}
    bv_in = nc.dram_tensor("bv", [1, E], F32, kind="ExternalInput").ap()
    bd_in = nc.dram_tensor("bd", [H, 1], F32, kind="ExternalInput").ap()
    wupNs = nc.dram_tensor("wupNs", [384, E], F32, kind="ExternalInput").ap()
    wgaNs = nc.dram_tensor("wgaNs", [384, E], F32, kind="ExternalInput").ap()
    wdoNs = nc.dram_tensor("wdoNs", [128, FF], F32, kind="ExternalInput").ap()
    yout = nc.dram_tensor("y", [E, T], F32, kind="ExternalOutput").ap()
    KDBG = os.environ.get("KDBG")
    if KDBG:
        dbg = {nm: nc.dram_tensor("dbg_" + nm, shp, F32, kind="ExternalOutput").ap()
               for nm, shp in (("y0", [128, T]), ("ld", [H, 1]), ("qf", [64, 2 * T]),
                               ("kf", [64, 2 * T]), ("vp", [128, T]), ("og0", [128, T]),
                               ("mneg", [128, 2]), ("cos", [64, T]), ("sin", [64, T]),
                               ("gate", [128, T]), ("og0p", [128, T]),
                               ("pf", [65, 1024]), ("pp", [65, 1024]),
                               ("qf0", [64, 2 * T]), ("kf0", [64, 2 * T]),
                               ("vp0", [128, T]), ("psq0", [128, T]),
                               ("qpre0", [64, 2 * T]), ("wbfq0", [128, E]))}
        dbg.update({f"yc{c}": nc.dram_tensor(f"dbg_yc{c}", [128, T], F32,
                                             kind="ExternalOutput").ap()
                    for c in range(8)})

    d_row = nc.dram_tensor("d_row", [4, T], F32)
    d_ld = nc.dram_tensor("d_ld", [H, 1], F32)
    d_c16 = nc.dram_tensor("d_c16", [H, T], F32)
    d_r16 = nc.dram_tensor("d_r16", [H, T], F32)
    d_sc = nc.dram_tensor("d_sc", [1, 64], F32)
    d_x1 = nc.dram_tensor("d_x1", [E, T], F32)
    d_up = nc.dram_tensor("d_up", [FF, T], BF16)
    d_ga = nc.dram_tensor("d_ga", [FF, T], BF16)
    d_ss = nc.dram_tensor("d_ss", [1, 12288], F32)
    d_red = nc.dram_tensor("d_red", [8, 128], F32)
    d_sc2 = nc.dram_tensor("d_sc2", [4, 8], F32)
    g_w_in = {nm: nc.dram_tensor(f"g_{nm}_in", [1, 128 * FF], BF16)
              for nm in ("up", "ga", "do")}
    g_w_out = {nm: nc.dram_tensor(f"g_{nm}_out", [8, 128 * FF], BF16)
               for nm in ("up", "ga", "do")}
    cc_in = [nc.dram_tensor(f"cc_in{k}", [1, 512], F32) for k in range(4)]
    cc_out = [nc.dram_tensor(f"cc_out{k}", [1, 512], F32) for k in range(4)]

    ctx0 = ExitStack()

    def P(t):
        return ctx0.enter_context(t)

    def SBL(name, shape, dt):
        return nc.sbuf_tensor(name, shape, dt, side="left")

    def SBR(name, shape, dt):
        return nc.sbuf_tensor(name, shape, dt, side="right")

    def PSL(name, shape, dt):
        return nc.psum_tensor(name, shape, dt, side="left")

    def PSR(name, shape, dt):
        return nc.psum_tensor(name, shape, dt, side="right")

    SEMS = {}

    def S(name):
        if name not in SEMS:
            SEMS[name] = Sem(ctx0, nc, name)
        return SEMS[name]

    X = [P(SBL(f"X{c}", [128, T], F32)) for c in range(NEC)]
    sb_b = {nm: P(SBL("s_" + nm, [128, NEC], F32)) for nm in bfm_names}
    bv_bf = P(SBL("bv_bf", [1, E], BF16))
    wd_sb = P(SBL("wd_sb", [128, NEC * H], BF16))
    bd_sb = P(SBL("bd_sb", [H, 1], F32))
    ones_row = P(SBL("ones_row", [1, 128], BF16))
    ones_col = P(SBL("ones_col", [128, 1], BF16))
    Drelu0 = P(SBL("Drelu0", [128, 128], BF16))
    ldI = P(SBL("ldI", [128, H * 128], BF16))
    lam = P(SBL("lam", [128, H * 16], F32))
    cos64 = P(SBL("cos64", [64, T], BF16))
    sinS = P(SBL("sinS", [64, T], BF16))
    ld16 = P(SBL("ld16", [H, 1], F32))
    sc2 = P(SBL("sc2", [128, 8], F32))
    Pi64 = P(SBL("Pi64", [64, 64], BF16))
    ceps = P(SBL("ceps", [65, 1], F32))
    cb5 = P(SBL("cb5", [16, 1], F32))
    z65 = P(SBL("z65", [1, 65], BF16))
    bk8 = P(SBL("bk8", [128, NEC], F32))

    EM = Emitter()
    em = EM.em

    ctxP0 = ExitStack()
    P0 = ctxP0.enter_context
    it_t = P0(SBR("it_t", [128, T], F32))
    it_jp = P0(SBR("it_jp", [128, 16], F32))
    I128 = P0(SBR("I128", [128, 128], F32))
    ldbc = P0(SBR("ldbc", [128, H], F32))
    ldn16 = P0(SBR("ldn16", [H, 1], F32))
    wd_f32 = P0(SBR("wd_f32", [128, NEC * H], F32))
    bv_f32 = P0(SBR("bv_f32", [1, E], F32))

    for c in range(NEC):
        em("sp", lambda e, c=c: e.dma_start(
            out=X[c][:], in_=xT[c * 128:(c + 1) * 128, :]), inc=(S("sLx"), 16))
    LX = S("sLx").n
    for nm in bfm_names:
        em("sp", lambda e, nm=nm: e.dma_start(out=sb_b[nm][:], in_=bfm[nm]),
           inc=(S("sLp"), 16))
    em("sp", lambda e: e.dma_start(out=bd_sb[:], in_=bd_in), inc=(S("sLp"), 16))
    em("sp", lambda e: e.dma_start(
        out=wd_f32[:].rearrange("p (c h) -> p c h", h=H),
        in_=wdT.rearrange("(c p) h -> p c h", p=128)), inc=(S("sLp"), 16))
    em("sp", lambda e: e.dma_start(out=bv_f32[:], in_=bv_in), inc=(S("sLp"), 16))
    LP = S("sLp").n

    em("pool", lambda e: e.memset(z65[:], 0.0))
    em("pool", lambda e: e.memset(ceps[:], 1e-5))
    em("pool", lambda e: e.memset(cb5[:], float(-5.0 * np.log(2.0))))
    em("pool", lambda e: e.iota(it_t[:], pattern=[[1, T]], base=0, channel_multiplier=0,
                                allow_small_or_imprecise_dtypes=True))
    em("pool", lambda e: e.iota(it_jp[:], pattern=[[128, 16]], base=0, channel_multiplier=1,
                                allow_small_or_imprecise_dtypes=True))
    em("pool", lambda e: e.memset(I128[:], 1.0))
    em("pool", lambda e: e.affine_select(I128[:], I128[:], pattern=[[1, 128]],
                                         compare_op=ALU.is_equal, fill=0.0,
                                         base=0, channel_multiplier=-1))
    em("pool", lambda e: e.tensor_copy(wd_sb[:], wd_f32[:]),
       waits=[(S("sLp"), LP)], inc=(S("cP"), 1))

    ctxT = ExitStack()
    PT = ctxT.enter_context
    it_tmp = PT(SBR("it_tmp", [64, T], F32))
    it_i32 = PT(SBR("it_i32", [64, T], I32))
    frac = PT(SBR("frac", [64, T], F32))
    invf = PT(SBR("invf", [64, 1], F32))
    idv = PT(SBR("idv", [64, 1], I32))
    par = PT(SBR("par", [64, 1], F32))
    emask = PT(SBR("emask", [64, 1], F32))
    pm0 = PT(SBR("pm0", [64, 64], F32))
    pm1 = PT(SBR("pm1", [64, 64], F32))
    dscr = PT(SBR("dscr", [128, 128], F32))

    em("pool", lambda e: e.iota(dscr[:], pattern=[[-1, 128]], base=0,
                                channel_multiplier=1, allow_small_or_imprecise_dtypes=True))
    em("pool", lambda e: e.tensor_scalar(Drelu0[:], dscr[:], 0.0, None, ALU.max),
       inc=(S("cDRELU"), 1))
    em("pool", lambda e: e.iota(idv[:], pattern=[[1, 1]], base=0, channel_multiplier=1),
       inc=(S("cIOTA"), 1))
    em("dve", lambda e: e.tensor_scalar(idv[:], idv[:], 1, None, ALU.bitwise_and),
       waits=[(S("cIOTA"), 1)], inc=(S("cPAR"), 1))
    em("dve", lambda e: e.tensor_copy(par[:], idv[:]), waits=[(S("cPAR"), 1)])
    em("dve", lambda e: e.tensor_scalar(emask[:], par[:], -1.0, 1.0, ALU.mult, ALU.add),
       inc=(S("vMASK"), 1))
    em("pool", lambda e: e.iota(idv[:], pattern=[[1, 1]], base=0, channel_multiplier=1),
       waits=[(S("vMASK"), 1)], inc=(S("cIOTA"), 1))
    em("dve", lambda e: e.tensor_scalar(idv[:], idv[:], 31, None, ALU.bitwise_and),
       waits=[(S("cIOTA"), 2)], inc=(S("cD32"), 1))
    em("dve", lambda e: e.tensor_copy(invf[:], idv[:]), waits=[(S("cD32"), 1)],
       inc=(S("vINVF"), 1))
    em("act", lambda e: e.activation(invf[:], invf[:], AF.Exp,
                                     scale=float(-np.log(10000.0) / 32.0)),
       waits=[(S("vINVF"), 1)], inc=(S("aINVF"), 1))
    em("pool", lambda e: e.memset(pm0[:], 1.0))
    em("pool", lambda e: e.affine_select(pm0[:], pm0[:], pattern=[[1, 64]],
                                         compare_op=ALU.is_equal, fill=0.0,
                                         base=-1, channel_multiplier=-1))
    em("pool", lambda e: e.memset(pm1[:], 1.0))
    em("pool", lambda e: e.affine_select(pm1[:], pm1[:], pattern=[[1, 64]],
                                         compare_op=ALU.is_equal, fill=0.0,
                                         base=1, channel_multiplier=-1), inc=(S("cPM"), 1))
    em("dve", lambda e: e.tensor_scalar(pm0[:], pm0[:], emask[:], None, ALU.mult),
       waits=[(S("cPM"), 1)])
    em("dve", lambda e: e.tensor_scalar(pm1[:], pm1[:], par[:], None, ALU.mult))
    em("dve", lambda e: e.tensor_tensor(Pi64[:], pm0[:], pm1[:], ALU.add),
       inc=(S("vPI"), 1))

    tcor = PT(SBR("tcor", [64, T], F32))

    def emit_trig(dst, shift, idx):
        w0 = [(S("aINVF"), 1), (S("cP"), 1)]
        if idx == 1:
            w0.append((S("aTRG"), 1))
        em("dve", lambda e: e.tensor_scalar(it_tmp[:], it_t[0:64, :], invf[:, 0:1],
                                            None, ALU.mult), waits=w0)
        em("dve", lambda e, sh=shift: e.tensor_scalar(it_tmp[:], it_tmp[:],
                                                      float(1.0 / TWOPI), sh + 1.0,
                                                      ALU.mult, ALU.add))
        em("dve", lambda e: e.tensor_copy(it_i32[:], it_tmp[:]))
        em("dve", lambda e: e.tensor_copy(frac[:], it_i32[:]))
        em("dve", lambda e: e.tensor_tensor(tcor[:], frac[:], it_tmp[:], ALU.is_gt))
        em("dve", lambda e: e.tensor_tensor(frac[:], frac[:], tcor[:], ALU.subtract))
        em("dve", lambda e: e.scalar_tensor_tensor(frac[:], it_tmp[:], -0.5, frac[:],
                                                   ALU.add, ALU.subtract),
           inc=(S("vTRGx"), 1))
        em("act", lambda e, d=dst: e.activation(d, frac[:], AF.Sin, scale=TWOPI),
           waits=[(S("vTRGx"), idx + 1)], inc=(S("aTRG"), 1))
    emit_trig(cos64[:], 0.25, 0)
    emit_trig(sinS[:], 0.0, 1)
    em("dve", lambda e: e.tensor_scalar(cos64[:], cos64[:], -1.0, None, ALU.mult),
       waits=[(S("aTRG"), 1)], inc=(S("vTRIG"), 1))
    em("dve", lambda e: e.tensor_scalar(par[:], par[:], -2.0, 1.0, ALU.mult, ALU.add),
       waits=[(S("aTRG"), 2)])
    em("dve", lambda e: e.tensor_scalar(sinS[:], sinS[:], par[:, 0:1], None, ALU.mult),
       inc=(S("vTRIG"), 1))
    em("dve", lambda e: e.memset(ones_row[:], 1.0))
    em("dve", lambda e: e.memset(ones_col[:], 1.0), inc=(S("vONES"), 1))
    em("dve", lambda e: e.tensor_copy(bv_bf[:], bv_f32[:]), waits=[(S("sLp"), LP)],
       inc=(S("vBV"), 1))
    ctxT.close()

    ctxA = ExitStack()
    A = ctxA.enter_context
    Y = [A(SBL(f"Y{c}", [128, T], BF16)) for c in range(NEC)]

    ctxL = ExitStack()
    PL = ctxL.enter_context
    cb = [PL(SBR(f"cb{k}", [128, T], BF16)) for k in range(2)]
    sqb = [PL(SBR(f"sqb{k}", [128, T], BF16)) for k in range(2)]
    Rbc = PL(SBR("Rbc", [128, T], F32))
    Nbc = PL(SBR("Nbc", [128, T], F32))
    tmpc = PL(SBR("tmpc", [128, T], F32))
    row_m = PL(SBR("row_m", [1, T], F32))
    row_v = PL(SBR("row_v", [1, T], F32))
    sc2bf = PL(SBR("sc2bf", [128, 8], BF16))
    c16t = PL(SBR("c16t", [H, T], F32))
    psStat = PL(PSR("psStat", [65, T], F32))
    psPD = PL(PSR("psPD", [H, 1], F32))
    selA = PL(SBR("selA", [128, 65], BF16))
    selB = PL(SBR("selB", [128, 65], BF16))

    def emit_ln(pref, src_chunks, dst_chunks, x_wait, sel_wait, tl):
        em("pool", lambda e: e.memset(tl["selA"][:], 0.0), waits=sel_wait)
        em("pool", lambda e: e.memset(tl["selA"][:, 0:1], 1.0))
        em("pool", lambda e: e.memset(tl["selB"][:], 0.0))
        em("pool", lambda e: e.memset(tl["selB"][:, 64:65], 1.0),
           inc=(S(pref + "SEL"), 1))
        nsel = S(pref + "SEL").n
        psStat = tl["psStat"]
        mm_done = [0, 0]
        for c in range(NEC):
            b = c % 2
            em("dve", lambda e, c=c, b=b: e.tensor_copy(tl["cb"][b][:], src_chunks[c][:]),
               waits=list(x_wait) + ([(S(pref + f"MM{b}"), mm_done[b])] if mm_done[b] else []),
               inc=(S(pref + f"CB{b}"), 1))
            vc = S(pref + f"CB{b}").n
            em("act", lambda e, c=c, b=b: e.activation(tl["sqb"][b][:], src_chunks[c][:],
                                                       AF.Square),
               waits=list(x_wait) + ([(S(pref + f"MM{b}"), mm_done[b])] if mm_done[b] else []),
               inc=(S(pref + f"SQ{b}"), 1))
            ac = S(pref + f"SQ{b}").n
            for ic in range(4):
                em("pe", lambda e, c=c, b=b, ic=ic: e.matmul(
                    psStat[:, ic * 512:(ic + 1) * 512], tl["selA"][:],
                    tl["cb"][b][:, ic * 512:(ic + 1) * 512],
                    start=(c == 0), stop=False),
                    waits=[(S(pref + f"CB{b}"), vc), (S(pref + "SEL"), nsel),
                           (S("vONES"), 1)] if ic == 0 else ())
            for ic in range(4):
                em("pe", lambda e, c=c, b=b, ic=ic: e.matmul(
                    psStat[:, ic * 512:(ic + 1) * 512], tl["selB"][:],
                    tl["sqb"][b][:, ic * 512:(ic + 1) * 512],
                    start=False, stop=(c == NEC - 1)),
                    waits=[(S(pref + f"SQ{b}"), ac)] if ic == 0 else (),
                    inc=(S(pref + f"MM{b}"), 1) if ic == 3 else None)
            mm_done[b] = S(pref + f"MM{b}").n
        em("act", lambda e: e.activation(tl["row_m"][:], psStat[0:1, :], AF.Copy,
                                         scale=1.0 / E),
           waits=[(S(pref + "MM0"), mm_done[0]), (S(pref + "MM1"), mm_done[1])])
        em("act", lambda e: e.activation(tl["row_v"][:], psStat[64:65, :], AF.Copy,
                                         scale=1.0 / E),
           inc=(S(pref + "ROWS"), 1))
        nrows = S(pref + "ROWS").n
        em("dve", lambda e: e.tensor_tensor(tl["tmpc"][0:1, :], tl["row_m"][:],
                                            tl["row_m"][:], ALU.mult),
           waits=[(S(pref + "ROWS"), nrows)])
        em("dve", lambda e: e.tensor_tensor(tl["row_v"][:], tl["row_v"][:],
                                            tl["tmpc"][0:1, :], ALU.subtract),
           inc=(S(pref + "VAR"), 1))
        em("act", lambda e: e.activation(tl["row_v"][:], tl["row_v"][:], AF.Sqrt,
                                         bias=ceps[0:1, :]),
           waits=[(S(pref + "VAR"), S(pref + "VAR").n), (S("cP"), 1)],
           inc=(S(pref + "SD"), 1))
        em("dve", lambda e: e.reciprocal(tl["row_v"][:], tl["row_v"][:]),
           waits=[(S(pref + "SD"), S(pref + "SD").n)])
        em("dve", lambda e: e.tensor_tensor(tl["row_m"][:], tl["row_m"][:],
                                            tl["row_v"][:], ALU.mult))
        em("dve", lambda e: e.tensor_scalar(tl["row_m"][:], tl["row_m"][:], -1.0, None,
                                            ALU.mult),
           inc=(S(pref + "RW"), 1))
        em("sp", lambda e: e.dma_start(out=d_row.ap()[0:1, :], in_=tl["row_v"][:]),
           waits=[(S(pref + "RW"), S(pref + "RW").n)], inc=(S(pref + "DW"), 16))
        em("sp", lambda e: e.dma_start(out=d_row.ap()[1:2, :], in_=tl["row_m"][:]),
           inc=(S(pref + "DW"), 16))
        em("sp", lambda e: e.dma_start(out=tl["Rbc"][:],
                                       in_=d_row.ap()[0, :].partition_broadcast(128)),
           waits=[(S(pref + "DW"), S(pref + "DW").n)], inc=(S(pref + "DB"), 16))
        em("sp", lambda e: e.dma_start(out=tl["Nbc"][:],
                                       in_=d_row.ap()[1, :].partition_broadcast(128)),
           inc=(S(pref + "DB"), 16))
        ndb = S(pref + "DB").n
        for c in range(NEC):
            em("dve", lambda e, c=c: e.tensor_tensor(tl["tmpc"][:], src_chunks[c][:],
                                                     tl["Rbc"][:], ALU.mult),
               waits=[(S(pref + "DB"), ndb)] if c == 0 else ())
            em("dve", lambda e, c=c: e.tensor_tensor(dst_chunks[c][:], tl["tmpc"][:],
                                                     tl["Nbc"][:], ALU.add),
               inc=(S(pref + "Y"), 1) if c == NEC - 1 else None)
        return (S(pref + "Y"), S(pref + "Y").n)

    em("dve", lambda e: e.tensor_scalar(bk8[:], sb_b["bk"][:], 0.125, None, ALU.mult),
       waits=[(S("sLp"), LP)], inc=(S("vBK8"), 1))
    tl1 = {"cb": cb, "sqb": sqb, "selA": selA, "selB": selB, "psStat": psStat,
           "row_m": row_m, "row_v": row_v, "tmpc": tmpc, "Rbc": Rbc, "Nbc": Nbc}
    w_y = emit_ln("L1", X, Y, [(S("sLx"), LX)], [], tl1)

    for c in range(NEC):
        em("dve", lambda e, c=c: e.reduce_sum(sc2[:, c:c + 1], Y[c][:], AX.X),
           waits=[w_y] if c == 0 else ())
    em("dve", lambda e: e.tensor_copy(sc2bf[:], sc2[:]), inc=(S("vYM"), 1))
    for c in range(NEC):
        em("pe", lambda e, c=c: e.matmul(
            psPD[:], wd_sb[:, c * H:(c + 1) * H], sc2bf[:, c:c + 1],
            start=(c == 0), stop=(c == NEC - 1)),
            waits=[(S("vYM"), 1), (S("cP"), 1)] if c == 0 else (),
            inc=(S("pPD"), 1) if c == NEC - 1 else None)
    em("dve", lambda e: e.tensor_copy(ld16[:], psPD[:]), waits=[(S("pPD"), 1)],
       inc=(S("vPD"), 1))
    em("act", lambda e: e.activation(ld16[:], ld16[:], AF.Sigmoid, scale=1.0 / T,
                                     bias=bd_sb[:]),
       waits=[(S("vPD"), 1), (S("sLp"), LP), (S("cP"), 1)])
    em("act", lambda e: e.activation(ld16[:], ld16[:], AF.Exp,
                                     scale=float(-8.0 * np.log(2.0)), bias=cb5[:]))
    em("act", lambda e: e.activation(ld16[:], ld16[:], AF.Ln, scale=-1.0, bias=1.0),
       inc=(S("aLD"), 1))
    em("sp", lambda e: e.dma_start(out=d_ld.ap(), in_=ld16[:]), waits=[(S("aLD"), 1)],
       inc=(S("sLD"), 16))
    em("sp", lambda e: e.dma_start(
        out=ldbc[:],
        in_=d_ld.ap().rearrange("h one -> one h")[0, :].partition_broadcast(128)),
        waits=[(S("sLD"), 16)], inc=(S("sLD"), 16))
    for h in range(H):
        em("act", lambda e, h=h: e.activation(lam[:, h * 16:(h + 1) * 16], it_jp[:],
                                              AF.Exp, scale=ldbc[:, h:h + 1]),
           waits=[(S("sLD"), 32), (S("cP"), 1)] if h == 0 else (),
           inc=(S("aLAM"), 1) if h == H - 1 else None)
    em("dve", lambda e: e.tensor_scalar(ldn16[:], ld16[:], -1.0, None, ALU.mult),
       waits=[(S("aLD"), 1)], inc=(S("vLDN"), 1))
    em("act", lambda e: e.activation(c16t[:], it_t[0:H, :], AF.Exp, scale=ldn16[:]),
       waits=[(S("vLDN"), 1), (S("cP"), 1)], inc=(S("aC16"), 1))
    em("sp", lambda e: e.dma_start(out=d_c16.ap(), in_=c16t[:]), waits=[(S("aC16"), 1)],
       inc=(S("sC16"), 16))
    for h in range(H):
        em("dve", lambda e, h=h: e.tensor_scalar(ldI[:, h * 128:(h + 1) * 128], I128[:],
                                                 ldbc[:, h:h + 1], None, ALU.mult),
           waits=[(S("sLD"), 32), (S("cP"), 1)] if h == 0 else (),
           inc=(S("vLDI"), 1) if h == H - 1 else None)

    ctxL.close()
    ctxP0.close()

    ctxB = ExitStack()
    B = ctxB.enter_context
    OG = [B(SBL(f"OG{c}", [128, T], BF16)) for c in range(NEC)]
    Qf = B(SBL("Qf", [64, 2 * T], BF16))
    Kf = B(SBL("Kf", [64, 2 * T], BF16))
    Vp = B(SBL("Vp", [128, T], BF16))
    vfut = B(SBL("vfut", [128, 16 * 65], BF16))
    vpast = B(SBL("vpast", [128, 16 * 65], BF16))
    wsl = B(SBL("wsl", [128, E], F32))
    wbf = B(SBL("wbf", [128, E], BF16))
    t64a = B(SBL("t64a", [64, T], BF16))
    t64b = B(SBL("t64b", [64, T], BF16))
    scoresb = [B(SBL(f"scoresb{k}", [128, 1024], BF16)) for k in range(2)]
    pfsb = B(SBL("pfsb", [65, 1024], F32))
    ppsb = B(SBL("ppsb", [65, 1024], F32))
    cbc = B(SBL("cbc", [65, 1024], F32))
    rbc = B(SBL("rbc", [128, 1024], BF16))
    mneg = B(SBL("mneg", [128, 2], F32))
    nqk = B(SBL("nqk", [1, 8], F32))
    gnst = B(SBL("gnst", [128, 2], F32))
    gnstb = B(SBL("gnstb", [128, 2], BF16))
    gnfold = B(SBL("gnfold", [128, 65], BF16))
    gn2 = B(SBL("gn2", [65, 4], F32))
    gsc = B(SBL("gsc", [128, 2], F32))
    gate_t = B(SBL("gate_t", [128, T], BF16))
    psQ = B(PSL("psQ", [128, T], F32))
    psPV = B(PSL("psPV", [65, T], F32))

    P0W = [w_y, (S("aC16"), 1), (S("aLAM"), 1), (S("vLDI"), 1), (S("vPD"), 1),
           (S("sC16"), 16)]
    em("pool", lambda e: e.memset(gnfold[:], 0.0), waits=P0W)
    em("pool", lambda e: e.memset(gnfold[0:64, 0:1], 1.0))
    em("pool", lambda e: e.memset(gnfold[64:128, 64:65], 1.0), inc=(S("cGNF"), 1))

    trk = {
        "wsl_free": P0W, "wbf_free": P0W, "psQ_free": P0W, "psPV_free": P0W,
        "QK_free": P0W, "t64_free": P0W, "Vp_free": P0W, "vt_free": P0W,
        "pfpp_free": P0W, "cbc_free": P0W, "rbc_free": P0W, "mneg_free": P0W,
        "qkbuf_free": [P0W, P0W], "sbuf_free": [P0W, P0W],
    }

    def w(*items):
        out = []
        for t in items:
            if t is None:
                continue
            if isinstance(t, list):
                out.extend(x for x in t if x is not None)
            else:
                out.append(t)
        return out

    def stream_weight(dram_col_slice):
        em("pool", lambda e, d=dram_col_slice: e.dma_start(
            out=wbf[:].rearrange("p (c o) -> p c o", o=128),
            in_=d.rearrange("(c p) o -> p c o", p=128)),
            waits=w(trk["wbf_free"]), inc=(S("cWS"), 16), drain=False, is_dma=True)
        return (S("cWS"), S("cWS").n)

    def proj_fm(g_w, rhs_chunks, psem, extra_first=()):
        for ic in range(4):
            for c in range(NEC):
                em("pe", lambda e, c=c, ic=ic: e.matmul(
                    psQ[:, ic * 512:(ic + 1) * 512],
                    wbf[:, c * 128:(c + 1) * 128],
                    rhs_chunks[c][:, ic * 512:(ic + 1) * 512],
                    start=(c == 0), stop=(c == NEC - 1)),
                    waits=(w(g_w, trk["psQ_free"]) + list(extra_first)
                           if (c == 0 and ic == 0) else ()),
                    inc=(S(psem), 1) if (ic == 3 and c == NEC - 1) else None)
        trk["wbf_free"] = (S(psem), S(psem).n)
        return (S(psem), S(psem).n)

    for hp in range(8):
        for which, (wnm, bnm, dst) in enumerate((("wq", "bq", Qf), ("wk", "bk", Kf))):
            g_w = stream_weight(wT[wnm][:, hp * 128:(hp + 1) * 128])
            pQ = proj_fm(g_w, Y, "pQK", extra_first=[w_y] + w(trk["QK_free"]))
            if KDBG and hp == 0 and which == 0:
                em("pool", lambda e: e.dma_start(out=dbg["wbfq0"], in_=wbf[:]),
                   waits=[g_w], inc=(S("sDBP"), 16), drain=False, is_dma=True)
                trk["wbf_free"] = w(trk["wbf_free"], (S("sDBP"), 16))
            bsrc = sb_b[bnm] if which == 0 else bk8
            ksc = 1.0 if which == 0 else 0.125
            em("act", lambda e, d=dst, b=bsrc, sc=ksc, hp=hp: e.activation(
                d[:, 0:T], psQ[0:64, :], AF.Identity, scale=sc, bias=b[0:64, hp:hp + 1]),
                waits=[pQ, (S("sLp"), LP), (S("vBK8"), 1)])
            em("act", lambda e, d=dst, b=bsrc, sc=ksc, hp=hp: e.activation(
                d[:, T:2 * T], psQ[64:128, :], AF.Identity, scale=sc,
                bias=b[64:128, hp:hp + 1]),
                inc=(S("aQK"), 1))
            aQ = (S("aQK"), S("aQK").n)
            trk["psQ_free"] = aQ
            if KDBG and hp == 0 and which == 0:
                em("pool", lambda e: e.dma_start(out=dbg["qpre0"], in_=Qf[:]),
                   waits=[aQ], inc=(S("sDBP2"), 16), drain=False, is_dma=True)
            last_comb = None
            pRs = {}
            for s in range(2):
                rps = psQ if s == 0 else psPV
                for ic in range(4):
                    em("pe", lambda e, s=s, ic=ic, d=dst, rps=rps: e.matmul(
                        rps[0:64, ic * 512:(ic + 1) * 512],
                        Pi64[:], d[:, s * T + ic * 512: s * T + (ic + 1) * 512],
                        start=True, stop=True),
                        waits=(w(aQ, (S("vPI"), 1),
                                 trk["psPV_free"] if s == 1 else None)
                               if ic == 0 else ()),
                        inc=(S("pROT"), 1) if ic == 3 else None)
                pRs[s] = (S("pROT"), S("pROT").n)
            for s in range(2):
                rps = psQ if s == 0 else psPV
                em("act", lambda e, rps=rps: e.activation(t64b[:], rps[0:64, :],
                                                          AF.Copy),
                   waits=w(pRs[s], trk["t64_free"]), inc=(S("aROT"), 1))
                aRot = (S("aROT"), S("aROT").n)
                if s == 0:
                    trk["psQ_free"] = w(aQ, aRot)
                else:
                    trk["psPV_free"] = aRot
                em("dve", lambda e, s=s, d=dst: e.tensor_tensor(
                    t64a[:], d[:, s * T:(s + 1) * T], cos64[:], ALU.mult),
                    waits=w((S("vTRIG"), 2), aRot,
                            (S("sDBP2"), 16) if (KDBG and hp == 0 and which == 0
                                                 and s == 0) else None))
                em("dve", lambda e: e.tensor_tensor(t64b[:], t64b[:], sinS[:], ALU.mult))
                em("dve", lambda e, s=s, d=dst: e.tensor_tensor(
                    d[:, s * T:(s + 1) * T], t64a[:], t64b[:], ALU.add),
                    inc=(S("vCOMB"), 1))
                last_comb = (S("vCOMB"), S("vCOMB").n)
                trk["t64_free"] = last_comb
        v_qk = (S("vCOMB"), S("vCOMB").n)
        if KDBG and hp == 0:
            em("pool", lambda e: e.dma_start(out=dbg["qf0"], in_=Qf[:]),
               waits=[v_qk], inc=(S("sDBQ"), 16), drain=False, is_dma=True)
            em("pool", lambda e: e.dma_start(out=dbg["kf0"], in_=Kf[:]),
               inc=(S("sDBQ"), 16), drain=False, is_dma=True)

        g_v = stream_weight(wT["wv"][:, hp * 128:(hp + 1) * 128])
        psV_free = [trk["psQ_free"]] * 4
        for tb in range(NTB):
            bk = tb % 4
            for c in range(NEC):
                em("pe", lambda e, tb=tb, c=c, bk=bk: e.matmul(
                    psQ[:, bk * 512: bk * 512 + 128],
                    Y[c][:, tb * 128:(tb + 1) * 128],
                    wbf[:, c * 128:(c + 1) * 128],
                    start=(c == 0), stop=False),
                    waits=(w(g_v, psV_free[bk], trk["Vp_free"]) if c == 0 else ()))
            em("pe", lambda e, bk=bk, hp=hp: e.matmul(
                psQ[:, bk * 512: bk * 512 + 128],
                ones_row[0:1, 0:128],
                bv_bf[0:1, hp * 128:(hp + 1) * 128],
                start=False, stop=True),
                waits=[(S("vBV"), 1), (S("vONES"), 1)] if (hp == 0 and tb == 0) else (),
                inc=(S("pV"), 1))
            pV = (S("pV"), S("pV").n)
            em("act", lambda e, tb=tb, bk=bk: e.activation(
                Vp[:, tb * 128:(tb + 1) * 128], psQ[:, bk * 512: bk * 512 + 128], AF.Copy),
                waits=[pV], inc=(S("aV"), 1))
            psV_free[bk] = (S("aV"), S("aV").n)
        a_v = (S("aV"), S("aV").n)
        if KDBG and hp == 0:
            em("pool", lambda e: e.dma_start(out=dbg["vp0"], in_=Vp[:]),
               waits=[a_v], inc=(S("sDBQ"), 16), drain=False, is_dma=True)
        trk["psQ_free"] = a_v
        trk["wbf_free"] = (S("pV"), S("pV").n)

        for w2, src in enumerate((Qf, Kf)):
            em("act", lambda e, s=src: e.activation(t64a[:], s[:, 0:T], AF.Square),
               waits=(w(v_qk, trk["t64_free"]) if w2 == 0
                      else [(S("pNRM"), S("pNRM").n)]))
            em("act", lambda e, s=src: e.activation(t64b[:], s[:, T:2 * T], AF.Square),
               inc=(S("aNSQ"), 1))
            a_sq = (S("aNSQ"), S("aNSQ").n)
            for half, tt in ((0, t64a), (1, t64b)):
                for ic in range(4):
                    em("pe", lambda e, tt=tt, ic=ic: e.matmul(
                        psQ[0:1, ic * 512:(ic + 1) * 512],
                        ones_col[0:64, :], tt[:, ic * 512:(ic + 1) * 512],
                        start=True, stop=True),
                        waits=w(a_sq, trk["psQ_free"]) if ic == 0 else (),
                        inc=(S("pNRM"), 1) if ic == 3 else None)
                pN = (S("pNRM"), S("pNRM").n)
                em("dve", lambda e: e.tensor_copy(pfsb[0:1, :], psQ[0:1, 0:1024]),
                   waits=w(pN, trk["pfpp_free"]))
                em("dve", lambda e: e.tensor_copy(ppsb[0:1, :], psQ[0:1, 1024:2048]))
                em("dve", lambda e, k=w2 * 2 + half: e.reduce_max(
                    nqk[0:1, k:k + 1], pfsb[0:1, :], AX.X))
                em("dve", lambda e, k=w2 * 2 + half: e.reduce_max(
                    nqk[0:1, 4 + k:5 + k], ppsb[0:1, :], AX.X))
                em("dve", lambda e, k=w2 * 2 + half: e.tensor_tensor(
                    nqk[0:1, k:k + 1], nqk[0:1, k:k + 1], nqk[0:1, 4 + k:5 + k], ALU.max),
                    inc=(S("vNRM"), 1))
                trk["psQ_free"] = (S("vNRM"), S("vNRM").n)
                trk["pfpp_free"] = (S("vNRM"), S("vNRM").n)
        em("dve", lambda e: e.tensor_tensor(nqk[0:1, 4:6], nqk[0:1, 0:2], nqk[0:1, 2:4],
                                            ALU.mult), inc=(S("vNQ"), 1))
        em("act", lambda e: e.activation(nqk[0:1, 4:6], nqk[0:1, 4:6], AF.Sqrt),
           waits=[(S("vNQ"), S("vNQ").n)], inc=(S("aNQ"), 1))
        em("dve", lambda e: e.tensor_scalar(nqk[0:1, 4:6], nqk[0:1, 4:6], -1.0,
                                            None, ALU.mult),
           waits=[(S("aNQ"), S("aNQ").n)], inc=(S("vNM"), 1))
        em("pool", lambda e, hp=hp: e.dma_start(out=d_sc.ap()[0:1, 2 * hp:2 * hp + 2],
                                               in_=nqk[0:1, 4:6]),
           waits=[(S("vNM"), S("vNM").n)], inc=(S("sNQ"), 16),
           drain=False, is_dma=True)
        em("pool", lambda e, hp=hp: e.dma_start(
            out=mneg[:], in_=d_sc.ap()[0, 2 * hp:2 * hp + 2].partition_broadcast(128)),
           waits=w((S("sNQ"), S("sNQ").n), trk["mneg_free"]), inc=(S("sNQ"), 16),
           drain=False, is_dma=True)
        ss_mneg = (S("sNQ"), S("sNQ").n)

        vfut3 = vfut[:].rearrange("p (j d) -> p j d", d=65)
        vpast3 = vpast[:].rearrange("p (j d) -> p j d", d=65)
        Vp3 = Vp[:].rearrange("p (j d) -> p j d", d=128)
        for s in range(2):
            h = 2 * hp + s
            lam3 = lam[:, h * 16:(h + 1) * 16].rearrange("p j -> p j ()")
            em("dve", lambda e, s=s, lam3=lam3: e.tensor_tensor(
                vfut3[:, :, 0:64], Vp3[:, :, s * 64:(s + 1) * 64],
                lam3.broadcast_to((128, 16, 64)), ALU.mult),
                waits=w(a_v, (S("aLAM"), 1), trk["vt_free"]))
            em("dve", lambda e, lam3=lam3: e.tensor_copy(vfut3[:, :, 64:65], lam3))
            em("dve", lambda e, s=s: e.tensor_copy(
                vpast3[:, :, 0:64], Vp3[:, :, s * 64:(s + 1) * 64]))
            em("dve", lambda e: e.memset(vpast3[:, :, 64:65], 1.0), inc=(S("vVT"), 1))
            v_vt = (S("vVT"), S("vVT").n)
            trk["Vp_free"] = (w(v_vt, (S("sDBQ"), 48))
                              if (KDBG and hp == 0) else v_vt)

            for ic2 in range(2):
                def _wf(jb):
                    return min(max(128 * jb - 1024 * ic2, 0), 1024)
                segs_by_jb = {}
                bank_events = {0: [], 1: [], 2: [], 3: []}
                for jb in range(16):
                    wf = _wf(jb)
                    wfp = _wf(jb - 1) if jb > 0 else 0
                    segs = []
                    cuts = sorted({0, wf, min(wfp, wf), 512, 1024} & set(range(0, wf + 1))
                                  | {0, min(wfp, wf), wf}
                                  | {c for c in (512, 1024) if c < wf})
                    cuts = sorted(set([0, min(wfp, wf), wf] +
                                      [c for c in (512, 1024) if 0 < c < wf]))
                    for a2, b2 in zip(cuts, cuts[1:]):
                        if b2 > a2:
                            segs.append(("f", 1024 + a2, 1024 + b2))
                    a2 = wf
                    while a2 < 1024:
                        b2 = min((a2 // 512 + 1) * 512, 1024)
                        segs.append(("p", a2, b2))
                        a2 = b2
                    segs_by_jb[jb] = segs
                    for k, (kind, ca, cb2) in enumerate(segs):
                        for bk in range(ca // 512, (cb2 - 1) // 512 + 1):
                            bank_events[bk].append((jb, k))
                first_ev = {bk: (ev[0] if ev else None) for bk, ev in bank_events.items()}
                last_ev = {bk: (ev[-1] if ev else None) for bk, ev in bank_events.items()}
                wfmax = _wf(15)
                if wfmax < 1024:
                    a2 = wfmax
                    while a2 < 1024:
                        b2 = min((a2 // 512 + 1) * 512, 1024)
                        em("pe", lambda e, a=a2, b=b2: e.matmul(
                            psPV[:, 1024 + a:1024 + b], z65[:], ones_row[0:1, 0:(b - a)],
                            start=True, stop=True),
                            waits=w(trk["psPV_free"]))
                        a2 = b2
                exp_done = {}
                pv_done = {}
                evac_done = {}

                def emit_qk(jb):
                    off = 128 * jb - 1024 * ic2
                    wf = min(max(off, 0), 1024)
                    buf = jb % 2
                    bs_a, bs_b = wf, min(off + 128, 1024)
                    has_bias = bs_b > bs_a
                    if jb >= 2:
                        wqk = [evac_done[jb - 2]]
                    else:
                        wqk = w(trk["qkbuf_free"][buf], trk["psQ_free"],
                                trk["QK_free"])
                    segh = {0: has_bias and min(bs_b, 512) > max(bs_a, 0),
                            1: has_bias and min(bs_b, 1024) > max(bs_a, 512)}
                    for half in range(2):
                        ha, hb = half * 512, (half + 1) * 512
                        seg_a, seg_b = max(bs_a, ha), min(bs_b, hb)
                        has_seg = segh[half]
                        is_last_qk = (half == 1 and not has_seg)
                        em("pe", lambda e, jb=jb, half=half, buf=buf, s=s, ic2=ic2,
                           st=not has_seg: e.matmul(
                            psQ[:, buf * 1024 + half * 512: buf * 1024 + (half + 1) * 512],
                            Kf[:, s * T + jb * 128: s * T + (jb + 1) * 128],
                            Qf[:, s * T + ic2 * 1024 + half * 512:
                               s * T + ic2 * 1024 + (half + 1) * 512],
                            start=True, stop=st),
                            waits=wqk if half == 0 else (),
                            inc=(S(f"qk{buf}"), 1) if is_last_qk else None)
                        wqk[:] = []
                        if has_seg:
                            u0 = seg_a - off
                            em("pe", lambda e, buf=buf, h=h, a=seg_a, b=seg_b, u0=u0:
                               e.matmul(
                                psQ[:, buf * 1024 + a: buf * 1024 + b],
                                ldI[:, h * 128:(h + 1) * 128],
                                Drelu0[:, u0:u0 + (b - a)],
                                start=False, stop=True),
                                waits=[(S("vLDI"), 1), (S("cDRELU"), 1)]
                                if (hp == 0 and s == 0 and ic2 == 0) else (),
                                inc=(S(f"qk{buf}"), 1) if half == 1 else None)
                    pQK = (S(f"qk{buf}"), S(f"qk{buf}").n)
                    wev = w(pQK, trk["sbuf_free"][buf])
                    em("dve", lambda e, buf=buf: e.tensor_copy(
                        scoresb[buf][:], psQ[:, buf * 1024: buf * 1024 + 1024]),
                        waits=wev, inc=(S(f"ev{buf}"), 1))
                    evac_done[jb] = (S(f"ev{buf}"), S(f"ev{buf}").n)
                    trk["qkbuf_free"][buf] = evac_done[jb]
                    em("act", lambda e, buf=buf, s=s: e.activation(
                        scoresb[buf][:], scoresb[buf][:], AF.Exp, bias=mneg[:, s:s + 1]),
                        waits=w(evac_done[jb],
                                ss_mneg if (s == 0 and ic2 == 0 and jb == 0)
                                else None),
                        inc=(S(f"ex{buf}"), 1))
                    exp_done[jb] = (S(f"ex{buf}"), S(f"ex{buf}").n)

                def emit_pv(jb):
                    buf = jb % 2
                    first_mm = True
                    for k, (kind, ca, cb2) in enumerate(segs_by_jb[jb]):
                        start = first_ev[ca // 512] == (jb, k)
                        stop = last_ev[ca // 512] == (jb, k)
                        wpv = []
                        if first_mm:
                            wpv = [exp_done[jb]]
                            if jb == 0:
                                wpv += w(trk["psPV_free"], v_vt)
                        vt = vfut3 if kind == "f" else vpast3
                        ea = ca - 1024 if kind == "f" else ca
                        eb = cb2 - 1024 if kind == "f" else cb2
                        em("pe", lambda e, jb=jb, buf=buf, ca=ca, cb2=cb2, ea=ea, eb=eb,
                           vt=vt, start=start, stop=stop: e.matmul(
                            psPV[:, ca:cb2], vt[:, jb, :], scoresb[buf][:, ea:eb],
                            start=start, stop=stop), waits=wpv,
                            inc=(S(f"pv{buf}"), 1) if k == len(segs_by_jb[jb]) - 1 else None)
                        first_mm = False
                    pv_done[jb] = (S(f"pv{buf}"), S(f"pv{buf}").n)
                    trk["sbuf_free"][buf] = pv_done[jb]

                for jb in range(17):
                    if jb < 16:
                        emit_qk(jb)
                    if jb >= 1:
                        emit_pv(jb - 1)
                em("act", lambda e: e.activation(ppsb[:], psPV[:, 0:1024], AF.Copy),
                   waits=w(pv_done[15], pv_done[14], trk["pfpp_free"]))
                em("act", lambda e: e.activation(pfsb[:], psPV[:, 1024:2048], AF.Copy),
                   inc=(S("aPP"), 1))
                aPV = (S("aPP"), S("aPP").n)
                trk["psPV_free"] = aPV
                em("pool", lambda e, h=h, ic2=ic2: e.dma_start(
                    out=cbc[:],
                    in_=d_c16.ap()[h, ic2 * 1024:(ic2 + 1) * 1024].partition_broadcast(65)),
                    waits=w((S("sC16"), 16), trk["cbc_free"]), inc=(S("sCB"), 16),
                    drain=False, is_dma=True)
                ss_cbc = (S("sCB"), S("sCB").n)
                em("dve", lambda e: e.tensor_tensor(pfsb[64:65, :], cbc[64:65, :],
                                                    pfsb[64:65, :], ALU.mult),
                   waits=w(aPV, ss_cbc))
                em("dve", lambda e: e.tensor_tensor(ppsb[64:65, :], pfsb[64:65, :],
                                                    ppsb[64:65, :], ALU.add))
                em("dve", lambda e: e.reciprocal(pfsb[64:65, :], ppsb[64:65, :]))
                em("dve", lambda e: e.tensor_tensor(cbc[0:64, :], cbc[0:64, :],
                                                    pfsb[0:64, :], ALU.mult))
                em("dve", lambda e, hp=hp, s=s, ic2=ic2: e.tensor_tensor(
                    OG[hp][s * 64:(s + 1) * 64, ic2 * 1024:(ic2 + 1) * 1024],
                    cbc[0:64, :], ppsb[0:64, :], ALU.add), inc=(S("vEPI"), 1))
                vDen = (S("vEPI"), S("vEPI").n)
                trk["cbc_free"] = vDen
                em("pool", lambda e, h=h, ic2=ic2: e.dma_start(
                    out=d_r16.ap()[h:h + 1, ic2 * 1024:(ic2 + 1) * 1024],
                    in_=pfsb[64:65, :]), waits=[vDen], inc=(S("sR16"), 16),
                    drain=False, is_dma=True)
                ss_r = (S("sR16"), S("sR16").n)
                trk["pfpp_free"] = w(vDen, ss_r)
                em("pool", lambda e, h=h, ic2=ic2: e.dma_start(
                    out=rbc[:],
                    in_=d_r16.ap()[h, ic2 * 1024:(ic2 + 1) * 1024].partition_broadcast(128)),
                    waits=w(ss_r, trk["rbc_free"]), inc=(S("gRBC"), 16), drain=False,
                    is_dma=True)
                ss_rbc2 = (S("gRBC"), S("gRBC").n)
                em("dve", lambda e, hp=hp, s=s, ic2=ic2: e.tensor_tensor(
                    OG[hp][s * 64:(s + 1) * 64, ic2 * 1024:(ic2 + 1) * 1024],
                    OG[hp][s * 64:(s + 1) * 64, ic2 * 1024:(ic2 + 1) * 1024],
                    rbc[s * 64:(s + 1) * 64, :], ALU.mult),
                    waits=[ss_rbc2], inc=(S("vOG"), 1))
                trk["rbc_free"] = (S("vOG"), S("vOG").n)
            trk["vt_free"] = pv_done[15]
        trk["QK_free"] = (w(pv_done[15], (S("sDBQ"), 48))
                          if (KDBG and hp == 0) else pv_done[15])
        trk["mneg_free"] = exp_done[15]
        trk["psQ_free"] = [evac_done[14], evac_done[15]]

    og_done = (S("vOG"), S("vOG").n)
    if KDBG:
        em("pool", lambda e: e.dma_start(out=dbg["og0p"], in_=OG[0][:]),
           waits=[og_done], inc=(S("sDBG"), 16), drain=False, is_dma=True)
    for hp in range(8):
        g_g = stream_weight(wT["wg"][:, hp * 128:(hp + 1) * 128])
        pG = proj_fm(g_g, Y, "pG")
        em("act", lambda e, hp=hp: e.activation(gate_t[:], psQ[:], AF.Silu,
                                                bias=sb_b["bg"][:, hp:hp + 1]),
           waits=w(pG, trk["gate_free"] if "gate_free" in trk else None),
           inc=(S("aG"), 1))
        a_g = (S("aG"), S("aG").n)
        trk["psQ_free"] = a_g
        em("dve", lambda e, hp=hp: e.reduce_sum(gnst[:, 0:1], OG[hp][:], AX.X),
           waits=w(og_done if hp == 0 else None))
        em("act", lambda e, hp=hp: e.activation(t64a[:], OG[hp][0:64, :], AF.Square),
           waits=w(og_done if hp == 0 else None, trk["t64_free"]))
        em("act", lambda e, hp=hp: e.activation(t64b[:], OG[hp][64:128, :], AF.Square),
           inc=(S("aGSQ"), 1))
        aGN = (S("aGSQ"), S("aGSQ").n)
        em("dve", lambda e: e.reduce_sum(gnst[0:64, 1:2], t64a[:], AX.X), waits=[aGN])
        em("dve", lambda e: e.reduce_sum(gnst[64:128, 1:2], t64b[:], AX.X))
        em("dve", lambda e: e.tensor_copy(gnstb[:], gnst[:]), inc=(S("vGST"), 1))
        vGN = (S("vGST"), S("vGST").n)
        trk["t64_free"] = vGN
        em("pe", lambda e: e.matmul(psPV[:, 0:2], gnfold[:], gnstb[:],
                                    start=True, stop=True),
           waits=w(vGN, (S("cGNF"), 1), trk["psPV_free"]), inc=(S("pGN"), 1))
        em("dve", lambda e: e.tensor_copy(gn2[:, 0:2], psPV[:, 0:2]),
           waits=[(S("pGN"), S("pGN").n)], inc=(S("vGN2"), 1))
        trk["psPV_free"] = (S("vGN2"), S("vGN2").n)
        NL = float(T * HD)
        em("dve", lambda e: e.tensor_scalar(gn2[:, 0:2], gn2[:, 0:2], 1.0 / NL, None,
                                            ALU.mult))
        em("dve", lambda e: e.tensor_tensor(gn2[:, 2:3], gn2[:, 0:1], gn2[:, 0:1],
                                            ALU.mult))
        em("dve", lambda e: e.tensor_tensor(gn2[:, 3:4], gn2[:, 1:2], gn2[:, 2:3],
                                            ALU.subtract), inc=(S("vGVAR"), 1))
        em("act", lambda e: e.activation(gn2[:, 3:4], gn2[:, 3:4], AF.Sqrt, bias=ceps[:]),
           waits=[(S("vGVAR"), S("vGVAR").n), (S("cP"), 1)], inc=(S("aGSD"), 1))
        em("dve", lambda e: e.reciprocal(gn2[:, 3:4], gn2[:, 3:4]),
           waits=[(S("aGSD"), S("aGSD").n)])
        em("dve", lambda e: e.tensor_copy(gn2[:, 2:3], gn2[:, 0:1]), inc=(S("vGR"), 1))
        vRstd = (S("vGR"), S("vGR").n)
        em("pool", lambda e: e.dma_start(out=d_sc.ap()[0:1, 4:6], in_=gn2[0:1, 2:4]),
           waits=[vRstd], inc=(S("sGN"), 16), drain=False, is_dma=True)
        em("pool", lambda e: e.dma_start(out=d_sc.ap()[0:1, 6:8], in_=gn2[64:65, 2:4]),
           inc=(S("sGN"), 16), drain=False, is_dma=True)
        em("pool", lambda e: e.dma_start(
            out=gsc[0:64, 0:2], in_=d_sc.ap()[0, 4:6].partition_broadcast(64)),
           waits=[(S("sGN"), S("sGN").n)], inc=(S("sGN2"), 16),
           drain=False, is_dma=True)
        em("pool", lambda e: e.dma_start(
            out=gsc[64:128, 0:2], in_=d_sc.ap()[0, 6:8].partition_broadcast(64)),
           inc=(S("sGN2"), 16), drain=False, is_dma=True)
        em("dve", lambda e, hp=hp: e.tensor_scalar(gsc[:, 1:2],
                                                   sb_b["gamma"][:, hp:hp + 1],
                                                   gsc[:, 1:2], None, ALU.mult),
           waits=[(S("sGN2"), S("sGN2").n)])
        em("dve", lambda e: e.tensor_tensor(gsc[:, 0:1], gsc[:, 0:1], gsc[:, 1:2],
                                            ALU.mult))
        em("dve", lambda e, hp=hp: e.scalar_tensor_tensor(
            gsc[:, 0:1], gsc[:, 0:1], -1.0, sb_b["beta"][:, hp:hp + 1], ALU.mult, ALU.add),
            inc=(S("vGSC"), 1))
        em("dve", lambda e, hp=hp: e.tensor_scalar(OG[hp][:], OG[hp][:], gsc[:, 1:2],
                                                   gsc[:, 0:1], ALU.mult, ALU.add),
           waits=[(S("vGSC"), S("vGSC").n)] +
                 ([(S("sDBG"), 16)] if (KDBG and hp == 0) else []))
        em("dve", lambda e, hp=hp: e.tensor_tensor(OG[hp][:], OG[hp][:], gate_t[:],
                                                   ALU.mult), waits=[a_g],
           inc=(S("vAPP"), 1))
        trk["gate_free"] = (S("vAPP"), S("vAPP").n)
    v_att = (S("vAPP"), S("vAPP").n)

    vO = [None, None, None, None]
    for oc in range(NEC):
        g_o = stream_weight(wT["wo"][:, oc * 128:(oc + 1) * 128])
        for ic in range(4):
            for c in range(NEC):
                em("pe", lambda e, c=c, ic=ic: e.matmul(
                    psQ[:, ic * 512:(ic + 1) * 512],
                    wbf[:, c * 128:(c + 1) * 128],
                    OG[c][:, ic * 512:(ic + 1) * 512],
                    start=(c == 0), stop=(c == NEC - 1)),
                    waits=(w(g_o, v_att if oc == 0 else None,
                             trk["psQ_free"] if oc == 0 and ic == 0 else None,
                             vO[ic] if oc > 0 else None) if c == 0 else ()),
                    inc=(S(f"pO{ic}"), 1) if c == NEC - 1 else None)
            pO = (S(f"pO{ic}"), S(f"pO{ic}").n)
            em("dve", lambda e, oc=oc, ic=ic: e.scalar_tensor_tensor(
                X[oc][:, ic * 512:(ic + 1) * 512], psQ[:, ic * 512:(ic + 1) * 512],
                sb_b["bo"][0:128, oc:oc + 1], X[oc][:, ic * 512:(ic + 1) * 512],
                ALU.add, ALU.add), waits=[pO], inc=(S(f"vO{ic}"), 1))
            vO[ic] = (S(f"vO{ic}"), S(f"vO{ic}").n)
        trk["wbf_free"] = (S("pO3"), S("pO3").n)

    if KDBG:
        ddone = w(v_att, (S("vOG"), S("vOG").n))
        for nm, tile in (("y0", Y[0][:]), ("ld", ld16[:]), ("qf", Qf[:]), ("kf", Kf[:]),
                         ("vp", Vp[:]), ("og0", OG[0][:]), ("mneg", mneg[:]),
                         ("cos", cos64[:]), ("sin", sinS[:]), ("gate", gate_t[:]),
                         ("pf", pfsb[:]), ("pp", ppsb[:])) + tuple(
                             (f"yc{c}", Y[c][:]) for c in range(8)):
            em("pool", lambda e, nm=nm, tile=tile: e.dma_start(out=dbg[nm], in_=tile),
               waits=ddone, inc=(S("sDBG"), 16), drain=False, is_dma=True)
            ddone = ()

    if KSTOP == "attn":
        for c in range(NEC):
            em("sp", lambda e, c=c: e.dma_start(out=yout[c * 128:(c + 1) * 128, :],
                                                in_=X[c][:]),
               waits=[vO[0], vO[1], vO[2], vO[3]] if c == 0 else (), inc=(S("sO"), 16))
        em("sp", lambda e: e.nop(), waits=[(S("sO"), S("sO").n)] +
           ([(S("sDBG"), S("sDBG").n), (S("sDBQ"), S("sDBQ").n),
             (S("sDBP"), S("sDBP").n), (S("sDBP2"), S("sDBP2").n)] if KDBG else []),
           drain=False)
    else:
        ctxB.close()
        ctxFP = ExitStack()
        FP = ctxFP.enter_context
        psF1 = FP(PSL("psF1", [128, T], F32))
        psF2 = FP(PSL("psF2", [128, T], F32))
        ctxF2 = ExitStack()
        F2 = ctxF2.enter_context
        iA = F2(SBR("iA", [128, T], I32))
        iA2 = F2(SBR("iA2", [128, T], I32))
        iA3 = F2(SBR("iA3", [128, T], I32))
        scb0 = F2(SBR("scb0", [128, 2], F32))
        scb1 = F2(SBR("scb1", [128, 4], F32))
        scb2 = F2(SBR("scb2", [128, 2], F32))
        scb3 = F2(SBR("scb3", [128, 2], F32))
        mx0 = F2(SBR("mx0", [128, 1], F32))
        mxu = F2(SBR("mxu", [128, 1], F32))
        mxg = F2(SBR("mxg", [128, 1], F32))
        mx2 = F2(SBR("mx2", [128, 1], F32))
        mx3 = F2(SBR("mx3", [128, 1], F32))
        tmpr = F2(SBR("tmpr", [128, 1], F32))
        mrow = F2(SBR("mrow", [1, 128], F32))
        sscl = F2(SBR("sscl", [1, 20], F32))
        XW = w(vO[0], vO[1], vO[2], vO[3],
               (S("sDBG"), S("sDBG").n) if KDBG else None,
               (S("sDBQ"), S("sDBQ").n) if KDBG else None)

        for c in range(NEC):
            em("sp", lambda e, c=c: e.dma_start(
                out=d_x1.ap()[c * 128:(c + 1) * 128, :], in_=X[c][:]),
                waits=XW if c == 0 else (), inc=(S("sLx"), 16))
        sX1 = (S("sLx"), S("sLx").n)
        for m in (mx0, mxu, mxg, mx2, mx3):
            em("pool", lambda e, m=m: e.memset(m[:], 0.0),
               waits=XW if m is mx0 else ())
        em("pool", lambda e: e.nop(), inc=(S("cPM"), 1), drain=True)
        mZ = (S("cPM"), S("cPM").n)

        ctxDQ = ExitStack()
        DQ = ctxDQ.enter_context
        wtF = DQ(SBR("wtF", [128, FF], F32))
        iW = DQ(SBR("iW", [128, FF], I32))
        wqF = DQ(SBR("wqF", [128, FF], F32))
        wqT = DQ(SBR("wqTs", [128, FF], BF16))
        s4t = DQ(SBR("s4t", [128, 96], F32))
        rin4 = DQ(SBR("rin4", [128, 96], F32))
        sI = DQ(SBR("sI", [128, 128], F32))
        em("pool", lambda e: e.memset(sI[:], 1.0), waits=XW)
        em("pool", lambda e: e.affine_select(sI[:], sI[:], pattern=[[1, 128]],
                                             compare_op=ALU.is_equal, fill=0.0,
                                             base=0, channel_multiplier=-1),
           inc=(S("cIOTA"), 1))
        nID = S("cIOTA").n
        prev_wq = None
        prev_tr = None

        def dequant_chunk(src2d, ncols, tcol_of_j, first):
            nonlocal prev_wq, prev_tr
            global_es = {}
            nj = ncols // 128
            nb = ncols // 32
            em("sp", lambda e, src2d=src2d, ncols=ncols: e.dma_start(
                out=wtF[:, 0:ncols], in_=src2d),
               waits=w(XW if first else None, prev_wq, prev_tr),
               inc=(S("sLw"), 16))
            em("dve", lambda e, ncols=ncols, nb=nb: e.tensor_reduce(
                s4t[:, 0:nb].rearrange("p (b one) -> p b one", one=1),
                wtF[:, 0:ncols].rearrange("p (b k) -> p b k", k=32),
                AX.X, ALU.max, apply_absolute_value=True),
               waits=w((S("sLw"), S("sLw").n), prev_tr), inc=(S("vNRM"), 1))
            em("dve", lambda e, nb=nb: e.tensor_scalar(
                s4t[:, 0:nb], s4t[:, 0:nb], 1.0 / 7.0, None, ALU.mult))
            em("dve", lambda e, nb=nb: e.tensor_scalar(
                s4t[:, 0:nb], s4t[:, 0:nb], 1e-12, None, ALU.max))
            em("dve", lambda e, nb=nb: e.reciprocal(rin4[:, 0:nb], s4t[:, 0:nb]))
            em("dve", lambda e, ncols=ncols, nb=nb: e.tensor_tensor(
                iW[:, 0:ncols].rearrange("p (b k) -> p b k", k=32),
                wtF[:, 0:ncols].rearrange("p (b k) -> p b k", k=32),
                rin4[:, 0:nb].rearrange("p (b one) -> p b one", one=1)
                .broadcast_to((128, nb, 32)),
                ALU.mult))
            em("dve", lambda e, ncols=ncols, nb=nb: e.tensor_tensor(
                wqF[:, 0:ncols].rearrange("p (b k) -> p b k", k=32),
                iW[:, 0:ncols].rearrange("p (b k) -> p b k", k=32),
                s4t[:, 0:nb].rearrange("p (b one) -> p b one", one=1)
                .broadcast_to((128, nb, 32)),
                ALU.mult),
               inc=(S("vCOMB"), 1))
            prev_wq = (S("vCOMB"), S("vCOMB").n)
            for j in range(nj):
                pslice = (psF1[:, 0:128] if j % 2 == 0 else psF1[:, 512:640])
                em("pe", lambda e, j=j, ps=pslice: e.transpose(
                    ps, wqF[:, j * 128:(j + 1) * 128], sI[:]),
                    waits=w(prev_wq if j == 0 else None,
                            XW if (first and j <= 1) else None,
                            (S("cIOTA"), nID) if (first and j == 0) else None,
                            global_es.get(j - 2)),
                    inc=(S("pROT"), 1))
                em("act", lambda e, j=j, ps=pslice, tc=tcol_of_j(j): e.activation(
                    wqT[:, tc:tc + 128], ps, AF.Copy),
                    waits=[(S("pROT"), S("pROT").n)], inc=(S("aROT"), 1))
                global_es[j] = (S("aROT"), S("aROT").n)
            prev_tr = (S("aROT"), S("aROT").n)

        gW = {}
        for nm, src, ncols, nch in (("up", wupNs, E, 3), ("ga", wgaNs, E, 3),
                                    ("do", wdoNs, FF, 1)):
            for ch in range(nch):
                tcol = (lambda j, ch=ch: j * 384 + ch * 128) if nch == 3 else \
                       (lambda j: j * 128)
                dequant_chunk(src[ch * 128:(ch + 1) * 128, :], ncols, tcol,
                              prev_wq is None)
            em("sp", lambda e, nm=nm: e.dma_start(
                out=g_w_in[nm].ap()[0:1, :].rearrange("one (p o) -> p o", p=128),
                in_=wqT[:]),
               waits=[(S("aROT"), S("aROT").n)], inc=(S("sCB"), 16))
            em("pool", lambda e, nm=nm: e.collective_compute(
                "AllGather", ALU.bypass, replica_groups=[list(range(8))],
                ins=[g_w_in[nm].ap().opt()], outs=[g_w_out[nm].ap().opt()]),
                waits=[(S("sCB"), S("sCB").n)], inc=(S("gRBC"), 1), drain=False,
                is_dma=True)
            gW[nm] = S("gRBC").n
            prev_tr = w(prev_tr, (S("sCB"), S("sCB").n))
        ctxDQ.close()

        ctxLN = ExitStack()
        LN = ctxLN.enter_context
        tl2 = {"cb": [LN(SBR(f"fcb{k}", [128, T], BF16)) for k in range(2)],
               "sqb": [LN(SBR(f"fsq{k}", [128, T], BF16)) for k in range(2)],
               "selA": LN(SBR("fselA", [128, 65], BF16)),
               "selB": LN(SBR("fselB", [128, 65], BF16)),
               "Rbc": LN(SBR("fRbc", [128, T], F32)),
               "Nbc": LN(SBR("fNbc", [128, T], F32)),
               "tmpc": LN(SBR("ftmpc", [128, T], F32)),
               "row_m": LN(SBR("frm", [1, T], F32)),
               "row_v": LN(SBR("frv", [1, T], F32)),
               "psStat": psF2[0:65, :]}
        w_y2 = emit_ln("L1", X, Y, XW, XW, tl2)

        def emit_barrier(k, mxs, wait_extra):
            n = len(mxs)
            for i, mx in enumerate(mxs):
                em("sp", lambda e, k=k, i=i, mx=mx: e.dma_start(
                    out=d_red.ap()[2 * k + i:2 * k + i + 1, :], in_=mx[:]),
                    waits=wait_extra if i == 0 else (), inc=(S("sGN"), 16))
            sA = (S("sGN"), S("sGN").n)
            for i in range(n):
                em("sp", lambda e, k=k, i=i: e.dma_start(
                    out=mrow[:], in_=d_red.ap()[2 * k + i:2 * k + i + 1, :]),
                    waits=w(sA if i == 0 else None,
                            (S("vGR"), S("vGR").n) if i > 0 else None),
                    inc=(S("sGN"), 16))
                em("dve", lambda e, i=i: e.reduce_max(sscl[0:1, i:i + 1], mrow[:],
                                                      AX.X),
                   waits=[(S("sGN"), S("sGN").n)], inc=(S("vGR"), 1))
            em("sp", lambda e, k=k, n=n: e.dma_start(out=cc_in[k].ap()[0:1, 0:n],
                                                     in_=sscl[0:1, 0:n]),
               waits=[(S("vGR"), S("vGR").n)], inc=(S("sGN"), 16))
            em("pool", lambda e, k=k: e.collective_compute(
                "AllReduce", ALU.max, replica_groups=[list(range(8))],
                ins=[cc_in[k].ap().opt()], outs=[cc_out[k].ap().opt()]),
                waits=[(S("sGN"), S("sGN").n)], inc=(S("gRBC"), 1), drain=False,
                is_dma=True)
            em("sp", lambda e, k=k, n=n: e.dma_start(out=sscl[0:1, 4:4 + n],
                                                     in_=cc_out[k].ap()[0:1, 0:n]),
               waits=[(S("gRBC"), S("gRBC").n)], inc=(S("sGN"), 16))
            return (S("sGN"), S("sGN").n)

        def bscal(fn, waits=(), inc=None):
            em("dve", fn, waits=waits, inc=inc)

        def brd(k, src_lanes, dst, n):
            for ci, ln in enumerate(src_lanes):
                em("sp", lambda e, k=k, ci=ci, ln=ln: e.dma_start(
                    out=d_sc2.ap()[k:k + 1, ci:ci + 1], in_=sscl[0:1, ln:ln + 1]),
                    waits=[(S("vGR"), S("vGR").n)] if ci == 0 else (),
                    inc=(S("sGN2"), 16))
            em("sp", lambda e, k=k, n=n, dst=dst: e.dma_start(
                out=dst[:], in_=d_sc2.ap()[k, 0:n].partition_broadcast(128)),
               waits=[(S("sGN2"), S("sGN2").n)], inc=(S("sGN2"), 16))
            return (S("sGN2"), S("sGN2").n)

        for c in range(NEC):
            em("dve", lambda e, c=c: e.reduce_max(tmpr[:], Y[c][:], AX.X,
                                                  apply_absolute_value=True),
               waits=w(w_y2, mZ) if c == 0 else ())
            em("dve", lambda e: e.tensor_tensor(mx0[:], mx0[:], tmpr[:], ALU.max),
               inc=(S("vYM"), 1) if c == NEC - 1 else None)
        bB0 = emit_barrier(0, [mx0], [(S("vYM"), S("vYM").n)])
        bscal(lambda e: e.tensor_scalar(sscl[0:1, 8:9], sscl[0:1, 4:5],
                                        1.0 / 32767.0, None, ALU.mult),
              waits=[bB0])
        bscal(lambda e: e.tensor_scalar(sscl[0:1, 8:9], sscl[0:1, 8:9], 1e-12,
                                        None, ALU.max))
        bscal(lambda e: e.reciprocal(sscl[0:1, 9:10], sscl[0:1, 8:9]),
              inc=(S("vGR"), 1))
        s0done = brd(0, [8, 9], scb0, 2)
        for c in range(NEC):
            em("dve", lambda e, c=c: e.tensor_scalar(iA[:], Y[c][:], scb0[:, 1:2],
                                                     None, ALU.mult),
               waits=[s0done] if c == 0 else ())
            em("dve", lambda e, c=c: e.tensor_scalar(Y[c][:], iA[:], scb0[:, 0:1],
                                                     None, ALU.mult),
               inc=(S("vYM"), 1))
        xqDone = (S("vYM"), S("vYM").n)

        ctxLN.close()
        ctxS = ExitStack()
        FS_ = ctxS.enter_context
        wtu = [FS_(SBR(f"wtu{k}", [128, E], BF16)) for k in range(2)]
        wtg = [FS_(SBR(f"wtg{k}", [128, E], BF16)) for k in range(2)]
        upsg = [FS_(SBR(f"upsg{k}", [128, T], BF16)) for k in range(2)]
        gasg = [FS_(SBR(f"gasg{k}", [128, T], BF16)) for k in range(2)]
        gup3 = g_w_out["up"].ap().rearrange("c (p ec o) -> c p ec o", p=128, ec=8)
        gga3 = g_w_out["ga"].ap().rearrange("c (p ec o) -> c p ec o", p=128, ec=8)
        NOB = FF // 128
        upmm, gamm, upev, gaev, updm, gadm = {}, {}, {}, {}, {}, {}
        upmx, gamx = {}, {}
        for ob in range(NOB):
            sl = ob % 2
            em("sp", lambda e, ob=ob, sl=sl: e.dma_start(
                out=wtu[sl][:].rearrange("p (c o) -> p c o", o=128),
                in_=gup3[ob // 3, :, :, (ob % 3) * 128:(ob % 3) * 128 + 128]),
                waits=w((S("gRBC"), gW["up"]) if ob == 0 else None, upmm.get(ob - 2)),
                inc=(S("ex0" if sl == 0 else "ex1"), 16))
            wU = (S("ex0" if sl == 0 else "ex1"), S("ex0" if sl == 0 else "ex1").n)
            em("sp", lambda e, ob=ob, sl=sl: e.dma_start(
                out=wtg[sl][:].rearrange("p (c o) -> p c o", o=128),
                in_=gga3[ob // 3, :, :, (ob % 3) * 128:(ob % 3) * 128 + 128]),
                waits=w((S("gRBC"), gW["ga"]) if ob == 0 else None, gamm.get(ob - 2)),
                inc=(S("pv0" if sl == 0 else "pv1"), 16))
            wG = (S("pv0" if sl == 0 else "pv1"), S("pv0" if sl == 0 else "pv1").n)
            for ic in range(4):
                for c in range(NEC):
                    em("pe", lambda e, sl=sl, ic=ic, c=c: e.matmul(
                        psF1[:, ic * 512:(ic + 1) * 512],
                        wtu[sl][:, c * 128:(c + 1) * 128],
                        Y[c][:, ic * 512:(ic + 1) * 512],
                        start=(c == 0), stop=(c == NEC - 1)),
                        waits=(w(wU, xqDone if ob == 0 else None,
                                 (S("aROT"), S("aROT").n) if ob == 0 else None,
                                 upev.get(ob - 1))
                               if (ic == 0 and c == 0) else ()),
                        inc=(S("pQK"), 1) if (ic == 3 and c == NEC - 1) else None)
            upmm[ob] = (S("pQK"), S("pQK").n)
            for ic in range(4):
                for c in range(NEC):
                    em("pe", lambda e, sl=sl, ic=ic, c=c: e.matmul(
                        psF2[:, ic * 512:(ic + 1) * 512],
                        wtg[sl][:, c * 128:(c + 1) * 128],
                        Y[c][:, ic * 512:(ic + 1) * 512],
                        start=(c == 0), stop=(c == NEC - 1)),
                        waits=(w(wG, xqDone if ob == 0 else None,
                                 (S("L1ROWS"), S("L1ROWS").n) if ob == 0 else None,
                                 gaev.get(ob - 1))
                               if (ic == 0 and c == 0) else ()),
                        inc=(S("pG"), 1) if (ic == 3 and c == NEC - 1) else None)
            gamm[ob] = (S("pG"), S("pG").n)
            em("act", lambda e, sl=sl: e.activation(upsg[sl][:], psF1[:], AF.Copy),
               waits=w(upmm[ob], updm.get(ob - 2), upmx.get(ob - 2)),
               inc=(S("aQK"), 1))
            upev[ob] = (S("aQK"), S("aQK").n)
            em("act", lambda e, sl=sl: e.activation(gasg[sl][:], psF2[:], AF.Copy),
               waits=w(gamm[ob], gadm.get(ob - 2), gamx.get(ob - 2)),
               inc=(S("aPP"), 1))
            gaev[ob] = (S("aPP"), S("aPP").n)
            em("act", lambda e, ob=ob, sl=sl: e.dma_start(
                out=d_up.ap()[ob * 128:(ob + 1) * 128, :], in_=upsg[sl][:]),
                waits=[upev[ob]], inc=(S("sO"), 16), drain=False, is_dma=True)
            updm[ob] = (S("sO"), S("sO").n)
            em("act", lambda e, ob=ob, sl=sl: e.dma_start(
                out=d_ga.ap()[ob * 128:(ob + 1) * 128, :], in_=gasg[sl][:]),
                waits=[gaev[ob]], inc=(S("sR16"), 16), drain=False, is_dma=True)
            gadm[ob] = (S("sR16"), S("sR16").n)
            em("dve", lambda e, sl=sl: e.reduce_max(tmpr[:], upsg[sl][:], AX.X,
                                                    apply_absolute_value=True),
               waits=w(upev[ob], mZ if ob == 0 else None))
            em("dve", lambda e: e.tensor_tensor(mxu[:], mxu[:], tmpr[:], ALU.max),
               inc=(S("aNQ"), 1))
            upmx[ob] = (S("aNQ"), S("aNQ").n)
            em("dve", lambda e, sl=sl: e.reduce_max(tmpr[:], gasg[sl][:], AX.X,
                                                    apply_absolute_value=True),
               waits=[gaev[ob]])
            em("dve", lambda e: e.tensor_tensor(mxg[:], mxg[:], tmpr[:], ALU.max),
               inc=(S("vNM"), 1))
            gamx[ob] = (S("vNM"), S("vNM").n)

        bB1 = emit_barrier(1, [mxu, mxg],
                           [(S("aNQ"), S("aNQ").n), (S("vNM"), S("vNM").n)])
        bscal(lambda e: e.tensor_scalar(sscl[0:1, 8:10], sscl[0:1, 4:6],
                                        1.0 / 32767.0, None, ALU.mult),
              waits=[bB1])
        bscal(lambda e: e.tensor_scalar(sscl[0:1, 8:10], sscl[0:1, 8:10], 1e-12,
                                        None, ALU.max))
        bscal(lambda e: e.reciprocal(sscl[0:1, 10:12], sscl[0:1, 8:10]))
        bscal(lambda e: e.tensor_scalar(sscl[0:1, 12:13], sscl[0:1, 9:10],
                                        float(2.0 ** -16), None, ALU.mult),
              inc=(S("vGR"), 1))
        s1done = brd(1, [11, 9, 12, 10], scb1, 4)

        ctxS.close()
        ctxA.close()
        ctxM = ExitStack()
        M_ = ctxM.enter_context
        garaw = [M_(SBL(f"gar{k}", [128, T], BF16)) for k in range(NOB)]
        dwt = [M_(SBL(f"dwt{k}", [128, 512], BF16)) for k in range(2)]
        garld = {}
        phase5_done = w(upmm[NOB - 1], gamm[NOB - 1], upev[NOB - 1], gaev[NOB - 1],
                        (S("sR16"), S("sR16").n), (S("sO"), S("sO").n))
        g2p = {}
        ia_done, sg_done, ib_done = {}, {}, {}

        def g2_stageA(ob):
            iA_t = iA if ob % 2 == 0 else iA3
            Xp = X[1 + ob % 2]
            em("pool", lambda e, ob=ob: e.dma_start(
                out=garaw[ob][:], in_=d_ga.ap()[ob * 128:(ob + 1) * 128, :]),
                waits=(phase5_done + [s1done] if ob == 0 else []),
                inc=(S("vGST"), 16), drain=False, is_dma=True)
            garld[ob] = (S("vGST"), S("vGST").n)
            em("dve", lambda e, ob=ob, iA_t=iA_t: e.tensor_scalar(
                iA_t[:], garaw[ob][:], scb1[:, 0:1], None, ALU.mult),
               waits=w(garld[ob], sX1 if ob == 0 else None),
               inc=(S("qk0"), 1))
            ia_done[ob] = (S("qk0"), S("qk0").n)
            em("act", lambda e, iA_t=iA_t, Xp=Xp: e.activation(
                Xp[:], iA_t[:], AF.Sigmoid, scale=scb1[:, 1:2]),
               waits=w(ia_done[ob], ib_done.get(ob - 2)), inc=(S("aG"), 1))
            sg_done[ob] = (S("aG"), S("aG").n)

        def g2_stageB(ob):
            iA_t = iA if ob % 2 == 0 else iA3
            iB_t = iA2
            Xp = X[1 + ob % 2]
            em("dve", lambda e, iB_t=iB_t, Xp=Xp: e.tensor_scalar(
                iB_t[:], Xp[:], 65536.0, 65535.0, ALU.mult, ALU.min),
               waits=[sg_done[ob]], inc=(S("qk0"), 1))
            ib_done[ob] = (S("qk0"), S("qk0").n)
            em("dve", lambda e, ob=ob, iA_t=iA_t, iB_t=iB_t: e.scalar_tensor_tensor(
                garaw[ob][:], iA_t[:], scb1[:, 2:3], iB_t[:], ALU.mult, ALU.mult))
            em("dve", lambda e, ob=ob: e.reduce_max(tmpr[:], garaw[ob][:], AX.X,
                                                    apply_absolute_value=True))
            em("dve", lambda e: e.tensor_tensor(mx2[:], mx2[:], tmpr[:], ALU.max),
               inc=(S("vGVAR"), 1))
            g2p[ob] = (S("vGVAR"), S("vGVAR").n)

        for ob in range(NOB + 1):
            if ob < NOB:
                g2_stageA(ob)
            if ob >= 1:
                g2_stageB(ob - 1)

        bB2 = emit_barrier(2, [mx2], [(S("vGVAR"), S("vGVAR").n)])
        bscal(lambda e: e.tensor_scalar(sscl[0:1, 13:14], sscl[0:1, 4:5],
                                        1.0 / 32767.0, None, ALU.mult),
              waits=[bB2])
        bscal(lambda e: e.tensor_scalar(sscl[0:1, 13:14], sscl[0:1, 13:14], 1e-12,
                                        None, ALU.max))
        bscal(lambda e: e.reciprocal(sscl[0:1, 14:15], sscl[0:1, 13:14]))
        bscal(lambda e: e.tensor_tensor(sscl[0:1, 15:16], sscl[0:1, 13:14],
                                        sscl[0:1, 8:9], ALU.mult),
              inc=(S("vGR"), 1))
        s2done = brd(2, [14, 15], scb2, 2)

        odm, ow, op_ = {}, {}, {}
        for ob in range(NOB):
            sl = ob % 2
            em("pool", lambda e, ob=ob, sl=sl: e.dma_start(
                out=X[6 + sl][:], in_=d_up.ap()[ob * 128:(ob + 1) * 128, :]),
                waits=w((S("sO"), S("sO").n) if ob == 0 else None,
                        op_.get(ob - 2)),
                inc=(S("aLD" if sl == 0 else "vLDN"), 16), drain=False, is_dma=True)
            lU = (S("aLD" if sl == 0 else "vLDN"),
                  S("aLD" if sl == 0 else "vLDN").n)
            em("dve", lambda e, sl=sl: e.tensor_scalar(iA[:], X[6 + sl][:],
                                                       scb1[:, 3:4], None, ALU.mult),
               waits=w(lU, s2done if ob == 0 else None, g2p.get(NOB - 1) if ob == 0 else None),
               inc=(S("ev0"), 1))
            op_[ob] = (S("ev0"), S("ev0").n)
            em("dve", lambda e, ob=ob: e.tensor_scalar(iA2[:], garaw[ob][:],
                                                       scb2[:, 0:1], None, ALU.mult))
            em("dve", lambda e, ob=ob: e.scalar_tensor_tensor(
                garaw[ob][:], iA[:], scb2[:, 1:2], iA2[:], ALU.mult, ALU.mult),
               inc=(S("ev0"), 1))
            em("dve", lambda e, ob=ob: e.reduce_max(tmpr[:], garaw[ob][:], AX.X,
                                                    apply_absolute_value=True))
            em("dve", lambda e: e.tensor_tensor(mx3[:], mx3[:], tmpr[:], ALU.max),
               inc=(S("aGSD"), 1))

        bB3 = emit_barrier(3, [mx3], [(S("aGSD"), S("aGSD").n)])
        bscal(lambda e: e.tensor_scalar(sscl[0:1, 16:17], sscl[0:1, 4:5],
                                        1.0 / 32767.0, None, ALU.mult),
              waits=[bB3])
        bscal(lambda e: e.tensor_scalar(sscl[0:1, 16:17], sscl[0:1, 16:17], 1e-12,
                                        None, ALU.max))
        bscal(lambda e: e.reciprocal(sscl[0:1, 17:18], sscl[0:1, 16:17]),
              inc=(S("vGR"), 1))
        s3done = brd(3, [17, 16], scb3, 2)

        oq = {}
        for ob in range(NOB):
            em("dve", lambda e, ob=ob: e.tensor_scalar(iA[:], garaw[ob][:],
                                                       scb3[:, 0:1], None, ALU.mult),
               waits=w(s3done if ob == 0 else None,
                       (S("aGSD"), S("aGSD").n) if ob == 0 else None,
                       oq.get(ob - 1)),
               inc=(S("ev0"), 1))
            em("act", lambda e, ob=ob: e.activation(garaw[ob][:], iA[:],
                                                    AF.Identity, scale=scb3[:, 1:2]),
               waits=[(S("ev0"), S("ev0").n)], inc=(S("vGSC"), 1))
            oq[ob] = (S("vGSC"), S("vGSC").n)

        gdo3 = g_w_out["do"].ap().rearrange("c (p ic o) -> c p ic o", p=128, ic=24)
        dmm, dev = {}, {}
        for o2c in range(NEC):
            ps = psF1 if o2c % 2 == 0 else psF2
            for g in range(NOB // 4):
                sl = g % 2
                prev_use = (dmm.get((o2c, 4 * g - 5)) if g >= 2
                            else (dmm.get((o2c - 1, 4 * g + 19))
                                  if o2c > 0 else None))
                em("sp", lambda e, g=g, sl=sl, o2c=o2c: e.dma_start(
                    out=dwt[sl][:].rearrange("p (j o) -> p j o", o=128),
                    in_=gdo3[o2c, :, 4 * g:4 * g + 4, :]),
                    waits=w((S("gRBC"), gW["do"]) if (o2c == 0 and g == 0) else None,
                            prev_use),
                    inc=(S("aV" if sl == 0 else "pV"), 16))
                lW = (S("aV" if sl == 0 else "pV"),
                      S("aV" if sl == 0 else "pV").n)
                for j in range(4):
                    i24 = 4 * g + j
                    for ic in range(4):
                        em("pe", lambda e, i24=i24, j=j, sl=sl, ic=ic, ps=ps: e.matmul(
                            ps[:, ic * 512:(ic + 1) * 512],
                            dwt[sl][:, j * 128:(j + 1) * 128],
                            garaw[i24][:, ic * 512:(ic + 1) * 512],
                            start=(i24 == 0), stop=(i24 == NOB - 1)),
                            waits=(w(lW if j == 0 else None, oq[i24],
                                     dev.get(o2c - 2) if j == 0 else None,
                                     (S("aQK"), S("aQK").n) if (o2c == 0 and i24 == 0) else None,
                                     (S("aPP"), S("aPP").n) if (o2c == 1 and i24 == 0) else None,
                                     (S("L1ROWS"), S("L1ROWS").n) if (o2c == 1 and i24 == 0) else None)
                                   if ic == 0 else ()),
                            inc=(S("pNRM"), 1) if ic == 3 else None)
                    dmm[(o2c, i24)] = (S("pNRM"), S("pNRM").n)
            xsl = 5 + o2c % 2
            em("sp", lambda e, o2c=o2c, xsl=xsl: e.dma_start(
                out=X[xsl][:], in_=d_x1.ap()[o2c * 128:(o2c + 1) * 128, :]),
                waits=w(sX1 if o2c == 0 else None, dev.get(o2c - 2)),
                inc=(S("sLp"), 16))
            ysl = 3 + o2c % 2
            em("dve", lambda e, ps=ps, xsl=xsl, ysl=ysl: e.tensor_tensor(
                X[ysl][:], ps[:], X[xsl][:], ALU.add),
               waits=w(dmm[(o2c, NOB - 1)], (S("sLp"), S("sLp").n),
                       dev.get(o2c - 2)),
               inc=(S("vOG"), 1))
            dv = (S("vOG"), S("vOG").n)
            em("pool", lambda e, o2c=o2c, ysl=ysl: e.dma_start(
                out=yout[o2c * 128:(o2c + 1) * 128, :], in_=X[ysl][:]),
                waits=[dv], inc=(S("vAPP"), 16), drain=False, is_dma=True)
            dev[o2c] = (S("vAPP"), S("vAPP").n)
        em("sp", lambda e: e.nop(), waits=[(S("vAPP"), S("vAPP").n)] +
           ([(S("sDBG"), S("sDBG").n), (S("sDBQ"), S("sDBQ").n),
             (S("sDBP"), S("sDBP").n), (S("sDBP2"), S("sDBP2").n)] if KDBG else []),
           drain=False)
        ctxM.close()
        ctxF2.close()
        ctxFP.close()

    with ctx0, ctxA, ctxB:
        with nc.Block() as block:
            @block.sync
            def _(sp):
                EM.run("sp", sp)

            @block.tensor
            def _(pe):
                EM.run("pe", pe)

            @block.vector
            def _(dve):
                EM.run("dve", dve)

            @block.scalar
            def _(act):
                EM.run("act", act)

            @block.gpsimd
            def _(pool):
                EM.run("pool", pool)
    return nc


def kernel(**inputs):
    g = lambda k: np.ascontiguousarray(np.asarray(inputs[k], dtype=np.float32))
    x = g("L0_attn_x")
    TT_ = lambda a: np.ascontiguousarray(a.T)
    base = {
        "wqT": TT_(g("L0_attn_wq")), "wkT": TT_(g("L0_attn_wk")),
        "wvT": TT_(g("L0_attn_wv")), "wgT": TT_(g("L0_attn_wg")),
        "woT": TT_(g("L0_attn_wo")), "wdT": TT_(g("L0_attn_wd")),
        "bq": np.ascontiguousarray(g("L0_attn_bq").reshape(NEC, 128).T),
        "bk": np.ascontiguousarray(g("L0_attn_bk").reshape(NEC, 128).T),
        "bg": np.ascontiguousarray(g("L0_attn_bg").reshape(NEC, 128).T),
        "bo": np.ascontiguousarray(g("L0_attn_bo").reshape(NEC, 128).T),
        "gamma": np.ascontiguousarray(g("L0_attn_gamma").reshape(NEC, 128).T),
        "beta": np.ascontiguousarray(g("L0_attn_beta").reshape(NEC, 128).T),
        "bv": g("L0_attn_bv").reshape(1, E),
        "bd": g("L0_attn_bd").reshape(H, 1),
    }
    wupN = g("L0_ffn_w_up")
    wgaN = g("L0_ffn_w_gate")
    wdoN = g("L0_ffn_w_down")
    nc = build_nc()
    in_maps = []
    for b in range(8):
        m = dict(base)
        m["x"] = TT_(x[b])
        m["wupNs"] = np.ascontiguousarray(wupN[b * 384:(b + 1) * 384, :])
        m["wgaNs"] = np.ascontiguousarray(wgaN[b * 384:(b + 1) * 384, :])
        m["wdoNs"] = np.ascontiguousarray(wdoN[b * 128:(b + 1) * 128, :])
        in_maps.append(m)
    res = run_bass_kernel_spmd(nc, in_maps, core_ids=list(range(8)))
    out = np.stack([np.asarray(res.results[b]["y"]).T for b in range(8)])
    return out.astype(np.float32)


# version 21
# speedup vs baseline: 1.0539x; 1.0539x over previous
import os
import numpy as np
from contextlib import ExitStack

import concourse.bass as bass
import concourse.mybir as mybir
from concourse.bass_utils import run_bass_kernel_spmd

F32 = mybir.dt.float32
BF16 = mybir.dt.bfloat16
I32 = mybir.dt.int32
FP16 = mybir.dt.float16
AF = mybir.ActivationFunctionType
ALU = mybir.AluOpType
AX = mybir.AxisListType

T, E, H, HD, FF = 2048, 1024, 16, 64, 3072
NEC = E // 128
NTB = T // 128
TWOPI = float(2.0 * np.pi)
KSTOP = os.environ.get("KSTOP", "full")


class Sem:
    def __init__(self, ctx, nc, name):
        self.h = ctx.enter_context(nc.semaphore(name))
        self.n = 0


class Emitter:
    ENGS = ("pe", "dve", "act", "pool", "sp")

    def __init__(self):
        self.ops = {e: [] for e in self.ENGS}

    def em(self, eng, fn, waits=(), inc=None, drain=None, is_dma=False):
        if drain is None:
            drain = eng in ("dve", "act", "pool")
        w = tuple((s, v) for (s, v) in waits if v > 0)
        r = None
        if inc is not None:
            s, k = inc
            s.n += k
            r = s.n
        self.ops[eng].append((w, fn, inc, r, drain, is_dma))
        return r

    def run(self, eng, engobj):
        compute = eng in ("dve", "act", "pool")
        for w, fn, inc, _r, drain, is_dma in self.ops[eng]:
            for s, v in w:
                engobj.wait_ge(s.h, v)
            if drain:
                engobj.drain()
            inst = fn(engobj)
            if inc is not None:
                if compute and not is_dma:
                    engobj.drain().then_inc(inc[0].h, inc[1])
                else:
                    inst.then_inc(inc[0].h, inc[1])


def build_nc():
    nc = bass.Bass("TRN2", target_bir_lowering=False)

    xT = nc.dram_tensor("x", [E, T], F32, kind="ExternalInput").ap()
    wT = {nm: nc.dram_tensor(nm + "T", [E, E], F32, kind="ExternalInput").ap()
          for nm in ("wq", "wk", "wv", "wg", "wo")}
    wdT = nc.dram_tensor("wdT", [E, H], F32, kind="ExternalInput").ap()
    bfm_names = ("bq", "bk", "bg", "bo", "gamma", "beta")
    bfm = {nm: nc.dram_tensor(nm, [128, NEC], F32, kind="ExternalInput").ap()
           for nm in bfm_names}
    bv_in = nc.dram_tensor("bv", [1, E], F32, kind="ExternalInput").ap()
    bd_in = nc.dram_tensor("bd", [H, 1], F32, kind="ExternalInput").ap()
    wupNs = nc.dram_tensor("wupNs", [384, E], F32, kind="ExternalInput").ap()
    wgaNs = nc.dram_tensor("wgaNs", [384, E], F32, kind="ExternalInput").ap()
    wdoNs = nc.dram_tensor("wdoNs", [128, FF], F32, kind="ExternalInput").ap()
    yout = nc.dram_tensor("y", [E, T], F32, kind="ExternalOutput").ap()
    KDBG = os.environ.get("KDBG")
    if KDBG:
        dbg = {nm: nc.dram_tensor("dbg_" + nm, shp, F32, kind="ExternalOutput").ap()
               for nm, shp in (("y0", [128, T]), ("ld", [H, 1]), ("qf", [64, 2 * T]),
                               ("kf", [64, 2 * T]), ("vp", [128, T]), ("og0", [128, T]),
                               ("mneg", [128, 2]), ("cos", [64, T]), ("sin", [64, T]),
                               ("gate", [128, T]), ("og0p", [128, T]),
                               ("pf", [65, 1024]), ("pp", [65, 1024]),
                               ("qf0", [64, 2 * T]), ("kf0", [64, 2 * T]),
                               ("vp0", [128, T]), ("psq0", [128, T]),
                               ("qpre0", [64, 2 * T]), ("wbfq0", [128, E]))}
        dbg.update({f"yc{c}": nc.dram_tensor(f"dbg_yc{c}", [128, T], F32,
                                             kind="ExternalOutput").ap()
                    for c in range(8)})

    d_row = nc.dram_tensor("d_row", [4, T], F32)
    d_ld = nc.dram_tensor("d_ld", [H, 1], F32)
    d_c16 = nc.dram_tensor("d_c16", [H, T], F32)
    d_r16 = nc.dram_tensor("d_r16", [H, T], F32)
    d_sc = nc.dram_tensor("d_sc", [1, 64], F32)
    d_x1 = nc.dram_tensor("d_x1", [E, T], F32)
    d_up = nc.dram_tensor("d_up", [FF, T], BF16)
    d_ga = nc.dram_tensor("d_ga", [FF, T], BF16)
    d_ss = nc.dram_tensor("d_ss", [1, 12288], F32)
    d_red = nc.dram_tensor("d_red", [8, 128], F32)
    d_sc2 = nc.dram_tensor("d_sc2", [4, 8], F32)
    g_w_in = {nm: nc.dram_tensor(f"g_{nm}_in", [1, 128 * FF], BF16)
              for nm in ("up", "ga", "do")}
    g_w_out = {nm: nc.dram_tensor(f"g_{nm}_out", [8, 128 * FF], BF16)
               for nm in ("up", "ga", "do")}
    cc_in = [nc.dram_tensor(f"cc_in{k}", [1, 512], F32) for k in range(4)]
    cc_out = [nc.dram_tensor(f"cc_out{k}", [1, 512], F32) for k in range(4)]

    ctx0 = ExitStack()

    def P(t):
        return ctx0.enter_context(t)

    def SBL(name, shape, dt):
        return nc.sbuf_tensor(name, shape, dt, side="left")

    def SBR(name, shape, dt):
        return nc.sbuf_tensor(name, shape, dt, side="right")

    def PSL(name, shape, dt):
        return nc.psum_tensor(name, shape, dt, side="left")

    def PSR(name, shape, dt):
        return nc.psum_tensor(name, shape, dt, side="right")

    SEMS = {}

    def S(name):
        if name not in SEMS:
            SEMS[name] = Sem(ctx0, nc, name)
        return SEMS[name]

    X = [P(SBL(f"X{c}", [128, T], F32)) for c in range(NEC)]
    sb_b = {nm: P(SBL("s_" + nm, [128, NEC], F32)) for nm in bfm_names}
    bv_bf = P(SBL("bv_bf", [1, E], BF16))
    wd_sb = P(SBL("wd_sb", [128, NEC * H], BF16))
    bd_sb = P(SBL("bd_sb", [H, 1], F32))
    ones_row = P(SBL("ones_row", [1, 128], BF16))
    ones_col = P(SBL("ones_col", [128, 1], BF16))
    Drelu0 = P(SBL("Drelu0", [128, 128], BF16))
    ldI = P(SBL("ldI", [128, H * 128], BF16))
    lam = P(SBL("lam", [128, H * 16], F32))
    cos64 = P(SBL("cos64", [64, T], BF16))
    sinS = P(SBL("sinS", [64, T], BF16))
    ld16 = P(SBL("ld16", [H, 1], F32))
    sc2 = P(SBL("sc2", [128, 8], F32))
    Pi64 = P(SBL("Pi64", [64, 64], BF16))
    ceps = P(SBL("ceps", [65, 1], F32))
    cb5 = P(SBL("cb5", [16, 1], F32))
    z65 = P(SBL("z65", [1, 65], BF16))
    bk8 = P(SBL("bk8", [128, NEC], F32))

    EM = Emitter()
    em = EM.em

    ctxP0 = ExitStack()
    P0 = ctxP0.enter_context
    it_t = P0(SBR("it_t", [128, T], F32))
    it_jp = P0(SBR("it_jp", [128, 16], F32))
    I128 = P0(SBR("I128", [128, 128], F32))
    ldbc = P0(SBR("ldbc", [128, H], F32))
    ldn16 = P0(SBR("ldn16", [H, 1], F32))
    wd_f32 = P0(SBR("wd_f32", [128, NEC * H], F32))
    bv_f32 = P0(SBR("bv_f32", [1, E], F32))

    for c in range(NEC):
        em("sp", lambda e, c=c: e.dma_start(
            out=X[c][:], in_=xT[c * 128:(c + 1) * 128, :]), inc=(S("sLx"), 16))
    LX = S("sLx").n
    for nm in bfm_names:
        em("sp", lambda e, nm=nm: e.dma_start(out=sb_b[nm][:], in_=bfm[nm]),
           inc=(S("sLp"), 16))
    em("sp", lambda e: e.dma_start(out=bd_sb[:], in_=bd_in), inc=(S("sLp"), 16))
    em("sp", lambda e: e.dma_start(
        out=wd_f32[:].rearrange("p (c h) -> p c h", h=H),
        in_=wdT.rearrange("(c p) h -> p c h", p=128)), inc=(S("sLp"), 16))
    em("sp", lambda e: e.dma_start(out=bv_f32[:], in_=bv_in), inc=(S("sLp"), 16))
    LP = S("sLp").n

    em("pool", lambda e: e.memset(z65[:], 0.0))
    em("pool", lambda e: e.memset(ceps[:], 1e-5))
    em("pool", lambda e: e.memset(cb5[:], float(-5.0 * np.log(2.0))))
    em("pool", lambda e: e.iota(it_t[:], pattern=[[1, T]], base=0, channel_multiplier=0,
                                allow_small_or_imprecise_dtypes=True))
    em("pool", lambda e: e.iota(it_jp[:], pattern=[[128, 16]], base=0, channel_multiplier=1,
                                allow_small_or_imprecise_dtypes=True))
    em("pool", lambda e: e.memset(I128[:], 1.0))
    em("pool", lambda e: e.affine_select(I128[:], I128[:], pattern=[[1, 128]],
                                         compare_op=ALU.is_equal, fill=0.0,
                                         base=0, channel_multiplier=-1))
    em("pool", lambda e: e.tensor_copy(wd_sb[:], wd_f32[:]),
       waits=[(S("sLp"), LP)], inc=(S("cP"), 1))

    ctxT = ExitStack()
    PT = ctxT.enter_context
    it_tmp = PT(SBR("it_tmp", [64, T], F32))
    it_i32 = PT(SBR("it_i32", [64, T], I32))
    frac = PT(SBR("frac", [64, T], F32))
    invf = PT(SBR("invf", [64, 1], F32))
    idv = PT(SBR("idv", [64, 1], I32))
    par = PT(SBR("par", [64, 1], F32))
    emask = PT(SBR("emask", [64, 1], F32))
    pm0 = PT(SBR("pm0", [64, 64], F32))
    pm1 = PT(SBR("pm1", [64, 64], F32))
    dscr = PT(SBR("dscr", [128, 128], F32))

    em("pool", lambda e: e.iota(dscr[:], pattern=[[-1, 128]], base=0,
                                channel_multiplier=1, allow_small_or_imprecise_dtypes=True))
    em("pool", lambda e: e.tensor_scalar(Drelu0[:], dscr[:], 0.0, None, ALU.max),
       inc=(S("cDRELU"), 1))
    em("pool", lambda e: e.iota(idv[:], pattern=[[1, 1]], base=0, channel_multiplier=1),
       inc=(S("cIOTA"), 1))
    em("dve", lambda e: e.tensor_scalar(idv[:], idv[:], 1, None, ALU.bitwise_and),
       waits=[(S("cIOTA"), 1)], inc=(S("cPAR"), 1))
    em("dve", lambda e: e.tensor_copy(par[:], idv[:]), waits=[(S("cPAR"), 1)])
    em("dve", lambda e: e.tensor_scalar(emask[:], par[:], -1.0, 1.0, ALU.mult, ALU.add),
       inc=(S("vMASK"), 1))
    em("pool", lambda e: e.iota(idv[:], pattern=[[1, 1]], base=0, channel_multiplier=1),
       waits=[(S("vMASK"), 1)], inc=(S("cIOTA"), 1))
    em("dve", lambda e: e.tensor_scalar(idv[:], idv[:], 31, None, ALU.bitwise_and),
       waits=[(S("cIOTA"), 2)], inc=(S("cD32"), 1))
    em("dve", lambda e: e.tensor_copy(invf[:], idv[:]), waits=[(S("cD32"), 1)],
       inc=(S("vINVF"), 1))
    em("act", lambda e: e.activation(invf[:], invf[:], AF.Exp,
                                     scale=float(-np.log(10000.0) / 32.0)),
       waits=[(S("vINVF"), 1)], inc=(S("aINVF"), 1))
    em("pool", lambda e: e.memset(pm0[:], 1.0))
    em("pool", lambda e: e.affine_select(pm0[:], pm0[:], pattern=[[1, 64]],
                                         compare_op=ALU.is_equal, fill=0.0,
                                         base=-1, channel_multiplier=-1))
    em("pool", lambda e: e.memset(pm1[:], 1.0))
    em("pool", lambda e: e.affine_select(pm1[:], pm1[:], pattern=[[1, 64]],
                                         compare_op=ALU.is_equal, fill=0.0,
                                         base=1, channel_multiplier=-1), inc=(S("cPM"), 1))
    em("dve", lambda e: e.tensor_scalar(pm0[:], pm0[:], emask[:], None, ALU.mult),
       waits=[(S("cPM"), 1)])
    em("dve", lambda e: e.tensor_scalar(pm1[:], pm1[:], par[:], None, ALU.mult))
    em("dve", lambda e: e.tensor_tensor(Pi64[:], pm0[:], pm1[:], ALU.add),
       inc=(S("vPI"), 1))

    tcor = PT(SBR("tcor", [64, T], F32))

    def emit_trig(dst, shift, idx):
        w0 = [(S("aINVF"), 1), (S("cP"), 1)]
        if idx == 1:
            w0.append((S("aTRG"), 1))
        em("dve", lambda e: e.tensor_scalar(it_tmp[:], it_t[0:64, :], invf[:, 0:1],
                                            None, ALU.mult), waits=w0)
        em("dve", lambda e, sh=shift: e.tensor_scalar(it_tmp[:], it_tmp[:],
                                                      float(1.0 / TWOPI), sh + 1.0,
                                                      ALU.mult, ALU.add))
        em("dve", lambda e: e.tensor_copy(it_i32[:], it_tmp[:]))
        em("dve", lambda e: e.tensor_copy(frac[:], it_i32[:]))
        em("dve", lambda e: e.tensor_tensor(tcor[:], frac[:], it_tmp[:], ALU.is_gt))
        em("dve", lambda e: e.tensor_tensor(frac[:], frac[:], tcor[:], ALU.subtract))
        em("dve", lambda e: e.scalar_tensor_tensor(frac[:], it_tmp[:], -0.5, frac[:],
                                                   ALU.add, ALU.subtract),
           inc=(S("vTRGx"), 1))
        em("act", lambda e, d=dst: e.activation(d, frac[:], AF.Sin, scale=TWOPI),
           waits=[(S("vTRGx"), idx + 1)], inc=(S("aTRG"), 1))
    emit_trig(cos64[:], 0.25, 0)
    emit_trig(sinS[:], 0.0, 1)
    em("dve", lambda e: e.tensor_scalar(cos64[:], cos64[:], -1.0, None, ALU.mult),
       waits=[(S("aTRG"), 1)], inc=(S("vTRIG"), 1))
    em("dve", lambda e: e.tensor_scalar(par[:], par[:], -2.0, 1.0, ALU.mult, ALU.add),
       waits=[(S("aTRG"), 2)])
    em("dve", lambda e: e.tensor_scalar(sinS[:], sinS[:], par[:, 0:1], None, ALU.mult),
       inc=(S("vTRIG"), 1))
    em("dve", lambda e: e.memset(ones_row[:], 1.0))
    em("dve", lambda e: e.memset(ones_col[:], 1.0), inc=(S("vONES"), 1))
    em("dve", lambda e: e.tensor_copy(bv_bf[:], bv_f32[:]), waits=[(S("sLp"), LP)],
       inc=(S("vBV"), 1))
    ctxT.close()

    ctxA = ExitStack()
    A = ctxA.enter_context
    Y = [A(SBL(f"Y{c}", [128, T], BF16)) for c in range(NEC)]

    ctxL = ExitStack()
    PL = ctxL.enter_context
    cb = [PL(SBR(f"cb{k}", [128, T], BF16)) for k in range(2)]
    sqb = [PL(SBR(f"sqb{k}", [128, T], BF16)) for k in range(2)]
    Rbc = PL(SBR("Rbc", [128, T], F32))
    Nbc = PL(SBR("Nbc", [128, T], F32))
    tmpc = PL(SBR("tmpc", [128, T], F32))
    row_m = PL(SBR("row_m", [1, T], F32))
    row_v = PL(SBR("row_v", [1, T], F32))
    sc2bf = PL(SBR("sc2bf", [128, 8], BF16))
    c16t = PL(SBR("c16t", [H, T], F32))
    psStat = PL(PSR("psStat", [65, T], F32))
    psPD = PL(PSR("psPD", [H, 1], F32))
    selA = PL(SBR("selA", [128, 65], BF16))
    selB = PL(SBR("selB", [128, 65], BF16))

    def emit_ln(pref, src_chunks, dst_chunks, x_wait, sel_wait, tl):
        em("pool", lambda e: e.memset(tl["selA"][:], 0.0), waits=sel_wait)
        em("pool", lambda e: e.memset(tl["selA"][:, 0:1], 1.0))
        em("pool", lambda e: e.memset(tl["selB"][:], 0.0))
        em("pool", lambda e: e.memset(tl["selB"][:, 64:65], 1.0),
           inc=(S(pref + "SEL"), 1))
        nsel = S(pref + "SEL").n
        psStat = tl["psStat"]
        mm_done = [0, 0]
        for c in range(NEC):
            b = c % 2
            em("dve", lambda e, c=c, b=b: e.tensor_copy(tl["cb"][b][:], src_chunks[c][:]),
               waits=list(x_wait) + ([(S(pref + f"MM{b}"), mm_done[b])] if mm_done[b] else []),
               inc=(S(pref + f"CB{b}"), 1))
            vc = S(pref + f"CB{b}").n
            em("act", lambda e, c=c, b=b: e.activation(tl["sqb"][b][:], src_chunks[c][:],
                                                       AF.Square),
               waits=list(x_wait) + ([(S(pref + f"MM{b}"), mm_done[b])] if mm_done[b] else []),
               inc=(S(pref + f"SQ{b}"), 1))
            ac = S(pref + f"SQ{b}").n
            for ic in range(4):
                em("pe", lambda e, c=c, b=b, ic=ic: e.matmul(
                    psStat[:, ic * 512:(ic + 1) * 512], tl["selA"][:],
                    tl["cb"][b][:, ic * 512:(ic + 1) * 512],
                    start=(c == 0), stop=False),
                    waits=[(S(pref + f"CB{b}"), vc), (S(pref + "SEL"), nsel),
                           (S("vONES"), 1)] if ic == 0 else ())
            for ic in range(4):
                em("pe", lambda e, c=c, b=b, ic=ic: e.matmul(
                    psStat[:, ic * 512:(ic + 1) * 512], tl["selB"][:],
                    tl["sqb"][b][:, ic * 512:(ic + 1) * 512],
                    start=False, stop=(c == NEC - 1)),
                    waits=[(S(pref + f"SQ{b}"), ac)] if ic == 0 else (),
                    inc=(S(pref + f"MM{b}"), 1) if ic == 3 else None)
            mm_done[b] = S(pref + f"MM{b}").n
        em("act", lambda e: e.activation(tl["row_m"][:], psStat[0:1, :], AF.Copy,
                                         scale=1.0 / E),
           waits=[(S(pref + "MM0"), mm_done[0]), (S(pref + "MM1"), mm_done[1])])
        em("act", lambda e: e.activation(tl["row_v"][:], psStat[64:65, :], AF.Copy,
                                         scale=1.0 / E),
           inc=(S(pref + "ROWS"), 1))
        nrows = S(pref + "ROWS").n
        em("dve", lambda e: e.tensor_tensor(tl["tmpc"][0:1, :], tl["row_m"][:],
                                            tl["row_m"][:], ALU.mult),
           waits=[(S(pref + "ROWS"), nrows)])
        em("dve", lambda e: e.tensor_tensor(tl["row_v"][:], tl["row_v"][:],
                                            tl["tmpc"][0:1, :], ALU.subtract),
           inc=(S(pref + "VAR"), 1))
        em("act", lambda e: e.activation(tl["row_v"][:], tl["row_v"][:], AF.Sqrt,
                                         bias=ceps[0:1, :]),
           waits=[(S(pref + "VAR"), S(pref + "VAR").n), (S("cP"), 1)],
           inc=(S(pref + "SD"), 1))
        em("dve", lambda e: e.reciprocal(tl["row_v"][:], tl["row_v"][:]),
           waits=[(S(pref + "SD"), S(pref + "SD").n)])
        em("dve", lambda e: e.tensor_tensor(tl["row_m"][:], tl["row_m"][:],
                                            tl["row_v"][:], ALU.mult))
        em("dve", lambda e: e.tensor_scalar(tl["row_m"][:], tl["row_m"][:], -1.0, None,
                                            ALU.mult),
           inc=(S(pref + "RW"), 1))
        em("sp", lambda e: e.dma_start(out=d_row.ap()[0:1, :], in_=tl["row_v"][:]),
           waits=[(S(pref + "RW"), S(pref + "RW").n)], inc=(S(pref + "DW"), 16))
        em("sp", lambda e: e.dma_start(out=d_row.ap()[1:2, :], in_=tl["row_m"][:]),
           inc=(S(pref + "DW"), 16))
        em("sp", lambda e: e.dma_start(out=tl["Rbc"][:],
                                       in_=d_row.ap()[0, :].partition_broadcast(128)),
           waits=[(S(pref + "DW"), S(pref + "DW").n)], inc=(S(pref + "DB"), 16))
        em("sp", lambda e: e.dma_start(out=tl["Nbc"][:],
                                       in_=d_row.ap()[1, :].partition_broadcast(128)),
           inc=(S(pref + "DB"), 16))
        ndb = S(pref + "DB").n
        for c in range(NEC):
            em("dve", lambda e, c=c: e.tensor_tensor(tl["tmpc"][:], src_chunks[c][:],
                                                     tl["Rbc"][:], ALU.mult),
               waits=[(S(pref + "DB"), ndb)] if c == 0 else ())
            em("dve", lambda e, c=c: e.tensor_tensor(dst_chunks[c][:], tl["tmpc"][:],
                                                     tl["Nbc"][:], ALU.add),
               inc=(S(pref + "Y"), 1) if c == NEC - 1 else None)
        return (S(pref + "Y"), S(pref + "Y").n)

    em("dve", lambda e: e.tensor_scalar(bk8[:], sb_b["bk"][:], 0.125, None, ALU.mult),
       waits=[(S("sLp"), LP)], inc=(S("vBK8"), 1))
    tl1 = {"cb": cb, "sqb": sqb, "selA": selA, "selB": selB, "psStat": psStat,
           "row_m": row_m, "row_v": row_v, "tmpc": tmpc, "Rbc": Rbc, "Nbc": Nbc}
    w_y = emit_ln("L1", X, Y, [(S("sLx"), LX)], [], tl1)

    for c in range(NEC):
        em("dve", lambda e, c=c: e.reduce_sum(sc2[:, c:c + 1], Y[c][:], AX.X),
           waits=[w_y] if c == 0 else ())
    em("dve", lambda e: e.tensor_copy(sc2bf[:], sc2[:]), inc=(S("vYM"), 1))
    for c in range(NEC):
        em("pe", lambda e, c=c: e.matmul(
            psPD[:], wd_sb[:, c * H:(c + 1) * H], sc2bf[:, c:c + 1],
            start=(c == 0), stop=(c == NEC - 1)),
            waits=[(S("vYM"), 1), (S("cP"), 1)] if c == 0 else (),
            inc=(S("pPD"), 1) if c == NEC - 1 else None)
    em("dve", lambda e: e.tensor_copy(ld16[:], psPD[:]), waits=[(S("pPD"), 1)],
       inc=(S("vPD"), 1))
    em("act", lambda e: e.activation(ld16[:], ld16[:], AF.Sigmoid, scale=1.0 / T,
                                     bias=bd_sb[:]),
       waits=[(S("vPD"), 1), (S("sLp"), LP), (S("cP"), 1)])
    em("act", lambda e: e.activation(ld16[:], ld16[:], AF.Exp,
                                     scale=float(-8.0 * np.log(2.0)), bias=cb5[:]))
    em("act", lambda e: e.activation(ld16[:], ld16[:], AF.Ln, scale=-1.0, bias=1.0),
       inc=(S("aLD"), 1))
    em("sp", lambda e: e.dma_start(out=d_ld.ap(), in_=ld16[:]), waits=[(S("aLD"), 1)],
       inc=(S("sLD"), 16))
    em("sp", lambda e: e.dma_start(
        out=ldbc[:],
        in_=d_ld.ap().rearrange("h one -> one h")[0, :].partition_broadcast(128)),
        waits=[(S("sLD"), 16)], inc=(S("sLD"), 16))
    for h in range(H):
        em("act", lambda e, h=h: e.activation(lam[:, h * 16:(h + 1) * 16], it_jp[:],
                                              AF.Exp, scale=ldbc[:, h:h + 1]),
           waits=[(S("sLD"), 32), (S("cP"), 1)] if h == 0 else (),
           inc=(S("aLAM"), 1) if h == H - 1 else None)
    em("dve", lambda e: e.tensor_scalar(ldn16[:], ld16[:], -1.0, None, ALU.mult),
       waits=[(S("aLD"), 1)], inc=(S("vLDN"), 1))
    em("act", lambda e: e.activation(c16t[:], it_t[0:H, :], AF.Exp, scale=ldn16[:]),
       waits=[(S("vLDN"), 1), (S("cP"), 1)], inc=(S("aC16"), 1))
    em("sp", lambda e: e.dma_start(out=d_c16.ap(), in_=c16t[:]), waits=[(S("aC16"), 1)],
       inc=(S("sC16"), 16))
    for h in range(H):
        em("dve", lambda e, h=h: e.tensor_scalar(ldI[:, h * 128:(h + 1) * 128], I128[:],
                                                 ldbc[:, h:h + 1], None, ALU.mult),
           waits=[(S("sLD"), 32), (S("cP"), 1)] if h == 0 else (),
           inc=(S("vLDI"), 1) if h == H - 1 else None)

    ctxL.close()
    ctxP0.close()

    ctxB = ExitStack()
    B = ctxB.enter_context
    OG = [B(SBL(f"OG{c}", [128, T], BF16)) for c in range(NEC)]
    Qf = B(SBL("Qf", [64, 2 * T], BF16))
    Kf = B(SBL("Kf", [64, 2 * T], BF16))
    Vp = B(SBL("Vp", [128, T], BF16))
    vfut = B(SBL("vfut", [128, 16 * 65], BF16))
    vpast = B(SBL("vpast", [128, 16 * 65], BF16))
    wsl = B(SBL("wsl", [128, E], F32))
    wbf = B(SBL("wbf", [128, E], BF16))
    t64a = B(SBL("t64a", [64, T], BF16))
    t64b = B(SBL("t64b", [64, T], BF16))
    scoresb = [B(SBL(f"scoresb{k}", [128, 1024], BF16)) for k in range(2)]
    pfsb = B(SBL("pfsb", [65, 1024], F32))
    ppsb = B(SBL("ppsb", [65, 1024], F32))
    cbc = B(SBL("cbc", [65, 1024], F32))
    rbc = B(SBL("rbc", [128, 1024], BF16))
    mneg = B(SBL("mneg", [128, 2], F32))
    nqk = B(SBL("nqk", [1, 8], F32))
    gnst = B(SBL("gnst", [128, 2], F32))
    gnstb = B(SBL("gnstb", [128, 2], BF16))
    gnfold = B(SBL("gnfold", [128, 65], BF16))
    gn2 = B(SBL("gn2", [65, 4], F32))
    gsc = B(SBL("gsc", [128, 2], F32))
    gate_t = B(SBL("gate_t", [128, T], BF16))
    psQ = B(PSL("psQ", [128, T], F32))
    psPV = B(PSL("psPV", [65, T], F32))

    P0W = [w_y, (S("aC16"), 1), (S("aLAM"), 1), (S("vLDI"), 1), (S("vPD"), 1),
           (S("sC16"), 16)]
    em("pool", lambda e: e.memset(gnfold[:], 0.0), waits=P0W)
    em("pool", lambda e: e.memset(gnfold[0:64, 0:1], 1.0))
    em("pool", lambda e: e.memset(gnfold[64:128, 64:65], 1.0), inc=(S("cGNF"), 1))

    trk = {
        "wsl_free": P0W, "wbf_free": P0W, "psQ_free": P0W, "psPV_free": P0W,
        "QK_free": P0W, "t64_free": P0W, "Vp_free": P0W, "vt_free": P0W,
        "pfpp_free": P0W, "cbc_free": P0W, "rbc_free": P0W, "mneg_free": P0W,
        "qkbuf_free": [P0W, P0W], "sbuf_free": [P0W, P0W],
    }

    def w(*items):
        out = []
        for t in items:
            if t is None:
                continue
            if isinstance(t, list):
                out.extend(x for x in t if x is not None)
            else:
                out.append(t)
        return out

    def stream_weight(dram_col_slice):
        em("pool", lambda e, d=dram_col_slice: e.dma_start(
            out=wbf[:].rearrange("p (c o) -> p c o", o=128),
            in_=d.rearrange("(c p) o -> p c o", p=128)),
            waits=w(trk["wbf_free"]), inc=(S("cWS"), 16), drain=False, is_dma=True)
        return (S("cWS"), S("cWS").n)

    def proj_fm(g_w, rhs_chunks, psem, extra_first=()):
        for ic in range(4):
            for c in range(NEC):
                em("pe", lambda e, c=c, ic=ic: e.matmul(
                    psQ[:, ic * 512:(ic + 1) * 512],
                    wbf[:, c * 128:(c + 1) * 128],
                    rhs_chunks[c][:, ic * 512:(ic + 1) * 512],
                    start=(c == 0), stop=(c == NEC - 1)),
                    waits=(w(g_w, trk["psQ_free"]) + list(extra_first)
                           if (c == 0 and ic == 0) else ()),
                    inc=(S(psem), 1) if (ic == 3 and c == NEC - 1) else None)
        trk["wbf_free"] = (S(psem), S(psem).n)
        return (S(psem), S(psem).n)

    for hp in range(8):
        for which, (wnm, bnm, dst) in enumerate((("wq", "bq", Qf), ("wk", "bk", Kf))):
            g_w = stream_weight(wT[wnm][:, hp * 128:(hp + 1) * 128])
            pQ = proj_fm(g_w, Y, "pQK", extra_first=[w_y] + w(trk["QK_free"]))
            if KDBG and hp == 0 and which == 0:
                em("pool", lambda e: e.dma_start(out=dbg["wbfq0"], in_=wbf[:]),
                   waits=[g_w], inc=(S("sDBP"), 16), drain=False, is_dma=True)
                trk["wbf_free"] = w(trk["wbf_free"], (S("sDBP"), 16))
            bsrc = sb_b[bnm] if which == 0 else bk8
            ksc = 1.0 if which == 0 else 0.125
            em("act", lambda e, d=dst, b=bsrc, sc=ksc, hp=hp: e.activation(
                d[:, 0:T], psQ[0:64, :], AF.Identity, scale=sc, bias=b[0:64, hp:hp + 1]),
                waits=[pQ, (S("sLp"), LP), (S("vBK8"), 1)])
            em("act", lambda e, d=dst, b=bsrc, sc=ksc, hp=hp: e.activation(
                d[:, T:2 * T], psQ[64:128, :], AF.Identity, scale=sc,
                bias=b[64:128, hp:hp + 1]),
                inc=(S("aQK"), 1))
            aQ = (S("aQK"), S("aQK").n)
            trk["psQ_free"] = aQ
            if KDBG and hp == 0 and which == 0:
                em("pool", lambda e: e.dma_start(out=dbg["qpre0"], in_=Qf[:]),
                   waits=[aQ], inc=(S("sDBP2"), 16), drain=False, is_dma=True)
            last_comb = None
            pRs = {}
            for s in range(2):
                rps = psQ if s == 0 else psPV
                for ic in range(4):
                    em("pe", lambda e, s=s, ic=ic, d=dst, rps=rps: e.matmul(
                        rps[0:64, ic * 512:(ic + 1) * 512],
                        Pi64[:], d[:, s * T + ic * 512: s * T + (ic + 1) * 512],
                        start=True, stop=True),
                        waits=(w(aQ, (S("vPI"), 1),
                                 trk["psPV_free"] if s == 1 else None)
                               if ic == 0 else ()),
                        inc=(S("pROT"), 1) if ic == 3 else None)
                pRs[s] = (S("pROT"), S("pROT").n)
            for s in range(2):
                rps = psQ if s == 0 else psPV
                em("act", lambda e, rps=rps: e.activation(t64b[:], rps[0:64, :],
                                                          AF.Copy),
                   waits=w(pRs[s], trk["t64_free"]), inc=(S("aROT"), 1))
                aRot = (S("aROT"), S("aROT").n)
                if s == 0:
                    trk["psQ_free"] = w(aQ, aRot)
                else:
                    trk["psPV_free"] = aRot
                em("dve", lambda e, s=s, d=dst: e.tensor_tensor(
                    t64a[:], d[:, s * T:(s + 1) * T], cos64[:], ALU.mult),
                    waits=w((S("vTRIG"), 2), aRot,
                            (S("sDBP2"), 16) if (KDBG and hp == 0 and which == 0
                                                 and s == 0) else None))
                em("dve", lambda e: e.tensor_tensor(t64b[:], t64b[:], sinS[:], ALU.mult))
                em("dve", lambda e, s=s, d=dst: e.tensor_tensor(
                    d[:, s * T:(s + 1) * T], t64a[:], t64b[:], ALU.add),
                    inc=(S("vCOMB"), 1))
                last_comb = (S("vCOMB"), S("vCOMB").n)
                trk["t64_free"] = last_comb
        v_qk = (S("vCOMB"), S("vCOMB").n)
        if KDBG and hp == 0:
            em("pool", lambda e: e.dma_start(out=dbg["qf0"], in_=Qf[:]),
               waits=[v_qk], inc=(S("sDBQ"), 16), drain=False, is_dma=True)
            em("pool", lambda e: e.dma_start(out=dbg["kf0"], in_=Kf[:]),
               inc=(S("sDBQ"), 16), drain=False, is_dma=True)

        g_v = stream_weight(wT["wv"][:, hp * 128:(hp + 1) * 128])
        psV_free = [trk["psQ_free"]] * 4
        for tb in range(NTB):
            bk = tb % 4
            for c in range(NEC):
                em("pe", lambda e, tb=tb, c=c, bk=bk: e.matmul(
                    psQ[:, bk * 512: bk * 512 + 128],
                    Y[c][:, tb * 128:(tb + 1) * 128],
                    wbf[:, c * 128:(c + 1) * 128],
                    start=(c == 0), stop=False),
                    waits=(w(g_v, psV_free[bk], trk["Vp_free"]) if c == 0 else ()))
            em("pe", lambda e, bk=bk, hp=hp: e.matmul(
                psQ[:, bk * 512: bk * 512 + 128],
                ones_row[0:1, 0:128],
                bv_bf[0:1, hp * 128:(hp + 1) * 128],
                start=False, stop=True),
                waits=[(S("vBV"), 1), (S("vONES"), 1)] if (hp == 0 and tb == 0) else (),
                inc=(S("pV"), 1))
            pV = (S("pV"), S("pV").n)
            em("act", lambda e, tb=tb, bk=bk: e.activation(
                Vp[:, tb * 128:(tb + 1) * 128], psQ[:, bk * 512: bk * 512 + 128], AF.Copy),
                waits=[pV], inc=(S("aV"), 1))
            psV_free[bk] = (S("aV"), S("aV").n)
        a_v = (S("aV"), S("aV").n)
        if KDBG and hp == 0:
            em("pool", lambda e: e.dma_start(out=dbg["vp0"], in_=Vp[:]),
               waits=[a_v], inc=(S("sDBQ"), 16), drain=False, is_dma=True)
        trk["psQ_free"] = a_v
        trk["wbf_free"] = (S("pV"), S("pV").n)

        for w2, src in enumerate((Qf, Kf)):
            em("act", lambda e, s=src: e.activation(t64a[:], s[:, 0:T], AF.Square),
               waits=(w(v_qk, trk["t64_free"]) if w2 == 0
                      else [(S("pNRM"), S("pNRM").n)]))
            em("act", lambda e, s=src: e.activation(t64b[:], s[:, T:2 * T], AF.Square),
               inc=(S("aNSQ"), 1))
            a_sq = (S("aNSQ"), S("aNSQ").n)
            for half, tt in ((0, t64a), (1, t64b)):
                for ic in range(4):
                    em("pe", lambda e, tt=tt, ic=ic: e.matmul(
                        psQ[0:1, ic * 512:(ic + 1) * 512],
                        ones_col[0:64, :], tt[:, ic * 512:(ic + 1) * 512],
                        start=True, stop=True),
                        waits=w(a_sq, trk["psQ_free"]) if ic == 0 else (),
                        inc=(S("pNRM"), 1) if ic == 3 else None)
                pN = (S("pNRM"), S("pNRM").n)
                em("dve", lambda e: e.tensor_copy(pfsb[0:1, :], psQ[0:1, 0:1024]),
                   waits=w(pN, trk["pfpp_free"]))
                em("dve", lambda e: e.tensor_copy(ppsb[0:1, :], psQ[0:1, 1024:2048]))
                em("dve", lambda e, k=w2 * 2 + half: e.reduce_max(
                    nqk[0:1, k:k + 1], pfsb[0:1, :], AX.X))
                em("dve", lambda e, k=w2 * 2 + half: e.reduce_max(
                    nqk[0:1, 4 + k:5 + k], ppsb[0:1, :], AX.X))
                em("dve", lambda e, k=w2 * 2 + half: e.tensor_tensor(
                    nqk[0:1, k:k + 1], nqk[0:1, k:k + 1], nqk[0:1, 4 + k:5 + k], ALU.max),
                    inc=(S("vNRM"), 1))
                trk["psQ_free"] = (S("vNRM"), S("vNRM").n)
                trk["pfpp_free"] = (S("vNRM"), S("vNRM").n)
        em("dve", lambda e: e.tensor_tensor(nqk[0:1, 4:6], nqk[0:1, 0:2], nqk[0:1, 2:4],
                                            ALU.mult), inc=(S("vNQ"), 1))
        em("act", lambda e: e.activation(nqk[0:1, 4:6], nqk[0:1, 4:6], AF.Sqrt),
           waits=[(S("vNQ"), S("vNQ").n)], inc=(S("aNQ"), 1))
        em("dve", lambda e: e.tensor_scalar(nqk[0:1, 4:6], nqk[0:1, 4:6], -1.0,
                                            None, ALU.mult),
           waits=[(S("aNQ"), S("aNQ").n)], inc=(S("vNM"), 1))
        em("pool", lambda e, hp=hp: e.dma_start(out=d_sc.ap()[0:1, 2 * hp:2 * hp + 2],
                                               in_=nqk[0:1, 4:6]),
           waits=[(S("vNM"), S("vNM").n)], inc=(S("sNQ"), 16),
           drain=False, is_dma=True)
        em("pool", lambda e, hp=hp: e.dma_start(
            out=mneg[:], in_=d_sc.ap()[0, 2 * hp:2 * hp + 2].partition_broadcast(128)),
           waits=w((S("sNQ"), S("sNQ").n), trk["mneg_free"]), inc=(S("sNQ"), 16),
           drain=False, is_dma=True)
        ss_mneg = (S("sNQ"), S("sNQ").n)

        vfut3 = vfut[:].rearrange("p (j d) -> p j d", d=65)
        vpast3 = vpast[:].rearrange("p (j d) -> p j d", d=65)
        Vp3 = Vp[:].rearrange("p (j d) -> p j d", d=128)
        for s in range(2):
            h = 2 * hp + s
            lam3 = lam[:, h * 16:(h + 1) * 16].rearrange("p j -> p j ()")
            em("dve", lambda e, s=s, lam3=lam3: e.tensor_tensor(
                vfut3[:, :, 0:64], Vp3[:, :, s * 64:(s + 1) * 64],
                lam3.broadcast_to((128, 16, 64)), ALU.mult),
                waits=w(a_v, (S("aLAM"), 1), trk["vt_free"]))
            em("dve", lambda e, lam3=lam3: e.tensor_copy(vfut3[:, :, 64:65], lam3))
            em("dve", lambda e, s=s: e.tensor_copy(
                vpast3[:, :, 0:64], Vp3[:, :, s * 64:(s + 1) * 64]))
            em("dve", lambda e: e.memset(vpast3[:, :, 64:65], 1.0), inc=(S("vVT"), 1))
            v_vt = (S("vVT"), S("vVT").n)
            trk["Vp_free"] = (w(v_vt, (S("sDBQ"), 48))
                              if (KDBG and hp == 0) else v_vt)

            for ic2 in range(2):
                def _wf(jb):
                    return min(max(128 * jb - 1024 * ic2, 0), 1024)
                segs_by_jb = {}
                bank_events = {0: [], 1: [], 2: [], 3: []}
                for jb in range(16):
                    wf = _wf(jb)
                    wfp = _wf(jb - 1) if jb > 0 else 0
                    segs = []
                    cuts = sorted({0, wf, min(wfp, wf), 512, 1024} & set(range(0, wf + 1))
                                  | {0, min(wfp, wf), wf}
                                  | {c for c in (512, 1024) if c < wf})
                    cuts = sorted(set([0, min(wfp, wf), wf] +
                                      [c for c in (512, 1024) if 0 < c < wf]))
                    for a2, b2 in zip(cuts, cuts[1:]):
                        if b2 > a2:
                            segs.append(("f", 1024 + a2, 1024 + b2))
                    a2 = wf
                    while a2 < 1024:
                        b2 = min((a2 // 512 + 1) * 512, 1024)
                        segs.append(("p", a2, b2))
                        a2 = b2
                    segs_by_jb[jb] = segs
                    for k, (kind, ca, cb2) in enumerate(segs):
                        for bk in range(ca // 512, (cb2 - 1) // 512 + 1):
                            bank_events[bk].append((jb, k))
                first_ev = {bk: (ev[0] if ev else None) for bk, ev in bank_events.items()}
                last_ev = {bk: (ev[-1] if ev else None) for bk, ev in bank_events.items()}
                wfmax = _wf(15)
                if wfmax < 1024:
                    a2 = wfmax
                    while a2 < 1024:
                        b2 = min((a2 // 512 + 1) * 512, 1024)
                        em("pe", lambda e, a=a2, b=b2: e.matmul(
                            psPV[:, 1024 + a:1024 + b], z65[:], ones_row[0:1, 0:(b - a)],
                            start=True, stop=True),
                            waits=w(trk["psPV_free"]))
                        a2 = b2
                exp_done = {}
                pv_done = {}
                evac_done = {}

                def emit_qk(jb):
                    off = 128 * jb - 1024 * ic2
                    wf = min(max(off, 0), 1024)
                    buf = jb % 2
                    bs_a, bs_b = wf, min(off + 128, 1024)
                    has_bias = bs_b > bs_a
                    if jb >= 2:
                        wqk = [evac_done[jb - 2]]
                    else:
                        wqk = w(trk["qkbuf_free"][buf], trk["psQ_free"],
                                trk["QK_free"])
                    segh = {0: has_bias and min(bs_b, 512) > max(bs_a, 0),
                            1: has_bias and min(bs_b, 1024) > max(bs_a, 512)}
                    for half in range(2):
                        ha, hb = half * 512, (half + 1) * 512
                        seg_a, seg_b = max(bs_a, ha), min(bs_b, hb)
                        has_seg = segh[half]
                        is_last_qk = (half == 1 and not has_seg)
                        em("pe", lambda e, jb=jb, half=half, buf=buf, s=s, ic2=ic2,
                           st=not has_seg: e.matmul(
                            psQ[:, buf * 1024 + half * 512: buf * 1024 + (half + 1) * 512],
                            Kf[:, s * T + jb * 128: s * T + (jb + 1) * 128],
                            Qf[:, s * T + ic2 * 1024 + half * 512:
                               s * T + ic2 * 1024 + (half + 1) * 512],
                            start=True, stop=st),
                            waits=wqk if half == 0 else (),
                            inc=(S(f"qk{buf}"), 1) if is_last_qk else None)
                        wqk[:] = []
                        if has_seg:
                            u0 = seg_a - off
                            em("pe", lambda e, buf=buf, h=h, a=seg_a, b=seg_b, u0=u0:
                               e.matmul(
                                psQ[:, buf * 1024 + a: buf * 1024 + b],
                                ldI[:, h * 128:(h + 1) * 128],
                                Drelu0[:, u0:u0 + (b - a)],
                                start=False, stop=True),
                                waits=[(S("vLDI"), 1), (S("cDRELU"), 1)]
                                if (hp == 0 and s == 0 and ic2 == 0) else (),
                                inc=(S(f"qk{buf}"), 1) if half == 1 else None)
                    pQK = (S(f"qk{buf}"), S(f"qk{buf}").n)
                    wev = w(pQK, trk["sbuf_free"][buf])
                    em("act", lambda e, buf=buf: e.activation(
                        scoresb[buf][:], psQ[:, buf * 1024: buf * 1024 + 1024],
                        AF.Copy),
                        waits=wev, inc=(S(f"ev{buf}"), 1))
                    evac_done[jb] = (S(f"ev{buf}"), S(f"ev{buf}").n)
                    trk["qkbuf_free"][buf] = evac_done[jb]
                    em("act", lambda e, buf=buf, s=s: e.activation(
                        scoresb[buf][:], scoresb[buf][:], AF.Exp, bias=mneg[:, s:s + 1]),
                        waits=w(evac_done[jb],
                                ss_mneg if (s == 0 and ic2 == 0 and jb == 0)
                                else None),
                        inc=(S(f"ex{buf}"), 1))
                    exp_done[jb] = (S(f"ex{buf}"), S(f"ex{buf}").n)

                def emit_pv(jb):
                    buf = jb % 2
                    first_mm = True
                    for k, (kind, ca, cb2) in enumerate(segs_by_jb[jb]):
                        start = first_ev[ca // 512] == (jb, k)
                        stop = last_ev[ca // 512] == (jb, k)
                        wpv = []
                        if first_mm:
                            wpv = [exp_done[jb]]
                            if jb == 0:
                                wpv += w(trk["psPV_free"], v_vt)
                        vt = vfut3 if kind == "f" else vpast3
                        ea = ca - 1024 if kind == "f" else ca
                        eb = cb2 - 1024 if kind == "f" else cb2
                        em("pe", lambda e, jb=jb, buf=buf, ca=ca, cb2=cb2, ea=ea, eb=eb,
                           vt=vt, start=start, stop=stop: e.matmul(
                            psPV[:, ca:cb2], vt[:, jb, :], scoresb[buf][:, ea:eb],
                            start=start, stop=stop), waits=wpv,
                            inc=(S(f"pv{buf}"), 1) if k == len(segs_by_jb[jb]) - 1 else None)
                        first_mm = False
                    pv_done[jb] = (S(f"pv{buf}"), S(f"pv{buf}").n)
                    trk["sbuf_free"][buf] = pv_done[jb]

                for jb in range(17):
                    if jb < 16:
                        emit_qk(jb)
                    if jb >= 1:
                        emit_pv(jb - 1)
                em("act", lambda e: e.activation(ppsb[:], psPV[:, 0:1024], AF.Copy),
                   waits=w(pv_done[15], pv_done[14], trk["pfpp_free"]))
                em("act", lambda e: e.activation(pfsb[:], psPV[:, 1024:2048], AF.Copy),
                   inc=(S("aPP"), 1))
                aPV = (S("aPP"), S("aPP").n)
                trk["psPV_free"] = aPV
                em("pool", lambda e, h=h, ic2=ic2: e.dma_start(
                    out=cbc[:],
                    in_=d_c16.ap()[h, ic2 * 1024:(ic2 + 1) * 1024].partition_broadcast(65)),
                    waits=w((S("sC16"), 16), trk["cbc_free"]), inc=(S("sCB"), 16),
                    drain=False, is_dma=True)
                ss_cbc = (S("sCB"), S("sCB").n)
                em("dve", lambda e: e.tensor_tensor(pfsb[64:65, :], cbc[64:65, :],
                                                    pfsb[64:65, :], ALU.mult),
                   waits=w(aPV, ss_cbc))
                em("dve", lambda e: e.tensor_tensor(ppsb[64:65, :], pfsb[64:65, :],
                                                    ppsb[64:65, :], ALU.add))
                em("dve", lambda e: e.reciprocal(pfsb[64:65, :], ppsb[64:65, :]))
                em("dve", lambda e: e.tensor_tensor(cbc[0:64, :], cbc[0:64, :],
                                                    pfsb[0:64, :], ALU.mult))
                em("dve", lambda e, hp=hp, s=s, ic2=ic2: e.tensor_tensor(
                    OG[hp][s * 64:(s + 1) * 64, ic2 * 1024:(ic2 + 1) * 1024],
                    cbc[0:64, :], ppsb[0:64, :], ALU.add), inc=(S("vEPI"), 1))
                vDen = (S("vEPI"), S("vEPI").n)
                trk["cbc_free"] = vDen
                em("pool", lambda e, h=h, ic2=ic2: e.dma_start(
                    out=d_r16.ap()[h:h + 1, ic2 * 1024:(ic2 + 1) * 1024],
                    in_=pfsb[64:65, :]), waits=[vDen], inc=(S("sR16"), 16),
                    drain=False, is_dma=True)
                ss_r = (S("sR16"), S("sR16").n)
                trk["pfpp_free"] = w(vDen, ss_r)
                em("pool", lambda e, h=h, ic2=ic2: e.dma_start(
                    out=rbc[:],
                    in_=d_r16.ap()[h, ic2 * 1024:(ic2 + 1) * 1024].partition_broadcast(128)),
                    waits=w(ss_r, trk["rbc_free"]), inc=(S("gRBC"), 16), drain=False,
                    is_dma=True)
                ss_rbc2 = (S("gRBC"), S("gRBC").n)
                em("dve", lambda e, hp=hp, s=s, ic2=ic2: e.tensor_tensor(
                    OG[hp][s * 64:(s + 1) * 64, ic2 * 1024:(ic2 + 1) * 1024],
                    OG[hp][s * 64:(s + 1) * 64, ic2 * 1024:(ic2 + 1) * 1024],
                    rbc[s * 64:(s + 1) * 64, :], ALU.mult),
                    waits=[ss_rbc2], inc=(S("vOG"), 1))
                trk["rbc_free"] = (S("vOG"), S("vOG").n)
            trk["vt_free"] = pv_done[15]
        trk["QK_free"] = (w(pv_done[15], (S("sDBQ"), 48))
                          if (KDBG and hp == 0) else pv_done[15])
        trk["mneg_free"] = exp_done[15]
        trk["psQ_free"] = [evac_done[14], evac_done[15]]

    og_done = (S("vOG"), S("vOG").n)
    if KDBG:
        em("pool", lambda e: e.dma_start(out=dbg["og0p"], in_=OG[0][:]),
           waits=[og_done], inc=(S("sDBG"), 16), drain=False, is_dma=True)
    for hp in range(8):
        g_g = stream_weight(wT["wg"][:, hp * 128:(hp + 1) * 128])
        pG = proj_fm(g_g, Y, "pG")
        em("act", lambda e, hp=hp: e.activation(gate_t[:], psQ[:], AF.Silu,
                                                bias=sb_b["bg"][:, hp:hp + 1]),
           waits=w(pG, trk["gate_free"] if "gate_free" in trk else None),
           inc=(S("aG"), 1))
        a_g = (S("aG"), S("aG").n)
        trk["psQ_free"] = a_g
        em("dve", lambda e, hp=hp: e.reduce_sum(gnst[:, 0:1], OG[hp][:], AX.X),
           waits=w(og_done if hp == 0 else None))
        em("act", lambda e, hp=hp: e.activation(t64a[:], OG[hp][0:64, :], AF.Square),
           waits=w(og_done if hp == 0 else None, trk["t64_free"]))
        em("act", lambda e, hp=hp: e.activation(t64b[:], OG[hp][64:128, :], AF.Square),
           inc=(S("aGSQ"), 1))
        aGN = (S("aGSQ"), S("aGSQ").n)
        em("dve", lambda e: e.reduce_sum(gnst[0:64, 1:2], t64a[:], AX.X), waits=[aGN])
        em("dve", lambda e: e.reduce_sum(gnst[64:128, 1:2], t64b[:], AX.X))
        em("dve", lambda e: e.tensor_copy(gnstb[:], gnst[:]), inc=(S("vGST"), 1))
        vGN = (S("vGST"), S("vGST").n)
        trk["t64_free"] = vGN
        em("pe", lambda e: e.matmul(psPV[:, 0:2], gnfold[:], gnstb[:],
                                    start=True, stop=True),
           waits=w(vGN, (S("cGNF"), 1), trk["psPV_free"]), inc=(S("pGN"), 1))
        em("dve", lambda e: e.tensor_copy(gn2[:, 0:2], psPV[:, 0:2]),
           waits=[(S("pGN"), S("pGN").n)], inc=(S("vGN2"), 1))
        trk["psPV_free"] = (S("vGN2"), S("vGN2").n)
        NL = float(T * HD)
        em("dve", lambda e: e.tensor_scalar(gn2[:, 0:2], gn2[:, 0:2], 1.0 / NL, None,
                                            ALU.mult))
        em("dve", lambda e: e.tensor_tensor(gn2[:, 2:3], gn2[:, 0:1], gn2[:, 0:1],
                                            ALU.mult))
        em("dve", lambda e: e.tensor_tensor(gn2[:, 3:4], gn2[:, 1:2], gn2[:, 2:3],
                                            ALU.subtract), inc=(S("vGVAR"), 1))
        em("act", lambda e: e.activation(gn2[:, 3:4], gn2[:, 3:4], AF.Sqrt, bias=ceps[:]),
           waits=[(S("vGVAR"), S("vGVAR").n), (S("cP"), 1)], inc=(S("aGSD"), 1))
        em("dve", lambda e: e.reciprocal(gn2[:, 3:4], gn2[:, 3:4]),
           waits=[(S("aGSD"), S("aGSD").n)])
        em("dve", lambda e: e.tensor_copy(gn2[:, 2:3], gn2[:, 0:1]), inc=(S("vGR"), 1))
        vRstd = (S("vGR"), S("vGR").n)
        em("pool", lambda e: e.dma_start(out=d_sc.ap()[0:1, 4:6], in_=gn2[0:1, 2:4]),
           waits=[vRstd], inc=(S("sGN"), 16), drain=False, is_dma=True)
        em("pool", lambda e: e.dma_start(out=d_sc.ap()[0:1, 6:8], in_=gn2[64:65, 2:4]),
           inc=(S("sGN"), 16), drain=False, is_dma=True)
        em("pool", lambda e: e.dma_start(
            out=gsc[0:64, 0:2], in_=d_sc.ap()[0, 4:6].partition_broadcast(64)),
           waits=[(S("sGN"), S("sGN").n)], inc=(S("sGN2"), 16),
           drain=False, is_dma=True)
        em("pool", lambda e: e.dma_start(
            out=gsc[64:128, 0:2], in_=d_sc.ap()[0, 6:8].partition_broadcast(64)),
           inc=(S("sGN2"), 16), drain=False, is_dma=True)
        em("dve", lambda e, hp=hp: e.tensor_scalar(gsc[:, 1:2],
                                                   sb_b["gamma"][:, hp:hp + 1],
                                                   gsc[:, 1:2], None, ALU.mult),
           waits=[(S("sGN2"), S("sGN2").n)])
        em("dve", lambda e: e.tensor_tensor(gsc[:, 0:1], gsc[:, 0:1], gsc[:, 1:2],
                                            ALU.mult))
        em("dve", lambda e, hp=hp: e.scalar_tensor_tensor(
            gsc[:, 0:1], gsc[:, 0:1], -1.0, sb_b["beta"][:, hp:hp + 1], ALU.mult, ALU.add),
            inc=(S("vGSC"), 1))
        em("dve", lambda e, hp=hp: e.tensor_scalar(OG[hp][:], OG[hp][:], gsc[:, 1:2],
                                                   gsc[:, 0:1], ALU.mult, ALU.add),
           waits=[(S("vGSC"), S("vGSC").n)] +
                 ([(S("sDBG"), 16)] if (KDBG and hp == 0) else []))
        em("dve", lambda e, hp=hp: e.tensor_tensor(OG[hp][:], OG[hp][:], gate_t[:],
                                                   ALU.mult), waits=[a_g],
           inc=(S("vAPP"), 1))
        trk["gate_free"] = (S("vAPP"), S("vAPP").n)
    v_att = (S("vAPP"), S("vAPP").n)

    vO = [None, None, None, None]
    for oc in range(NEC):
        g_o = stream_weight(wT["wo"][:, oc * 128:(oc + 1) * 128])
        for ic in range(4):
            for c in range(NEC):
                em("pe", lambda e, c=c, ic=ic: e.matmul(
                    psQ[:, ic * 512:(ic + 1) * 512],
                    wbf[:, c * 128:(c + 1) * 128],
                    OG[c][:, ic * 512:(ic + 1) * 512],
                    start=(c == 0), stop=(c == NEC - 1)),
                    waits=(w(g_o, v_att if oc == 0 else None,
                             trk["psQ_free"] if oc == 0 and ic == 0 else None,
                             vO[ic] if oc > 0 else None) if c == 0 else ()),
                    inc=(S(f"pO{ic}"), 1) if c == NEC - 1 else None)
            pO = (S(f"pO{ic}"), S(f"pO{ic}").n)
            em("dve", lambda e, oc=oc, ic=ic: e.scalar_tensor_tensor(
                X[oc][:, ic * 512:(ic + 1) * 512], psQ[:, ic * 512:(ic + 1) * 512],
                sb_b["bo"][0:128, oc:oc + 1], X[oc][:, ic * 512:(ic + 1) * 512],
                ALU.add, ALU.add), waits=[pO], inc=(S(f"vO{ic}"), 1))
            vO[ic] = (S(f"vO{ic}"), S(f"vO{ic}").n)
        trk["wbf_free"] = (S("pO3"), S("pO3").n)

    if KDBG:
        ddone = w(v_att, (S("vOG"), S("vOG").n))
        for nm, tile in (("y0", Y[0][:]), ("ld", ld16[:]), ("qf", Qf[:]), ("kf", Kf[:]),
                         ("vp", Vp[:]), ("og0", OG[0][:]), ("mneg", mneg[:]),
                         ("cos", cos64[:]), ("sin", sinS[:]), ("gate", gate_t[:]),
                         ("pf", pfsb[:]), ("pp", ppsb[:])) + tuple(
                             (f"yc{c}", Y[c][:]) for c in range(8)):
            em("pool", lambda e, nm=nm, tile=tile: e.dma_start(out=dbg[nm], in_=tile),
               waits=ddone, inc=(S("sDBG"), 16), drain=False, is_dma=True)
            ddone = ()

    if KSTOP == "attn":
        for c in range(NEC):
            em("sp", lambda e, c=c: e.dma_start(out=yout[c * 128:(c + 1) * 128, :],
                                                in_=X[c][:]),
               waits=[vO[0], vO[1], vO[2], vO[3]] if c == 0 else (), inc=(S("sO"), 16))
        em("sp", lambda e: e.nop(), waits=[(S("sO"), S("sO").n)] +
           ([(S("sDBG"), S("sDBG").n), (S("sDBQ"), S("sDBQ").n),
             (S("sDBP"), S("sDBP").n), (S("sDBP2"), S("sDBP2").n)] if KDBG else []),
           drain=False)
    else:
        ctxB.close()
        ctxFP = ExitStack()
        FP = ctxFP.enter_context
        psF1 = FP(PSL("psF1", [128, T], F32))
        psF2 = FP(PSL("psF2", [128, T], F32))
        ctxF2 = ExitStack()
        F2 = ctxF2.enter_context
        iA = F2(SBR("iA", [128, T], I32))
        iA2 = F2(SBR("iA2", [128, T], I32))
        iA3 = F2(SBR("iA3", [128, T], I32))
        scb0 = F2(SBR("scb0", [128, 2], F32))
        scb1 = F2(SBR("scb1", [128, 4], F32))
        scb2 = F2(SBR("scb2", [128, 2], F32))
        scb3 = F2(SBR("scb3", [128, 2], F32))
        mx0 = F2(SBR("mx0", [128, 1], F32))
        mxu = F2(SBR("mxu", [128, 1], F32))
        mxg = F2(SBR("mxg", [128, 1], F32))
        mx2 = F2(SBR("mx2", [128, 1], F32))
        mx3 = F2(SBR("mx3", [128, 1], F32))
        tmpr = F2(SBR("tmpr", [128, 1], F32))
        mrow = F2(SBR("mrow", [1, 128], F32))
        sscl = F2(SBR("sscl", [1, 20], F32))
        XW = w(vO[0], vO[1], vO[2], vO[3],
               (S("sDBG"), S("sDBG").n) if KDBG else None,
               (S("sDBQ"), S("sDBQ").n) if KDBG else None)

        for c in range(NEC):
            em("act", lambda e, c=c: e.dma_start(
                out=d_x1.ap()[c * 128:(c + 1) * 128, :], in_=X[c][:]),
                waits=XW if c == 0 else (), inc=(S("sLx"), 16),
                drain=False, is_dma=True)
        sX1 = (S("sLx"), S("sLx").n)
        for m in (mx0, mxu, mxg, mx2, mx3):
            em("pool", lambda e, m=m: e.memset(m[:], 0.0),
               waits=XW if m is mx0 else ())
        em("pool", lambda e: e.nop(), inc=(S("cPM"), 1), drain=True)
        mZ = (S("cPM"), S("cPM").n)

        ctxDQ = ExitStack()
        DQ = ctxDQ.enter_context
        wtF = DQ(SBR("wtF", [128, FF], F32))
        iW = DQ(SBR("iW", [128, FF], I32))
        wqF = DQ(SBR("wqF", [128, FF], F32))
        wqT = DQ(SBR("wqTs", [128, FF], BF16))
        s4t = DQ(SBR("s4t", [128, 96], F32))
        rin4 = DQ(SBR("rin4", [128, 96], F32))
        sI = DQ(SBR("sI", [128, 128], F32))
        em("pool", lambda e: e.memset(sI[:], 1.0), waits=XW)
        em("pool", lambda e: e.affine_select(sI[:], sI[:], pattern=[[1, 128]],
                                             compare_op=ALU.is_equal, fill=0.0,
                                             base=0, channel_multiplier=-1),
           inc=(S("cIOTA"), 1))
        nID = S("cIOTA").n
        prev_wq = None
        prev_tr = None

        def dequant_chunk(src2d, ncols, tcol_of_j, first):
            nonlocal prev_wq, prev_tr
            global_es = {}
            nj = ncols // 128
            nb = ncols // 32
            em("sp", lambda e, src2d=src2d, ncols=ncols: e.dma_start(
                out=wtF[:, 0:ncols], in_=src2d),
               waits=w(XW if first else None, prev_wq, prev_tr),
               inc=(S("sLw"), 16))
            em("dve", lambda e, ncols=ncols, nb=nb: e.tensor_reduce(
                s4t[:, 0:nb].rearrange("p (b one) -> p b one", one=1),
                wtF[:, 0:ncols].rearrange("p (b k) -> p b k", k=32),
                AX.X, ALU.max, apply_absolute_value=True),
               waits=w((S("sLw"), S("sLw").n), prev_tr), inc=(S("vNRM"), 1))
            em("dve", lambda e, nb=nb: e.tensor_scalar(
                s4t[:, 0:nb], s4t[:, 0:nb], 1.0 / 7.0, None, ALU.mult))
            em("dve", lambda e, nb=nb: e.tensor_scalar(
                s4t[:, 0:nb], s4t[:, 0:nb], 1e-12, None, ALU.max))
            em("dve", lambda e, nb=nb: e.reciprocal(rin4[:, 0:nb], s4t[:, 0:nb]))
            em("dve", lambda e, ncols=ncols, nb=nb: e.tensor_tensor(
                iW[:, 0:ncols].rearrange("p (b k) -> p b k", k=32),
                wtF[:, 0:ncols].rearrange("p (b k) -> p b k", k=32),
                rin4[:, 0:nb].rearrange("p (b one) -> p b one", one=1)
                .broadcast_to((128, nb, 32)),
                ALU.mult))
            em("dve", lambda e, ncols=ncols, nb=nb: e.tensor_tensor(
                wqF[:, 0:ncols].rearrange("p (b k) -> p b k", k=32),
                iW[:, 0:ncols].rearrange("p (b k) -> p b k", k=32),
                s4t[:, 0:nb].rearrange("p (b one) -> p b one", one=1)
                .broadcast_to((128, nb, 32)),
                ALU.mult),
               inc=(S("vCOMB"), 1))
            prev_wq = (S("vCOMB"), S("vCOMB").n)
            for j in range(nj):
                pslice = (psF1[:, 0:128] if j % 2 == 0 else psF1[:, 512:640])
                em("pe", lambda e, j=j, ps=pslice: e.transpose(
                    ps, wqF[:, j * 128:(j + 1) * 128], sI[:]),
                    waits=w(prev_wq if j == 0 else None,
                            XW if (first and j <= 1) else None,
                            (S("cIOTA"), nID) if (first and j == 0) else None,
                            global_es.get(j - 2)),
                    inc=(S("pROT"), 1))
                em("act", lambda e, j=j, ps=pslice, tc=tcol_of_j(j): e.activation(
                    wqT[:, tc:tc + 128], ps, AF.Copy),
                    waits=[(S("pROT"), S("pROT").n)], inc=(S("aROT"), 1))
                global_es[j] = (S("aROT"), S("aROT").n)
            prev_tr = (S("aROT"), S("aROT").n)

        gW = {}
        for nm, src, ncols, nch in (("up", wupNs, E, 3), ("ga", wgaNs, E, 3),
                                    ("do", wdoNs, FF, 1)):
            for ch in range(nch):
                tcol = (lambda j, ch=ch: j * 384 + ch * 128) if nch == 3 else \
                       (lambda j: j * 128)
                dequant_chunk(src[ch * 128:(ch + 1) * 128, :], ncols, tcol,
                              prev_wq is None)
            em("sp", lambda e, nm=nm: e.dma_start(
                out=g_w_in[nm].ap()[0:1, :].rearrange("one (p o) -> p o", p=128),
                in_=wqT[:]),
               waits=[(S("aROT"), S("aROT").n)], inc=(S("sCB"), 16))
            em("pool", lambda e, nm=nm: e.collective_compute(
                "AllGather", ALU.bypass, replica_groups=[list(range(8))],
                ins=[g_w_in[nm].ap().opt()], outs=[g_w_out[nm].ap().opt()]),
                waits=[(S("sCB"), S("sCB").n)], inc=(S("gRBC"), 1), drain=False,
                is_dma=True)
            gW[nm] = S("gRBC").n
            prev_tr = w(prev_tr, (S("sCB"), S("sCB").n))
        ctxDQ.close()

        ctxLN = ExitStack()
        LN = ctxLN.enter_context
        tl2 = {"cb": [LN(SBR(f"fcb{k}", [128, T], BF16)) for k in range(2)],
               "sqb": [LN(SBR(f"fsq{k}", [128, T], BF16)) for k in range(2)],
               "selA": LN(SBR("fselA", [128, 65], BF16)),
               "selB": LN(SBR("fselB", [128, 65], BF16)),
               "Rbc": LN(SBR("fRbc", [128, T], F32)),
               "Nbc": LN(SBR("fNbc", [128, T], F32)),
               "tmpc": LN(SBR("ftmpc", [128, T], F32)),
               "row_m": LN(SBR("frm", [1, T], F32)),
               "row_v": LN(SBR("frv", [1, T], F32)),
               "psStat": psF2[0:65, :]}
        w_y2 = emit_ln("L1", X, Y, XW, XW, tl2)

        def emit_barrier(k, mxs, wait_extra):
            n = len(mxs)
            for i, mx in enumerate(mxs):
                em("sp", lambda e, k=k, i=i, mx=mx: e.dma_start(
                    out=d_red.ap()[2 * k + i:2 * k + i + 1, :], in_=mx[:]),
                    waits=wait_extra if i == 0 else (), inc=(S("sGN"), 16))
            sA = (S("sGN"), S("sGN").n)
            for i in range(n):
                em("sp", lambda e, k=k, i=i: e.dma_start(
                    out=mrow[:], in_=d_red.ap()[2 * k + i:2 * k + i + 1, :]),
                    waits=w(sA if i == 0 else None,
                            (S("vGR"), S("vGR").n) if i > 0 else None),
                    inc=(S("sGN"), 16))
                em("dve", lambda e, i=i: e.reduce_max(sscl[0:1, i:i + 1], mrow[:],
                                                      AX.X),
                   waits=[(S("sGN"), S("sGN").n)], inc=(S("vGR"), 1))
            em("sp", lambda e, k=k, n=n: e.dma_start(out=cc_in[k].ap()[0:1, 0:n],
                                                     in_=sscl[0:1, 0:n]),
               waits=[(S("vGR"), S("vGR").n)], inc=(S("sGN"), 16))
            em("pool", lambda e, k=k: e.collective_compute(
                "AllReduce", ALU.max, replica_groups=[list(range(8))],
                ins=[cc_in[k].ap().opt()], outs=[cc_out[k].ap().opt()]),
                waits=[(S("sGN"), S("sGN").n)], inc=(S("gRBC"), 1), drain=False,
                is_dma=True)
            em("sp", lambda e, k=k, n=n: e.dma_start(out=sscl[0:1, 4:4 + n],
                                                     in_=cc_out[k].ap()[0:1, 0:n]),
               waits=[(S("gRBC"), S("gRBC").n)], inc=(S("sGN"), 16))
            return (S("sGN"), S("sGN").n)

        def bscal(fn, waits=(), inc=None):
            em("dve", fn, waits=waits, inc=inc)

        def brd(k, src_lanes, dst, n):
            for ci, ln in enumerate(src_lanes):
                em("sp", lambda e, k=k, ci=ci, ln=ln: e.dma_start(
                    out=d_sc2.ap()[k:k + 1, ci:ci + 1], in_=sscl[0:1, ln:ln + 1]),
                    waits=[(S("vGR"), S("vGR").n)] if ci == 0 else (),
                    inc=(S("sGN2"), 16))
            em("sp", lambda e, k=k, n=n, dst=dst: e.dma_start(
                out=dst[:], in_=d_sc2.ap()[k, 0:n].partition_broadcast(128)),
               waits=[(S("sGN2"), S("sGN2").n)], inc=(S("sGN2"), 16))
            return (S("sGN2"), S("sGN2").n)

        for c in range(NEC):
            em("dve", lambda e, c=c: e.reduce_max(tmpr[:], Y[c][:], AX.X,
                                                  apply_absolute_value=True),
               waits=w(w_y2, mZ) if c == 0 else ())
            em("dve", lambda e: e.tensor_tensor(mx0[:], mx0[:], tmpr[:], ALU.max),
               inc=(S("vYM"), 1) if c == NEC - 1 else None)
        bB0 = emit_barrier(0, [mx0], [(S("vYM"), S("vYM").n)])
        bscal(lambda e: e.tensor_scalar(sscl[0:1, 8:9], sscl[0:1, 4:5],
                                        1.0 / 32767.0, None, ALU.mult),
              waits=[bB0])
        bscal(lambda e: e.tensor_scalar(sscl[0:1, 8:9], sscl[0:1, 8:9], 1e-12,
                                        None, ALU.max))
        bscal(lambda e: e.reciprocal(sscl[0:1, 9:10], sscl[0:1, 8:9]),
              inc=(S("vGR"), 1))
        s0done = brd(0, [8, 9], scb0, 2)
        for c in range(NEC):
            em("dve", lambda e, c=c: e.tensor_scalar(iA[:], Y[c][:], scb0[:, 1:2],
                                                     None, ALU.mult),
               waits=[s0done] if c == 0 else ())
            em("dve", lambda e, c=c: e.tensor_scalar(Y[c][:], iA[:], scb0[:, 0:1],
                                                     None, ALU.mult),
               inc=(S("vYM"), 1))
        xqDone = (S("vYM"), S("vYM").n)

        ctxLN.close()
        ctxS = ExitStack()
        FS_ = ctxS.enter_context
        wtu = [FS_(SBR(f"wtu{k}", [128, E], BF16)) for k in range(2)]
        wtg = [FS_(SBR(f"wtg{k}", [128, E], BF16)) for k in range(2)]
        upsg = [FS_(SBR(f"upsg{k}", [128, T], BF16)) for k in range(2)]
        gasg = [FS_(SBR(f"gasg{k}", [128, T], BF16)) for k in range(2)]
        gup3 = g_w_out["up"].ap().rearrange("c (p ec o) -> c p ec o", p=128, ec=8)
        gga3 = g_w_out["ga"].ap().rearrange("c (p ec o) -> c p ec o", p=128, ec=8)
        NOB = FF // 128
        upmm, gamm, upev, gaev, updm, gadm = {}, {}, {}, {}, {}, {}
        upmx, gamx = {}, {}
        for ob in range(NOB):
            sl = ob % 2
            em("sp", lambda e, ob=ob, sl=sl: e.dma_start(
                out=wtu[sl][:].rearrange("p (c o) -> p c o", o=128),
                in_=gup3[ob // 3, :, :, (ob % 3) * 128:(ob % 3) * 128 + 128]),
                waits=w((S("gRBC"), gW["up"]) if ob == 0 else None, upmm.get(ob - 2)),
                inc=(S("ex0" if sl == 0 else "ex1"), 16))
            wU = (S("ex0" if sl == 0 else "ex1"), S("ex0" if sl == 0 else "ex1").n)
            em("sp", lambda e, ob=ob, sl=sl: e.dma_start(
                out=wtg[sl][:].rearrange("p (c o) -> p c o", o=128),
                in_=gga3[ob // 3, :, :, (ob % 3) * 128:(ob % 3) * 128 + 128]),
                waits=w((S("gRBC"), gW["ga"]) if ob == 0 else None, gamm.get(ob - 2)),
                inc=(S("pv0" if sl == 0 else "pv1"), 16))
            wG = (S("pv0" if sl == 0 else "pv1"), S("pv0" if sl == 0 else "pv1").n)
            for ic in range(4):
                for c in range(NEC):
                    em("pe", lambda e, sl=sl, ic=ic, c=c: e.matmul(
                        psF1[:, ic * 512:(ic + 1) * 512],
                        wtu[sl][:, c * 128:(c + 1) * 128],
                        Y[c][:, ic * 512:(ic + 1) * 512],
                        start=(c == 0), stop=(c == NEC - 1)),
                        waits=(w(wU, xqDone if ob == 0 else None,
                                 (S("aROT"), S("aROT").n) if ob == 0 else None,
                                 upev.get(ob - 1))
                               if (ic == 0 and c == 0) else ()),
                        inc=(S("pQK"), 1) if (ic == 3 and c == NEC - 1) else None)
            upmm[ob] = (S("pQK"), S("pQK").n)
            for ic in range(4):
                for c in range(NEC):
                    em("pe", lambda e, sl=sl, ic=ic, c=c: e.matmul(
                        psF2[:, ic * 512:(ic + 1) * 512],
                        wtg[sl][:, c * 128:(c + 1) * 128],
                        Y[c][:, ic * 512:(ic + 1) * 512],
                        start=(c == 0), stop=(c == NEC - 1)),
                        waits=(w(wG, xqDone if ob == 0 else None,
                                 (S("L1ROWS"), S("L1ROWS").n) if ob == 0 else None,
                                 gaev.get(ob - 1))
                               if (ic == 0 and c == 0) else ()),
                        inc=(S("pG"), 1) if (ic == 3 and c == NEC - 1) else None)
            gamm[ob] = (S("pG"), S("pG").n)
            em("act", lambda e, sl=sl: e.activation(upsg[sl][:], psF1[:], AF.Copy),
               waits=w(upmm[ob], updm.get(ob - 2), upmx.get(ob - 2)),
               inc=(S("aQK"), 1))
            upev[ob] = (S("aQK"), S("aQK").n)
            em("act", lambda e, sl=sl: e.activation(gasg[sl][:], psF2[:], AF.Copy),
               waits=w(gamm[ob], gadm.get(ob - 2), gamx.get(ob - 2)),
               inc=(S("aPP"), 1))
            gaev[ob] = (S("aPP"), S("aPP").n)
            em("act", lambda e, ob=ob, sl=sl: e.dma_start(
                out=d_up.ap()[ob * 128:(ob + 1) * 128, :], in_=upsg[sl][:]),
                waits=[upev[ob]], inc=(S("sO"), 16), drain=False, is_dma=True)
            updm[ob] = (S("sO"), S("sO").n)
            em("act", lambda e, ob=ob, sl=sl: e.dma_start(
                out=d_ga.ap()[ob * 128:(ob + 1) * 128, :], in_=gasg[sl][:]),
                waits=[gaev[ob]], inc=(S("sR16"), 16), drain=False, is_dma=True)
            gadm[ob] = (S("sR16"), S("sR16").n)
            em("dve", lambda e, sl=sl: e.reduce_max(tmpr[:], upsg[sl][:], AX.X,
                                                    apply_absolute_value=True),
               waits=w(upev[ob], mZ if ob == 0 else None))
            em("dve", lambda e: e.tensor_tensor(mxu[:], mxu[:], tmpr[:], ALU.max),
               inc=(S("aNQ"), 1))
            upmx[ob] = (S("aNQ"), S("aNQ").n)
            em("dve", lambda e, sl=sl: e.reduce_max(tmpr[:], gasg[sl][:], AX.X,
                                                    apply_absolute_value=True),
               waits=[gaev[ob]])
            em("dve", lambda e: e.tensor_tensor(mxg[:], mxg[:], tmpr[:], ALU.max),
               inc=(S("vNM"), 1))
            gamx[ob] = (S("vNM"), S("vNM").n)

        bB1 = emit_barrier(1, [mxu, mxg],
                           [(S("aNQ"), S("aNQ").n), (S("vNM"), S("vNM").n)])
        bscal(lambda e: e.tensor_scalar(sscl[0:1, 8:10], sscl[0:1, 4:6],
                                        1.0 / 32767.0, None, ALU.mult),
              waits=[bB1])
        bscal(lambda e: e.tensor_scalar(sscl[0:1, 8:10], sscl[0:1, 8:10], 1e-12,
                                        None, ALU.max))
        bscal(lambda e: e.reciprocal(sscl[0:1, 10:12], sscl[0:1, 8:10]))
        bscal(lambda e: e.tensor_scalar(sscl[0:1, 12:13], sscl[0:1, 9:10],
                                        float(2.0 ** -16), None, ALU.mult),
              inc=(S("vGR"), 1))
        s1done = brd(1, [11, 9, 12, 10], scb1, 4)

        ctxS.close()
        ctxA.close()
        ctxM = ExitStack()
        M_ = ctxM.enter_context
        garaw = [M_(SBL(f"gar{k}", [128, T], BF16)) for k in range(NOB)]
        dwt = [M_(SBL(f"dwt{k}", [128, 512], BF16)) for k in range(2)]
        garld = {}
        phase5_done = w(upmm[NOB - 1], gamm[NOB - 1], upev[NOB - 1], gaev[NOB - 1],
                        (S("sR16"), S("sR16").n), (S("sO"), S("sO").n))
        g2p = {}
        ia_done, sg_done, ib_done = {}, {}, {}

        def g2_stageA(ob):
            iA_t = iA if ob % 2 == 0 else iA3
            Xp = X[1 + ob % 2]
            em("pool", lambda e, ob=ob: e.dma_start(
                out=garaw[ob][:], in_=d_ga.ap()[ob * 128:(ob + 1) * 128, :]),
                waits=(phase5_done + [s1done] if ob == 0 else []),
                inc=(S("vGST"), 16), drain=False, is_dma=True)
            garld[ob] = (S("vGST"), S("vGST").n)
            em("dve", lambda e, ob=ob, iA_t=iA_t: e.tensor_scalar(
                iA_t[:], garaw[ob][:], scb1[:, 0:1], None, ALU.mult),
               waits=w(garld[ob], sX1 if ob == 0 else None),
               inc=(S("qk0"), 1))
            ia_done[ob] = (S("qk0"), S("qk0").n)
            em("act", lambda e, iA_t=iA_t, Xp=Xp: e.activation(
                Xp[:], iA_t[:], AF.Sigmoid, scale=scb1[:, 1:2]),
               waits=w(ia_done[ob], ib_done.get(ob - 2)), inc=(S("aG"), 1))
            sg_done[ob] = (S("aG"), S("aG").n)

        def g2_stageB(ob):
            iA_t = iA if ob % 2 == 0 else iA3
            iB_t = iA2
            Xp = X[1 + ob % 2]
            em("dve", lambda e, iB_t=iB_t, Xp=Xp: e.tensor_scalar(
                iB_t[:], Xp[:], 65536.0, 65535.0, ALU.mult, ALU.min),
               waits=[sg_done[ob]], inc=(S("qk0"), 1))
            ib_done[ob] = (S("qk0"), S("qk0").n)
            em("dve", lambda e, ob=ob, iA_t=iA_t, iB_t=iB_t: e.scalar_tensor_tensor(
                garaw[ob][:], iA_t[:], scb1[:, 2:3], iB_t[:], ALU.mult, ALU.mult))
            em("dve", lambda e, ob=ob: e.reduce_max(tmpr[:], garaw[ob][:], AX.X,
                                                    apply_absolute_value=True))
            em("dve", lambda e: e.tensor_tensor(mx2[:], mx2[:], tmpr[:], ALU.max),
               inc=(S("vGVAR"), 1))
            g2p[ob] = (S("vGVAR"), S("vGVAR").n)

        for ob in range(NOB + 1):
            if ob < NOB:
                g2_stageA(ob)
            if ob >= 1:
                g2_stageB(ob - 1)

        bB2 = emit_barrier(2, [mx2], [(S("vGVAR"), S("vGVAR").n)])
        bscal(lambda e: e.tensor_scalar(sscl[0:1, 13:14], sscl[0:1, 4:5],
                                        1.0 / 32767.0, None, ALU.mult),
              waits=[bB2])
        bscal(lambda e: e.tensor_scalar(sscl[0:1, 13:14], sscl[0:1, 13:14], 1e-12,
                                        None, ALU.max))
        bscal(lambda e: e.reciprocal(sscl[0:1, 14:15], sscl[0:1, 13:14]))
        bscal(lambda e: e.tensor_tensor(sscl[0:1, 15:16], sscl[0:1, 13:14],
                                        sscl[0:1, 8:9], ALU.mult),
              inc=(S("vGR"), 1))
        s2done = brd(2, [14, 15], scb2, 2)

        odm, ow, op_ = {}, {}, {}
        for ob in range(NOB):
            sl = ob % 2
            em("pool", lambda e, ob=ob, sl=sl: e.dma_start(
                out=X[6 + sl][:], in_=d_up.ap()[ob * 128:(ob + 1) * 128, :]),
                waits=w((S("sO"), S("sO").n) if ob == 0 else None,
                        op_.get(ob - 2)),
                inc=(S("aLD" if sl == 0 else "vLDN"), 16), drain=False, is_dma=True)
            lU = (S("aLD" if sl == 0 else "vLDN"),
                  S("aLD" if sl == 0 else "vLDN").n)
            em("dve", lambda e, sl=sl: e.tensor_scalar(iA[:], X[6 + sl][:],
                                                       scb1[:, 3:4], None, ALU.mult),
               waits=w(lU, s2done if ob == 0 else None, g2p.get(NOB - 1) if ob == 0 else None),
               inc=(S("ev0"), 1))
            op_[ob] = (S("ev0"), S("ev0").n)
            em("dve", lambda e, ob=ob: e.tensor_scalar(iA2[:], garaw[ob][:],
                                                       scb2[:, 0:1], None, ALU.mult))
            em("dve", lambda e, ob=ob: e.scalar_tensor_tensor(
                garaw[ob][:], iA[:], scb2[:, 1:2], iA2[:], ALU.mult, ALU.mult),
               inc=(S("ev0"), 1))
            em("dve", lambda e, ob=ob: e.reduce_max(tmpr[:], garaw[ob][:], AX.X,
                                                    apply_absolute_value=True))
            em("dve", lambda e: e.tensor_tensor(mx3[:], mx3[:], tmpr[:], ALU.max),
               inc=(S("aGSD"), 1))

        bB3 = emit_barrier(3, [mx3], [(S("aGSD"), S("aGSD").n)])
        bscal(lambda e: e.tensor_scalar(sscl[0:1, 16:17], sscl[0:1, 4:5],
                                        1.0 / 32767.0, None, ALU.mult),
              waits=[bB3])
        bscal(lambda e: e.tensor_scalar(sscl[0:1, 16:17], sscl[0:1, 16:17], 1e-12,
                                        None, ALU.max))
        bscal(lambda e: e.reciprocal(sscl[0:1, 17:18], sscl[0:1, 16:17]),
              inc=(S("vGR"), 1))
        s3done = brd(3, [17, 16], scb3, 2)

        oq = {}
        for ob in range(NOB):
            em("dve", lambda e, ob=ob: e.tensor_scalar(iA[:], garaw[ob][:],
                                                       scb3[:, 0:1], None, ALU.mult),
               waits=w(s3done if ob == 0 else None,
                       (S("aGSD"), S("aGSD").n) if ob == 0 else None,
                       oq.get(ob - 1)),
               inc=(S("ev0"), 1))
            em("act", lambda e, ob=ob: e.activation(garaw[ob][:], iA[:],
                                                    AF.Identity, scale=scb3[:, 1:2]),
               waits=[(S("ev0"), S("ev0").n)], inc=(S("vGSC"), 1))
            oq[ob] = (S("vGSC"), S("vGSC").n)

        gdo3 = g_w_out["do"].ap().rearrange("c (p ic o) -> c p ic o", p=128, ic=24)
        dmm, dev = {}, {}
        for o2c in range(NEC):
            ps = psF1 if o2c % 2 == 0 else psF2
            for g in range(NOB // 4):
                sl = g % 2
                prev_use = (dmm.get((o2c, 4 * g - 5)) if g >= 2
                            else (dmm.get((o2c - 1, 4 * g + 19))
                                  if o2c > 0 else None))
                em("sp", lambda e, g=g, sl=sl, o2c=o2c: e.dma_start(
                    out=dwt[sl][:].rearrange("p (j o) -> p j o", o=128),
                    in_=gdo3[o2c, :, 4 * g:4 * g + 4, :]),
                    waits=w((S("gRBC"), gW["do"]) if (o2c == 0 and g == 0) else None,
                            prev_use),
                    inc=(S("aV" if sl == 0 else "pV"), 16))
                lW = (S("aV" if sl == 0 else "pV"),
                      S("aV" if sl == 0 else "pV").n)
                for j in range(4):
                    i24 = 4 * g + j
                    for ic in range(4):
                        em("pe", lambda e, i24=i24, j=j, sl=sl, ic=ic, ps=ps: e.matmul(
                            ps[:, ic * 512:(ic + 1) * 512],
                            dwt[sl][:, j * 128:(j + 1) * 128],
                            garaw[i24][:, ic * 512:(ic + 1) * 512],
                            start=(i24 == 0), stop=(i24 == NOB - 1)),
                            waits=(w(lW if j == 0 else None, oq[i24],
                                     dev.get(o2c - 2) if j == 0 else None,
                                     (S("aQK"), S("aQK").n) if (o2c == 0 and i24 == 0) else None,
                                     (S("aPP"), S("aPP").n) if (o2c == 1 and i24 == 0) else None,
                                     (S("L1ROWS"), S("L1ROWS").n) if (o2c == 1 and i24 == 0) else None)
                                   if ic == 0 else ()),
                            inc=(S("pNRM"), 1) if ic == 3 else None)
                    dmm[(o2c, i24)] = (S("pNRM"), S("pNRM").n)
            xsl = 5 + o2c % 2
            em("sp", lambda e, o2c=o2c, xsl=xsl: e.dma_start(
                out=X[xsl][:], in_=d_x1.ap()[o2c * 128:(o2c + 1) * 128, :]),
                waits=w(sX1 if o2c == 0 else None, dev.get(o2c - 2)),
                inc=(S("sLp"), 16))
            ysl = 3 + o2c % 2
            em("dve", lambda e, ps=ps, xsl=xsl, ysl=ysl: e.tensor_tensor(
                X[ysl][:], ps[:], X[xsl][:], ALU.add),
               waits=w(dmm[(o2c, NOB - 1)], (S("sLp"), S("sLp").n),
                       dev.get(o2c - 2)),
               inc=(S("vOG"), 1))
            dv = (S("vOG"), S("vOG").n)
            em("pool", lambda e, o2c=o2c, ysl=ysl: e.dma_start(
                out=yout[o2c * 128:(o2c + 1) * 128, :], in_=X[ysl][:]),
                waits=[dv], inc=(S("vAPP"), 16), drain=False, is_dma=True)
            dev[o2c] = (S("vAPP"), S("vAPP").n)
        em("sp", lambda e: e.nop(), waits=[(S("vAPP"), S("vAPP").n)] +
           ([(S("sDBG"), S("sDBG").n), (S("sDBQ"), S("sDBQ").n),
             (S("sDBP"), S("sDBP").n), (S("sDBP2"), S("sDBP2").n)] if KDBG else []),
           drain=False)
        ctxM.close()
        ctxF2.close()
        ctxFP.close()

    with ctx0, ctxA, ctxB:
        with nc.Block() as block:
            @block.sync
            def _(sp):
                EM.run("sp", sp)

            @block.tensor
            def _(pe):
                EM.run("pe", pe)

            @block.vector
            def _(dve):
                EM.run("dve", dve)

            @block.scalar
            def _(act):
                EM.run("act", act)

            @block.gpsimd
            def _(pool):
                EM.run("pool", pool)
    return nc


def kernel(**inputs):
    g = lambda k: np.ascontiguousarray(np.asarray(inputs[k], dtype=np.float32))
    x = g("L0_attn_x")
    TT_ = lambda a: np.ascontiguousarray(a.T)
    base = {
        "wqT": TT_(g("L0_attn_wq")), "wkT": TT_(g("L0_attn_wk")),
        "wvT": TT_(g("L0_attn_wv")), "wgT": TT_(g("L0_attn_wg")),
        "woT": TT_(g("L0_attn_wo")), "wdT": TT_(g("L0_attn_wd")),
        "bq": np.ascontiguousarray(g("L0_attn_bq").reshape(NEC, 128).T),
        "bk": np.ascontiguousarray(g("L0_attn_bk").reshape(NEC, 128).T),
        "bg": np.ascontiguousarray(g("L0_attn_bg").reshape(NEC, 128).T),
        "bo": np.ascontiguousarray(g("L0_attn_bo").reshape(NEC, 128).T),
        "gamma": np.ascontiguousarray(g("L0_attn_gamma").reshape(NEC, 128).T),
        "beta": np.ascontiguousarray(g("L0_attn_beta").reshape(NEC, 128).T),
        "bv": g("L0_attn_bv").reshape(1, E),
        "bd": g("L0_attn_bd").reshape(H, 1),
    }
    wupN = g("L0_ffn_w_up")
    wgaN = g("L0_ffn_w_gate")
    wdoN = g("L0_ffn_w_down")
    nc = build_nc()
    in_maps = []
    for b in range(8):
        m = dict(base)
        m["x"] = TT_(x[b])
        m["wupNs"] = np.ascontiguousarray(wupN[b * 384:(b + 1) * 384, :])
        m["wgaNs"] = np.ascontiguousarray(wgaN[b * 384:(b + 1) * 384, :])
        m["wdoNs"] = np.ascontiguousarray(wdoN[b * 128:(b + 1) * 128, :])
        in_maps.append(m)
    res = run_bass_kernel_spmd(nc, in_maps, core_ids=list(range(8)))
    out = np.stack([np.asarray(res.results[b]["y"]).T for b in range(8)])
    return out.astype(np.float32)


# version 22 (final)
# speedup vs baseline: 1.0670x; 1.0124x over previous
import os
import numpy as np
from contextlib import ExitStack

import concourse.bass as bass
import concourse.mybir as mybir
from concourse.bass_utils import run_bass_kernel_spmd

F32 = mybir.dt.float32
BF16 = mybir.dt.bfloat16
I32 = mybir.dt.int32
FP16 = mybir.dt.float16
AF = mybir.ActivationFunctionType
ALU = mybir.AluOpType
AX = mybir.AxisListType

T, E, H, HD, FF = 2048, 1024, 16, 64, 3072
NEC = E // 128
NTB = T // 128
TWOPI = float(2.0 * np.pi)
KSTOP = os.environ.get("KSTOP", "full")


class Sem:
    def __init__(self, ctx, nc, name):
        self.h = ctx.enter_context(nc.semaphore(name))
        self.n = 0


class Emitter:
    ENGS = ("pe", "dve", "act", "pool", "sp")

    def __init__(self):
        self.ops = {e: [] for e in self.ENGS}

    def em(self, eng, fn, waits=(), inc=None, drain=None, is_dma=False):
        if drain is None:
            drain = eng in ("dve", "act", "pool")
        w = tuple((s, v) for (s, v) in waits if v > 0)
        r = None
        if inc is not None:
            s, k = inc
            s.n += k
            r = s.n
        self.ops[eng].append((w, fn, inc, r, drain, is_dma))
        return r

    def run(self, eng, engobj):
        compute = eng in ("dve", "act", "pool")
        for w, fn, inc, _r, drain, is_dma in self.ops[eng]:
            for s, v in w:
                engobj.wait_ge(s.h, v)
            if drain:
                engobj.drain()
            inst = fn(engobj)
            if inc is not None:
                if compute and not is_dma:
                    engobj.drain().then_inc(inc[0].h, inc[1])
                else:
                    inst.then_inc(inc[0].h, inc[1])


def build_nc():
    nc = bass.Bass("TRN2", target_bir_lowering=False)

    xT = nc.dram_tensor("x", [E, T], F32, kind="ExternalInput").ap()
    wT = {nm: nc.dram_tensor(nm + "T", [E, E], F32, kind="ExternalInput").ap()
          for nm in ("wq", "wk", "wv", "wg", "wo")}
    wdT = nc.dram_tensor("wdT", [E, H], F32, kind="ExternalInput").ap()
    bfm_names = ("bq", "bk", "bg", "bo", "gamma", "beta")
    bfm = {nm: nc.dram_tensor(nm, [128, NEC], F32, kind="ExternalInput").ap()
           for nm in bfm_names}
    bv_in = nc.dram_tensor("bv", [1, E], F32, kind="ExternalInput").ap()
    bd_in = nc.dram_tensor("bd", [H, 1], F32, kind="ExternalInput").ap()
    wupNs = nc.dram_tensor("wupNs", [384, E], F32, kind="ExternalInput").ap()
    wgaNs = nc.dram_tensor("wgaNs", [384, E], F32, kind="ExternalInput").ap()
    wdoNs = nc.dram_tensor("wdoNs", [128, FF], F32, kind="ExternalInput").ap()
    yout = nc.dram_tensor("y", [E, T], F32, kind="ExternalOutput").ap()
    KDBG = os.environ.get("KDBG")
    if KDBG:
        dbg = {nm: nc.dram_tensor("dbg_" + nm, shp, F32, kind="ExternalOutput").ap()
               for nm, shp in (("y0", [128, T]), ("ld", [H, 1]), ("qf", [64, 2 * T]),
                               ("kf", [64, 2 * T]), ("vp", [128, T]), ("og0", [128, T]),
                               ("mneg", [128, 2]), ("cos", [64, T]), ("sin", [64, T]),
                               ("gate", [128, T]), ("og0p", [128, T]),
                               ("pf", [65, 1024]), ("pp", [65, 1024]),
                               ("qf0", [64, 2 * T]), ("kf0", [64, 2 * T]),
                               ("vp0", [128, T]), ("psq0", [128, T]),
                               ("qpre0", [64, 2 * T]), ("wbfq0", [128, E]))}
        dbg.update({f"yc{c}": nc.dram_tensor(f"dbg_yc{c}", [128, T], F32,
                                             kind="ExternalOutput").ap()
                    for c in range(8)})

    d_row = nc.dram_tensor("d_row", [4, T], F32)
    d_ld = nc.dram_tensor("d_ld", [H, 1], F32)
    d_c16 = nc.dram_tensor("d_c16", [H, T], F32)
    d_r16 = nc.dram_tensor("d_r16", [H, T], F32)
    d_sc = nc.dram_tensor("d_sc", [1, 64], F32)
    d_x1 = nc.dram_tensor("d_x1", [E, T], F32)
    d_up = nc.dram_tensor("d_up", [FF, T], BF16)
    d_ga = nc.dram_tensor("d_ga", [FF, T], BF16)
    d_ss = nc.dram_tensor("d_ss", [1, 12288], F32)
    d_red = nc.dram_tensor("d_red", [8, 128], F32)
    d_sc2 = nc.dram_tensor("d_sc2", [4, 8], F32)
    g_w_in = {nm: nc.dram_tensor(f"g_{nm}_in", [1, 128 * FF], BF16)
              for nm in ("up", "ga", "do")}
    g_w_out = {nm: nc.dram_tensor(f"g_{nm}_out", [8, 128 * FF], BF16)
               for nm in ("up", "ga", "do")}
    cc_in = [nc.dram_tensor(f"cc_in{k}", [1, 512], F32) for k in range(4)]
    cc_out = [nc.dram_tensor(f"cc_out{k}", [1, 512], F32) for k in range(4)]

    ctx0 = ExitStack()

    def P(t):
        return ctx0.enter_context(t)

    def SBL(name, shape, dt):
        return nc.sbuf_tensor(name, shape, dt, side="left")

    def SBR(name, shape, dt):
        return nc.sbuf_tensor(name, shape, dt, side="right")

    def PSL(name, shape, dt):
        return nc.psum_tensor(name, shape, dt, side="left")

    def PSR(name, shape, dt):
        return nc.psum_tensor(name, shape, dt, side="right")

    SEMS = {}

    def S(name):
        if name not in SEMS:
            SEMS[name] = Sem(ctx0, nc, name)
        return SEMS[name]

    X = [P(SBL(f"X{c}", [128, T], F32)) for c in range(NEC)]
    sb_b = {nm: P(SBL("s_" + nm, [128, NEC], F32)) for nm in bfm_names}
    bv_bf = P(SBL("bv_bf", [1, E], BF16))
    wd_sb = P(SBL("wd_sb", [128, NEC * H], BF16))
    bd_sb = P(SBL("bd_sb", [H, 1], F32))
    ones_row = P(SBL("ones_row", [1, 128], BF16))
    ones_col = P(SBL("ones_col", [128, 1], BF16))
    Drelu0 = P(SBL("Drelu0", [128, 128], BF16))
    ldI = P(SBL("ldI", [128, H * 128], BF16))
    lam = P(SBL("lam", [128, H * 16], F32))
    cos64 = P(SBL("cos64", [64, T], BF16))
    sinS = P(SBL("sinS", [64, T], BF16))
    ld16 = P(SBL("ld16", [H, 1], F32))
    sc2 = P(SBL("sc2", [128, 8], F32))
    Pi64 = P(SBL("Pi64", [64, 64], BF16))
    ceps = P(SBL("ceps", [65, 1], F32))
    cb5 = P(SBL("cb5", [16, 1], F32))
    z65 = P(SBL("z65", [1, 65], BF16))
    bk8 = P(SBL("bk8", [128, NEC], F32))

    EM = Emitter()
    em = EM.em

    ctxP0 = ExitStack()
    P0 = ctxP0.enter_context
    it_t = P0(SBR("it_t", [128, T], F32))
    it_jp = P0(SBR("it_jp", [128, 16], F32))
    I128 = P0(SBR("I128", [128, 128], F32))
    ldbc = P0(SBR("ldbc", [128, H], F32))
    ldn16 = P0(SBR("ldn16", [H, 1], F32))
    wd_f32 = P0(SBR("wd_f32", [128, NEC * H], F32))
    bv_f32 = P0(SBR("bv_f32", [1, E], F32))

    for c in range(NEC):
        em("sp", lambda e, c=c: e.dma_start(
            out=X[c][:], in_=xT[c * 128:(c + 1) * 128, :]), inc=(S("sLx"), 16))
    LX = S("sLx").n
    for nm in bfm_names:
        em("sp", lambda e, nm=nm: e.dma_start(out=sb_b[nm][:], in_=bfm[nm]),
           inc=(S("sLp"), 16))
    em("sp", lambda e: e.dma_start(out=bd_sb[:], in_=bd_in), inc=(S("sLp"), 16))
    em("sp", lambda e: e.dma_start(
        out=wd_f32[:].rearrange("p (c h) -> p c h", h=H),
        in_=wdT.rearrange("(c p) h -> p c h", p=128)), inc=(S("sLp"), 16))
    em("sp", lambda e: e.dma_start(out=bv_f32[:], in_=bv_in), inc=(S("sLp"), 16))
    LP = S("sLp").n

    em("pool", lambda e: e.memset(z65[:], 0.0))
    em("pool", lambda e: e.memset(ceps[:], 1e-5))
    em("pool", lambda e: e.memset(cb5[:], float(-5.0 * np.log(2.0))))
    em("pool", lambda e: e.iota(it_t[:], pattern=[[1, T]], base=0, channel_multiplier=0,
                                allow_small_or_imprecise_dtypes=True))
    em("pool", lambda e: e.iota(it_jp[:], pattern=[[128, 16]], base=0, channel_multiplier=1,
                                allow_small_or_imprecise_dtypes=True))
    em("pool", lambda e: e.memset(I128[:], 1.0))
    em("pool", lambda e: e.affine_select(I128[:], I128[:], pattern=[[1, 128]],
                                         compare_op=ALU.is_equal, fill=0.0,
                                         base=0, channel_multiplier=-1))
    em("pool", lambda e: e.tensor_copy(wd_sb[:], wd_f32[:]),
       waits=[(S("sLp"), LP)], inc=(S("cP"), 1))

    ctxT = ExitStack()
    PT = ctxT.enter_context
    it_tmp = PT(SBR("it_tmp", [64, T], F32))
    it_i32 = PT(SBR("it_i32", [64, T], I32))
    frac = PT(SBR("frac", [64, T], F32))
    invf = PT(SBR("invf", [64, 1], F32))
    idv = PT(SBR("idv", [64, 1], I32))
    par = PT(SBR("par", [64, 1], F32))
    emask = PT(SBR("emask", [64, 1], F32))
    pm0 = PT(SBR("pm0", [64, 64], F32))
    pm1 = PT(SBR("pm1", [64, 64], F32))
    dscr = PT(SBR("dscr", [128, 128], F32))

    em("pool", lambda e: e.iota(dscr[:], pattern=[[-1, 128]], base=0,
                                channel_multiplier=1, allow_small_or_imprecise_dtypes=True))
    em("pool", lambda e: e.tensor_scalar(Drelu0[:], dscr[:], 0.0, None, ALU.max),
       inc=(S("cDRELU"), 1))
    em("pool", lambda e: e.iota(idv[:], pattern=[[1, 1]], base=0, channel_multiplier=1),
       inc=(S("cIOTA"), 1))
    em("dve", lambda e: e.tensor_scalar(idv[:], idv[:], 1, None, ALU.bitwise_and),
       waits=[(S("cIOTA"), 1)], inc=(S("cPAR"), 1))
    em("dve", lambda e: e.tensor_copy(par[:], idv[:]), waits=[(S("cPAR"), 1)])
    em("dve", lambda e: e.tensor_scalar(emask[:], par[:], -1.0, 1.0, ALU.mult, ALU.add),
       inc=(S("vMASK"), 1))
    em("pool", lambda e: e.iota(idv[:], pattern=[[1, 1]], base=0, channel_multiplier=1),
       waits=[(S("vMASK"), 1)], inc=(S("cIOTA"), 1))
    em("dve", lambda e: e.tensor_scalar(idv[:], idv[:], 31, None, ALU.bitwise_and),
       waits=[(S("cIOTA"), 2)], inc=(S("cD32"), 1))
    em("dve", lambda e: e.tensor_copy(invf[:], idv[:]), waits=[(S("cD32"), 1)],
       inc=(S("vINVF"), 1))
    em("act", lambda e: e.activation(invf[:], invf[:], AF.Exp,
                                     scale=float(-np.log(10000.0) / 32.0)),
       waits=[(S("vINVF"), 1)], inc=(S("aINVF"), 1))
    em("pool", lambda e: e.memset(pm0[:], 1.0))
    em("pool", lambda e: e.affine_select(pm0[:], pm0[:], pattern=[[1, 64]],
                                         compare_op=ALU.is_equal, fill=0.0,
                                         base=-1, channel_multiplier=-1))
    em("pool", lambda e: e.memset(pm1[:], 1.0))
    em("pool", lambda e: e.affine_select(pm1[:], pm1[:], pattern=[[1, 64]],
                                         compare_op=ALU.is_equal, fill=0.0,
                                         base=1, channel_multiplier=-1), inc=(S("cPM"), 1))
    em("dve", lambda e: e.tensor_scalar(pm0[:], pm0[:], emask[:], None, ALU.mult),
       waits=[(S("cPM"), 1)])
    em("dve", lambda e: e.tensor_scalar(pm1[:], pm1[:], par[:], None, ALU.mult))
    em("dve", lambda e: e.tensor_tensor(Pi64[:], pm0[:], pm1[:], ALU.add),
       inc=(S("vPI"), 1))

    tcor = PT(SBR("tcor", [64, T], F32))

    def emit_trig(dst, shift, idx):
        w0 = [(S("aINVF"), 1), (S("cP"), 1)]
        if idx == 1:
            w0.append((S("aTRG"), 1))
        em("dve", lambda e: e.tensor_scalar(it_tmp[:], it_t[0:64, :], invf[:, 0:1],
                                            None, ALU.mult), waits=w0)
        em("dve", lambda e, sh=shift: e.tensor_scalar(it_tmp[:], it_tmp[:],
                                                      float(1.0 / TWOPI), sh + 1.0,
                                                      ALU.mult, ALU.add))
        em("dve", lambda e: e.tensor_copy(it_i32[:], it_tmp[:]))
        em("dve", lambda e: e.tensor_copy(frac[:], it_i32[:]))
        em("dve", lambda e: e.tensor_tensor(tcor[:], frac[:], it_tmp[:], ALU.is_gt))
        em("dve", lambda e: e.tensor_tensor(frac[:], frac[:], tcor[:], ALU.subtract))
        em("dve", lambda e: e.scalar_tensor_tensor(frac[:], it_tmp[:], -0.5, frac[:],
                                                   ALU.add, ALU.subtract),
           inc=(S("vTRGx"), 1))
        em("act", lambda e, d=dst: e.activation(d, frac[:], AF.Sin, scale=TWOPI),
           waits=[(S("vTRGx"), idx + 1)], inc=(S("aTRG"), 1))
    emit_trig(cos64[:], 0.25, 0)
    emit_trig(sinS[:], 0.0, 1)
    em("dve", lambda e: e.tensor_scalar(cos64[:], cos64[:], -1.0, None, ALU.mult),
       waits=[(S("aTRG"), 1)], inc=(S("vTRIG"), 1))
    em("dve", lambda e: e.tensor_scalar(par[:], par[:], -2.0, 1.0, ALU.mult, ALU.add),
       waits=[(S("aTRG"), 2)])
    em("dve", lambda e: e.tensor_scalar(sinS[:], sinS[:], par[:, 0:1], None, ALU.mult),
       inc=(S("vTRIG"), 1))
    em("dve", lambda e: e.memset(ones_row[:], 1.0))
    em("dve", lambda e: e.memset(ones_col[:], 1.0), inc=(S("vONES"), 1))
    em("dve", lambda e: e.tensor_copy(bv_bf[:], bv_f32[:]), waits=[(S("sLp"), LP)],
       inc=(S("vBV"), 1))
    ctxT.close()

    ctxA = ExitStack()
    A = ctxA.enter_context
    Y = [A(SBL(f"Y{c}", [128, T], BF16)) for c in range(NEC)]

    ctxL = ExitStack()
    PL = ctxL.enter_context
    cb = [PL(SBR(f"cb{k}", [128, T], BF16)) for k in range(2)]
    sqb = [PL(SBR(f"sqb{k}", [128, T], BF16)) for k in range(2)]
    Rbc = PL(SBR("Rbc", [128, T], F32))
    Nbc = PL(SBR("Nbc", [128, T], F32))
    tmpc = PL(SBR("tmpc", [128, T], F32))
    row_m = PL(SBR("row_m", [1, T], F32))
    row_v = PL(SBR("row_v", [1, T], F32))
    sc2bf = PL(SBR("sc2bf", [128, 8], BF16))
    c16t = PL(SBR("c16t", [H, T], F32))
    psStat = PL(PSR("psStat", [65, T], F32))
    psPD = PL(PSR("psPD", [H, 1], F32))
    selA = PL(SBR("selA", [128, 65], BF16))
    selB = PL(SBR("selB", [128, 65], BF16))

    def emit_ln(pref, src_chunks, dst_chunks, x_wait, sel_wait, tl):
        em("pool", lambda e: e.memset(tl["selA"][:], 0.0), waits=sel_wait)
        em("pool", lambda e: e.memset(tl["selA"][:, 0:1], 1.0))
        em("pool", lambda e: e.memset(tl["selB"][:], 0.0))
        em("pool", lambda e: e.memset(tl["selB"][:, 64:65], 1.0),
           inc=(S(pref + "SEL"), 1))
        nsel = S(pref + "SEL").n
        psStat = tl["psStat"]
        mm_done = [0, 0]
        for c in range(NEC):
            b = c % 2
            em("dve", lambda e, c=c, b=b: e.tensor_copy(tl["cb"][b][:], src_chunks[c][:]),
               waits=list(x_wait) + ([(S(pref + f"MM{b}"), mm_done[b])] if mm_done[b] else []),
               inc=(S(pref + f"CB{b}"), 1))
            vc = S(pref + f"CB{b}").n
            em("act", lambda e, c=c, b=b: e.activation(tl["sqb"][b][:], src_chunks[c][:],
                                                       AF.Square),
               waits=list(x_wait) + ([(S(pref + f"MM{b}"), mm_done[b])] if mm_done[b] else []),
               inc=(S(pref + f"SQ{b}"), 1))
            ac = S(pref + f"SQ{b}").n
            for ic in range(4):
                em("pe", lambda e, c=c, b=b, ic=ic: e.matmul(
                    psStat[:, ic * 512:(ic + 1) * 512], tl["selA"][:],
                    tl["cb"][b][:, ic * 512:(ic + 1) * 512],
                    start=(c == 0), stop=False),
                    waits=[(S(pref + f"CB{b}"), vc), (S(pref + "SEL"), nsel),
                           (S("vONES"), 1)] if ic == 0 else ())
            for ic in range(4):
                em("pe", lambda e, c=c, b=b, ic=ic: e.matmul(
                    psStat[:, ic * 512:(ic + 1) * 512], tl["selB"][:],
                    tl["sqb"][b][:, ic * 512:(ic + 1) * 512],
                    start=False, stop=(c == NEC - 1)),
                    waits=[(S(pref + f"SQ{b}"), ac)] if ic == 0 else (),
                    inc=(S(pref + f"MM{b}"), 1) if ic == 3 else None)
            mm_done[b] = S(pref + f"MM{b}").n
        em("act", lambda e: e.activation(tl["row_m"][:], psStat[0:1, :], AF.Copy,
                                         scale=1.0 / E),
           waits=[(S(pref + "MM0"), mm_done[0]), (S(pref + "MM1"), mm_done[1])])
        em("act", lambda e: e.activation(tl["row_v"][:], psStat[64:65, :], AF.Copy,
                                         scale=1.0 / E),
           inc=(S(pref + "ROWS"), 1))
        nrows = S(pref + "ROWS").n
        em("dve", lambda e: e.tensor_tensor(tl["tmpc"][0:1, :], tl["row_m"][:],
                                            tl["row_m"][:], ALU.mult),
           waits=[(S(pref + "ROWS"), nrows)])
        em("dve", lambda e: e.tensor_tensor(tl["row_v"][:], tl["row_v"][:],
                                            tl["tmpc"][0:1, :], ALU.subtract),
           inc=(S(pref + "VAR"), 1))
        em("act", lambda e: e.activation(tl["row_v"][:], tl["row_v"][:], AF.Sqrt,
                                         bias=ceps[0:1, :]),
           waits=[(S(pref + "VAR"), S(pref + "VAR").n), (S("cP"), 1)],
           inc=(S(pref + "SD"), 1))
        em("dve", lambda e: e.reciprocal(tl["row_v"][:], tl["row_v"][:]),
           waits=[(S(pref + "SD"), S(pref + "SD").n)])
        em("dve", lambda e: e.tensor_tensor(tl["row_m"][:], tl["row_m"][:],
                                            tl["row_v"][:], ALU.mult))
        em("dve", lambda e: e.tensor_scalar(tl["row_m"][:], tl["row_m"][:], -1.0, None,
                                            ALU.mult),
           inc=(S(pref + "RW"), 1))
        em("sp", lambda e: e.dma_start(out=d_row.ap()[0:1, :], in_=tl["row_v"][:]),
           waits=[(S(pref + "RW"), S(pref + "RW").n)], inc=(S(pref + "DW"), 16))
        em("sp", lambda e: e.dma_start(out=d_row.ap()[1:2, :], in_=tl["row_m"][:]),
           inc=(S(pref + "DW"), 16))
        em("sp", lambda e: e.dma_start(out=tl["Rbc"][:],
                                       in_=d_row.ap()[0, :].partition_broadcast(128)),
           waits=[(S(pref + "DW"), S(pref + "DW").n)], inc=(S(pref + "DB"), 16))
        em("sp", lambda e: e.dma_start(out=tl["Nbc"][:],
                                       in_=d_row.ap()[1, :].partition_broadcast(128)),
           inc=(S(pref + "DB"), 16))
        ndb = S(pref + "DB").n
        for c in range(NEC):
            em("dve", lambda e, c=c: e.tensor_tensor(tl["tmpc"][:], src_chunks[c][:],
                                                     tl["Rbc"][:], ALU.mult),
               waits=[(S(pref + "DB"), ndb)] if c == 0 else ())
            em("dve", lambda e, c=c: e.tensor_tensor(dst_chunks[c][:], tl["tmpc"][:],
                                                     tl["Nbc"][:], ALU.add),
               inc=(S(pref + "Y"), 1) if c == NEC - 1 else None)
        return (S(pref + "Y"), S(pref + "Y").n)

    em("dve", lambda e: e.tensor_scalar(bk8[:], sb_b["bk"][:], 0.125, None, ALU.mult),
       waits=[(S("sLp"), LP)], inc=(S("vBK8"), 1))
    tl1 = {"cb": cb, "sqb": sqb, "selA": selA, "selB": selB, "psStat": psStat,
           "row_m": row_m, "row_v": row_v, "tmpc": tmpc, "Rbc": Rbc, "Nbc": Nbc}
    w_y = emit_ln("L1", X, Y, [(S("sLx"), LX)], [], tl1)

    for c in range(NEC):
        em("dve", lambda e, c=c: e.reduce_sum(sc2[:, c:c + 1], Y[c][:], AX.X),
           waits=[w_y] if c == 0 else ())
    em("dve", lambda e: e.tensor_copy(sc2bf[:], sc2[:]), inc=(S("vYM"), 1))
    for c in range(NEC):
        em("pe", lambda e, c=c: e.matmul(
            psPD[:], wd_sb[:, c * H:(c + 1) * H], sc2bf[:, c:c + 1],
            start=(c == 0), stop=(c == NEC - 1)),
            waits=[(S("vYM"), 1), (S("cP"), 1)] if c == 0 else (),
            inc=(S("pPD"), 1) if c == NEC - 1 else None)
    em("dve", lambda e: e.tensor_copy(ld16[:], psPD[:]), waits=[(S("pPD"), 1)],
       inc=(S("vPD"), 1))
    em("act", lambda e: e.activation(ld16[:], ld16[:], AF.Sigmoid, scale=1.0 / T,
                                     bias=bd_sb[:]),
       waits=[(S("vPD"), 1), (S("sLp"), LP), (S("cP"), 1)])
    em("act", lambda e: e.activation(ld16[:], ld16[:], AF.Exp,
                                     scale=float(-8.0 * np.log(2.0)), bias=cb5[:]))
    em("act", lambda e: e.activation(ld16[:], ld16[:], AF.Ln, scale=-1.0, bias=1.0),
       inc=(S("aLD"), 1))
    em("sp", lambda e: e.dma_start(out=d_ld.ap(), in_=ld16[:]), waits=[(S("aLD"), 1)],
       inc=(S("sLD"), 16))
    em("sp", lambda e: e.dma_start(
        out=ldbc[:],
        in_=d_ld.ap().rearrange("h one -> one h")[0, :].partition_broadcast(128)),
        waits=[(S("sLD"), 16)], inc=(S("sLD"), 16))
    for h in range(H):
        em("act", lambda e, h=h: e.activation(lam[:, h * 16:(h + 1) * 16], it_jp[:],
                                              AF.Exp, scale=ldbc[:, h:h + 1]),
           waits=[(S("sLD"), 32), (S("cP"), 1)] if h == 0 else (),
           inc=(S("aLAM"), 1) if h == H - 1 else None)
    em("dve", lambda e: e.tensor_scalar(ldn16[:], ld16[:], -1.0, None, ALU.mult),
       waits=[(S("aLD"), 1)], inc=(S("vLDN"), 1))
    em("act", lambda e: e.activation(c16t[:], it_t[0:H, :], AF.Exp, scale=ldn16[:]),
       waits=[(S("vLDN"), 1), (S("cP"), 1)], inc=(S("aC16"), 1))
    em("sp", lambda e: e.dma_start(out=d_c16.ap(), in_=c16t[:]), waits=[(S("aC16"), 1)],
       inc=(S("sC16"), 16))
    for h in range(H):
        em("dve", lambda e, h=h: e.tensor_scalar(ldI[:, h * 128:(h + 1) * 128], I128[:],
                                                 ldbc[:, h:h + 1], None, ALU.mult),
           waits=[(S("sLD"), 32), (S("cP"), 1)] if h == 0 else (),
           inc=(S("vLDI"), 1) if h == H - 1 else None)

    ctxL.close()
    ctxP0.close()

    ctxB = ExitStack()
    B = ctxB.enter_context
    OG = [B(SBL(f"OG{c}", [128, T], BF16)) for c in range(NEC)]
    Qf = B(SBL("Qf", [64, 2 * T], BF16))
    Kf = B(SBL("Kf", [64, 2 * T], BF16))
    Vp = B(SBL("Vp", [128, T], BF16))
    vfut = B(SBL("vfut", [128, 16 * 65], BF16))
    vpast = B(SBL("vpast", [128, 16 * 65], BF16))
    wsl = B(SBL("wsl", [128, E], F32))
    wbf = B(SBL("wbf", [128, E], BF16))
    t64a = B(SBL("t64a", [64, T], BF16))
    t64b = B(SBL("t64b", [64, T], BF16))
    scoresb = [B(SBL(f"scoresb{k}", [128, 1024], BF16)) for k in range(2)]
    pfsb = B(SBL("pfsb", [65, 1024], F32))
    ppsb = B(SBL("ppsb", [65, 1024], F32))
    cbc = B(SBL("cbc", [65, 1024], F32))
    rbc = B(SBL("rbc", [128, 1024], BF16))
    mneg = B(SBL("mneg", [128, 2], F32))
    nqk = B(SBL("nqk", [1, 8], F32))
    gnst = B(SBL("gnst", [128, 2], F32))
    gnstb = B(SBL("gnstb", [128, 2], BF16))
    gnfold = B(SBL("gnfold", [128, 65], BF16))
    gn2 = B(SBL("gn2", [65, 4], F32))
    gsc = B(SBL("gsc", [128, 2], F32))
    gate_t = B(SBL("gate_t", [128, T], BF16))
    psQ = B(PSL("psQ", [128, T], F32))
    psPV = B(PSL("psPV", [65, T], F32))

    P0W = [w_y, (S("aC16"), 1), (S("aLAM"), 1), (S("vLDI"), 1), (S("vPD"), 1),
           (S("sC16"), 16)]
    em("pool", lambda e: e.memset(gnfold[:], 0.0), waits=P0W)
    em("pool", lambda e: e.memset(gnfold[0:64, 0:1], 1.0))
    em("pool", lambda e: e.memset(gnfold[64:128, 64:65], 1.0), inc=(S("cGNF"), 1))

    trk = {
        "wsl_free": P0W, "wbf_free": P0W, "psQ_free": P0W, "psPV_free": P0W,
        "QK_free": P0W, "t64_free": P0W, "Vp_free": P0W, "vt_free": P0W,
        "pfpp_free": P0W, "cbc_free": P0W, "rbc_free": P0W, "mneg_free": P0W,
        "qkbuf_free": [P0W, P0W], "sbuf_free": [P0W, P0W],
    }

    def w(*items):
        out = []
        for t in items:
            if t is None:
                continue
            if isinstance(t, list):
                out.extend(x for x in t if x is not None)
            else:
                out.append(t)
        return out

    def stream_weight(dram_col_slice):
        em("pool", lambda e, d=dram_col_slice: e.dma_start(
            out=wbf[:].rearrange("p (c o) -> p c o", o=128),
            in_=d.rearrange("(c p) o -> p c o", p=128)),
            waits=w(trk["wbf_free"]), inc=(S("cWS"), 16), drain=False, is_dma=True)
        return (S("cWS"), S("cWS").n)

    def proj_fm(g_w, rhs_chunks, psem, extra_first=()):
        for ic in range(4):
            for c in range(NEC):
                em("pe", lambda e, c=c, ic=ic: e.matmul(
                    psQ[:, ic * 512:(ic + 1) * 512],
                    wbf[:, c * 128:(c + 1) * 128],
                    rhs_chunks[c][:, ic * 512:(ic + 1) * 512],
                    start=(c == 0), stop=(c == NEC - 1)),
                    waits=(w(g_w, trk["psQ_free"]) + list(extra_first)
                           if (c == 0 and ic == 0) else ()),
                    inc=(S(psem), 1) if (ic == 3 and c == NEC - 1) else None)
        trk["wbf_free"] = (S(psem), S(psem).n)
        return (S(psem), S(psem).n)

    for hp in range(8):
        for which, (wnm, bnm, dst) in enumerate((("wq", "bq", Qf), ("wk", "bk", Kf))):
            g_w = stream_weight(wT[wnm][:, hp * 128:(hp + 1) * 128])
            pQ = proj_fm(g_w, Y, "pQK", extra_first=[w_y] + w(trk["QK_free"]))
            if KDBG and hp == 0 and which == 0:
                em("pool", lambda e: e.dma_start(out=dbg["wbfq0"], in_=wbf[:]),
                   waits=[g_w], inc=(S("sDBP"), 16), drain=False, is_dma=True)
                trk["wbf_free"] = w(trk["wbf_free"], (S("sDBP"), 16))
            bsrc = sb_b[bnm] if which == 0 else bk8
            ksc = 1.0 if which == 0 else 0.125
            em("act", lambda e, d=dst, b=bsrc, sc=ksc, hp=hp: e.activation(
                d[:, 0:T], psQ[0:64, :], AF.Identity, scale=sc, bias=b[0:64, hp:hp + 1]),
                waits=[pQ, (S("sLp"), LP), (S("vBK8"), 1)])
            em("act", lambda e, d=dst, b=bsrc, sc=ksc, hp=hp: e.activation(
                d[:, T:2 * T], psQ[64:128, :], AF.Identity, scale=sc,
                bias=b[64:128, hp:hp + 1]),
                inc=(S("aQK"), 1))
            aQ = (S("aQK"), S("aQK").n)
            trk["psQ_free"] = aQ
            if KDBG and hp == 0 and which == 0:
                em("pool", lambda e: e.dma_start(out=dbg["qpre0"], in_=Qf[:]),
                   waits=[aQ], inc=(S("sDBP2"), 16), drain=False, is_dma=True)
            last_comb = None
            pRs = {}
            for s in range(2):
                rps = psQ if s == 0 else psPV
                for ic in range(4):
                    em("pe", lambda e, s=s, ic=ic, d=dst, rps=rps: e.matmul(
                        rps[0:64, ic * 512:(ic + 1) * 512],
                        Pi64[:], d[:, s * T + ic * 512: s * T + (ic + 1) * 512],
                        start=True, stop=True),
                        waits=(w(aQ, (S("vPI"), 1),
                                 trk["psPV_free"] if s == 1 else None)
                               if ic == 0 else ()),
                        inc=(S("pROT"), 1) if ic == 3 else None)
                pRs[s] = (S("pROT"), S("pROT").n)
            for s in range(2):
                rps = psQ if s == 0 else psPV
                em("act", lambda e, rps=rps: e.activation(t64b[:], rps[0:64, :],
                                                          AF.Copy),
                   waits=w(pRs[s], trk["t64_free"]), inc=(S("aROT"), 1))
                aRot = (S("aROT"), S("aROT").n)
                if s == 0:
                    trk["psQ_free"] = w(aQ, aRot)
                else:
                    trk["psPV_free"] = aRot
                em("dve", lambda e, s=s, d=dst: e.tensor_tensor(
                    t64a[:], d[:, s * T:(s + 1) * T], cos64[:], ALU.mult),
                    waits=w((S("vTRIG"), 2), aRot,
                            (S("sDBP2"), 16) if (KDBG and hp == 0 and which == 0
                                                 and s == 0) else None))
                em("dve", lambda e: e.tensor_tensor(t64b[:], t64b[:], sinS[:], ALU.mult))
                em("dve", lambda e, s=s, d=dst: e.tensor_tensor(
                    d[:, s * T:(s + 1) * T], t64a[:], t64b[:], ALU.add),
                    inc=(S("vCOMB"), 1))
                last_comb = (S("vCOMB"), S("vCOMB").n)
                trk["t64_free"] = last_comb
        v_qk = (S("vCOMB"), S("vCOMB").n)
        if KDBG and hp == 0:
            em("pool", lambda e: e.dma_start(out=dbg["qf0"], in_=Qf[:]),
               waits=[v_qk], inc=(S("sDBQ"), 16), drain=False, is_dma=True)
            em("pool", lambda e: e.dma_start(out=dbg["kf0"], in_=Kf[:]),
               inc=(S("sDBQ"), 16), drain=False, is_dma=True)

        g_v = stream_weight(wT["wv"][:, hp * 128:(hp + 1) * 128])
        psV_free = [trk["psQ_free"]] * 4
        for tb in range(NTB):
            bk = tb % 4
            for c in range(NEC):
                em("pe", lambda e, tb=tb, c=c, bk=bk: e.matmul(
                    psQ[:, bk * 512: bk * 512 + 128],
                    Y[c][:, tb * 128:(tb + 1) * 128],
                    wbf[:, c * 128:(c + 1) * 128],
                    start=(c == 0), stop=False),
                    waits=(w(g_v, psV_free[bk], trk["Vp_free"]) if c == 0 else ()))
            em("pe", lambda e, bk=bk, hp=hp: e.matmul(
                psQ[:, bk * 512: bk * 512 + 128],
                ones_row[0:1, 0:128],
                bv_bf[0:1, hp * 128:(hp + 1) * 128],
                start=False, stop=True),
                waits=[(S("vBV"), 1), (S("vONES"), 1)] if (hp == 0 and tb == 0) else (),
                inc=(S("pV"), 1))
            pV = (S("pV"), S("pV").n)
            em("act", lambda e, tb=tb, bk=bk: e.activation(
                Vp[:, tb * 128:(tb + 1) * 128], psQ[:, bk * 512: bk * 512 + 128], AF.Copy),
                waits=[pV], inc=(S("aV"), 1))
            psV_free[bk] = (S("aV"), S("aV").n)
        a_v = (S("aV"), S("aV").n)
        if KDBG and hp == 0:
            em("pool", lambda e: e.dma_start(out=dbg["vp0"], in_=Vp[:]),
               waits=[a_v], inc=(S("sDBQ"), 16), drain=False, is_dma=True)
        trk["psQ_free"] = a_v
        trk["wbf_free"] = (S("pV"), S("pV").n)

        for w2, src in enumerate((Qf, Kf)):
            em("act", lambda e, s=src: e.activation(t64a[:], s[:, 0:T], AF.Square),
               waits=(w(v_qk, trk["t64_free"]) if w2 == 0
                      else [(S("pNRM"), S("pNRM").n)]))
            em("act", lambda e, s=src: e.activation(t64b[:], s[:, T:2 * T], AF.Square),
               inc=(S("aNSQ"), 1))
            a_sq = (S("aNSQ"), S("aNSQ").n)
            for half, tt in ((0, t64a), (1, t64b)):
                for ic in range(4):
                    em("pe", lambda e, tt=tt, ic=ic: e.matmul(
                        psQ[0:1, ic * 512:(ic + 1) * 512],
                        ones_col[0:64, :], tt[:, ic * 512:(ic + 1) * 512],
                        start=True, stop=True),
                        waits=w(a_sq, trk["psQ_free"]) if ic == 0 else (),
                        inc=(S("pNRM"), 1) if ic == 3 else None)
                pN = (S("pNRM"), S("pNRM").n)
                em("dve", lambda e: e.tensor_copy(pfsb[0:1, :], psQ[0:1, 0:1024]),
                   waits=w(pN, trk["pfpp_free"]))
                em("dve", lambda e: e.tensor_copy(ppsb[0:1, :], psQ[0:1, 1024:2048]))
                em("dve", lambda e, k=w2 * 2 + half: e.reduce_max(
                    nqk[0:1, k:k + 1], pfsb[0:1, :], AX.X))
                em("dve", lambda e, k=w2 * 2 + half: e.reduce_max(
                    nqk[0:1, 4 + k:5 + k], ppsb[0:1, :], AX.X))
                em("dve", lambda e, k=w2 * 2 + half: e.tensor_tensor(
                    nqk[0:1, k:k + 1], nqk[0:1, k:k + 1], nqk[0:1, 4 + k:5 + k], ALU.max),
                    inc=(S("vNRM"), 1))
                trk["psQ_free"] = (S("vNRM"), S("vNRM").n)
                trk["pfpp_free"] = (S("vNRM"), S("vNRM").n)
        em("dve", lambda e: e.tensor_tensor(nqk[0:1, 4:6], nqk[0:1, 0:2], nqk[0:1, 2:4],
                                            ALU.mult), inc=(S("vNQ"), 1))
        em("act", lambda e: e.activation(nqk[0:1, 4:6], nqk[0:1, 4:6], AF.Sqrt),
           waits=[(S("vNQ"), S("vNQ").n)], inc=(S("aNQ"), 1))
        em("dve", lambda e: e.tensor_scalar(nqk[0:1, 4:6], nqk[0:1, 4:6], -1.0,
                                            None, ALU.mult),
           waits=[(S("aNQ"), S("aNQ").n)], inc=(S("vNM"), 1))
        em("pool", lambda e, hp=hp: e.dma_start(out=d_sc.ap()[0:1, 2 * hp:2 * hp + 2],
                                               in_=nqk[0:1, 4:6]),
           waits=[(S("vNM"), S("vNM").n)], inc=(S("sNQ"), 16),
           drain=False, is_dma=True)
        em("pool", lambda e, hp=hp: e.dma_start(
            out=mneg[:], in_=d_sc.ap()[0, 2 * hp:2 * hp + 2].partition_broadcast(128)),
           waits=w((S("sNQ"), S("sNQ").n), trk["mneg_free"]), inc=(S("sNQ"), 16),
           drain=False, is_dma=True)
        ss_mneg = (S("sNQ"), S("sNQ").n)

        vfut3 = vfut[:].rearrange("p (j d) -> p j d", d=65)
        vpast3 = vpast[:].rearrange("p (j d) -> p j d", d=65)
        Vp3 = Vp[:].rearrange("p (j d) -> p j d", d=128)
        for s in range(2):
            h = 2 * hp + s
            lam3 = lam[:, h * 16:(h + 1) * 16].rearrange("p j -> p j ()")
            em("dve", lambda e, s=s, lam3=lam3: e.tensor_tensor(
                vfut3[:, :, 0:64], Vp3[:, :, s * 64:(s + 1) * 64],
                lam3.broadcast_to((128, 16, 64)), ALU.mult),
                waits=w(a_v, (S("aLAM"), 1), trk["vt_free"]))
            em("dve", lambda e, lam3=lam3: e.tensor_copy(vfut3[:, :, 64:65], lam3))
            em("dve", lambda e, s=s: e.tensor_copy(
                vpast3[:, :, 0:64], Vp3[:, :, s * 64:(s + 1) * 64]))
            em("dve", lambda e: e.memset(vpast3[:, :, 64:65], 1.0), inc=(S("vVT"), 1))
            v_vt = (S("vVT"), S("vVT").n)
            trk["Vp_free"] = (w(v_vt, (S("sDBQ"), 48))
                              if (KDBG and hp == 0) else v_vt)

            for ic2 in range(2):
                def _wf(jb):
                    return min(max(128 * jb - 1024 * ic2, 0), 1024)
                segs_by_jb = {}
                bank_events = {0: [], 1: [], 2: [], 3: []}
                for jb in range(16):
                    wf = _wf(jb)
                    wfp = _wf(jb - 1) if jb > 0 else 0
                    segs = []
                    cuts = sorted({0, wf, min(wfp, wf), 512, 1024} & set(range(0, wf + 1))
                                  | {0, min(wfp, wf), wf}
                                  | {c for c in (512, 1024) if c < wf})
                    cuts = sorted(set([0, min(wfp, wf), wf] +
                                      [c for c in (512, 1024) if 0 < c < wf]))
                    for a2, b2 in zip(cuts, cuts[1:]):
                        if b2 > a2:
                            segs.append(("f", 1024 + a2, 1024 + b2))
                    a2 = wf
                    while a2 < 1024:
                        b2 = min((a2 // 512 + 1) * 512, 1024)
                        segs.append(("p", a2, b2))
                        a2 = b2
                    segs_by_jb[jb] = segs
                    for k, (kind, ca, cb2) in enumerate(segs):
                        for bk in range(ca // 512, (cb2 - 1) // 512 + 1):
                            bank_events[bk].append((jb, k))
                first_ev = {bk: (ev[0] if ev else None) for bk, ev in bank_events.items()}
                last_ev = {bk: (ev[-1] if ev else None) for bk, ev in bank_events.items()}
                wfmax = _wf(15)
                if wfmax < 1024:
                    a2 = wfmax
                    while a2 < 1024:
                        b2 = min((a2 // 512 + 1) * 512, 1024)
                        em("pe", lambda e, a=a2, b=b2: e.matmul(
                            psPV[:, 1024 + a:1024 + b], z65[:], ones_row[0:1, 0:(b - a)],
                            start=True, stop=True),
                            waits=w(trk["psPV_free"]))
                        a2 = b2
                exp_done = {}
                pv_done = {}
                evac_done = {}

                def emit_qk(jb):
                    off = 128 * jb - 1024 * ic2
                    wf = min(max(off, 0), 1024)
                    buf = jb % 2
                    bs_a, bs_b = wf, min(off + 128, 1024)
                    has_bias = bs_b > bs_a
                    if jb >= 2:
                        wqk = [evac_done[jb - 2]]
                    else:
                        wqk = w(trk["qkbuf_free"][buf], trk["psQ_free"],
                                trk["QK_free"])
                    segh = {0: has_bias and min(bs_b, 512) > max(bs_a, 0),
                            1: has_bias and min(bs_b, 1024) > max(bs_a, 512)}
                    for half in range(2):
                        ha, hb = half * 512, (half + 1) * 512
                        seg_a, seg_b = max(bs_a, ha), min(bs_b, hb)
                        has_seg = segh[half]
                        is_last_qk = (half == 1 and not has_seg)
                        em("pe", lambda e, jb=jb, half=half, buf=buf, s=s, ic2=ic2,
                           st=not has_seg: e.matmul(
                            psQ[:, buf * 1024 + half * 512: buf * 1024 + (half + 1) * 512],
                            Kf[:, s * T + jb * 128: s * T + (jb + 1) * 128],
                            Qf[:, s * T + ic2 * 1024 + half * 512:
                               s * T + ic2 * 1024 + (half + 1) * 512],
                            start=True, stop=st),
                            waits=wqk if half == 0 else (),
                            inc=(S(f"qk{buf}"), 1) if is_last_qk else None)
                        wqk[:] = []
                        if has_seg:
                            u0 = seg_a - off
                            em("pe", lambda e, buf=buf, h=h, a=seg_a, b=seg_b, u0=u0:
                               e.matmul(
                                psQ[:, buf * 1024 + a: buf * 1024 + b],
                                ldI[:, h * 128:(h + 1) * 128],
                                Drelu0[:, u0:u0 + (b - a)],
                                start=False, stop=True),
                                waits=[(S("vLDI"), 1), (S("cDRELU"), 1)]
                                if (hp == 0 and s == 0 and ic2 == 0) else (),
                                inc=(S(f"qk{buf}"), 1) if half == 1 else None)
                    pQK = (S(f"qk{buf}"), S(f"qk{buf}").n)
                    wev = w(pQK, trk["sbuf_free"][buf])
                    em("act", lambda e, buf=buf: e.activation(
                        scoresb[buf][:], psQ[:, buf * 1024: buf * 1024 + 1024],
                        AF.Copy),
                        waits=wev, inc=(S(f"ev{buf}"), 1))
                    evac_done[jb] = (S(f"ev{buf}"), S(f"ev{buf}").n)
                    trk["qkbuf_free"][buf] = evac_done[jb]
                    em("act", lambda e, buf=buf, s=s: e.activation(
                        scoresb[buf][:], scoresb[buf][:], AF.Exp, bias=mneg[:, s:s + 1]),
                        waits=w(ss_mneg if (s == 0 and ic2 == 0 and jb == 0)
                                else None),
                        inc=(S(f"ex{buf}"), 1))
                    exp_done[jb] = (S(f"ex{buf}"), S(f"ex{buf}").n)

                def emit_pv(jb):
                    buf = jb % 2
                    first_mm = True
                    for k, (kind, ca, cb2) in enumerate(segs_by_jb[jb]):
                        start = first_ev[ca // 512] == (jb, k)
                        stop = last_ev[ca // 512] == (jb, k)
                        wpv = []
                        if first_mm:
                            wpv = [exp_done[jb]]
                            if jb == 0:
                                wpv += w(trk["psPV_free"], v_vt)
                        vt = vfut3 if kind == "f" else vpast3
                        ea = ca - 1024 if kind == "f" else ca
                        eb = cb2 - 1024 if kind == "f" else cb2
                        em("pe", lambda e, jb=jb, buf=buf, ca=ca, cb2=cb2, ea=ea, eb=eb,
                           vt=vt, start=start, stop=stop: e.matmul(
                            psPV[:, ca:cb2], vt[:, jb, :], scoresb[buf][:, ea:eb],
                            start=start, stop=stop), waits=wpv,
                            inc=(S(f"pv{buf}"), 1) if k == len(segs_by_jb[jb]) - 1 else None)
                        first_mm = False
                    pv_done[jb] = (S(f"pv{buf}"), S(f"pv{buf}").n)
                    trk["sbuf_free"][buf] = pv_done[jb]

                for jb in range(17):
                    if jb < 16:
                        emit_qk(jb)
                    if jb >= 1:
                        emit_pv(jb - 1)
                em("act", lambda e: e.activation(ppsb[:], psPV[:, 0:1024], AF.Copy),
                   waits=w(pv_done[15], pv_done[14], trk["pfpp_free"]))
                em("act", lambda e: e.activation(pfsb[:], psPV[:, 1024:2048], AF.Copy),
                   inc=(S("aPP"), 1))
                aPV = (S("aPP"), S("aPP").n)
                trk["psPV_free"] = aPV
                em("pool", lambda e, h=h, ic2=ic2: e.dma_start(
                    out=cbc[:],
                    in_=d_c16.ap()[h, ic2 * 1024:(ic2 + 1) * 1024].partition_broadcast(65)),
                    waits=w((S("sC16"), 16), trk["cbc_free"]), inc=(S("sCB"), 16),
                    drain=False, is_dma=True)
                ss_cbc = (S("sCB"), S("sCB").n)
                em("dve", lambda e: e.tensor_tensor(pfsb[64:65, :], cbc[64:65, :],
                                                    pfsb[64:65, :], ALU.mult),
                   waits=w(aPV, ss_cbc))
                em("dve", lambda e: e.tensor_tensor(ppsb[64:65, :], pfsb[64:65, :],
                                                    ppsb[64:65, :], ALU.add))
                em("dve", lambda e: e.reciprocal(pfsb[64:65, :], ppsb[64:65, :]))
                em("dve", lambda e: e.tensor_tensor(cbc[0:64, :], cbc[0:64, :],
                                                    pfsb[0:64, :], ALU.mult))
                em("dve", lambda e, hp=hp, s=s, ic2=ic2: e.tensor_tensor(
                    OG[hp][s * 64:(s + 1) * 64, ic2 * 1024:(ic2 + 1) * 1024],
                    cbc[0:64, :], ppsb[0:64, :], ALU.add), inc=(S("vEPI"), 1))
                vDen = (S("vEPI"), S("vEPI").n)
                trk["cbc_free"] = vDen
                em("pool", lambda e, h=h, ic2=ic2: e.dma_start(
                    out=d_r16.ap()[h:h + 1, ic2 * 1024:(ic2 + 1) * 1024],
                    in_=pfsb[64:65, :]), waits=[vDen], inc=(S("sR16"), 16),
                    drain=False, is_dma=True)
                ss_r = (S("sR16"), S("sR16").n)
                trk["pfpp_free"] = w(vDen, ss_r)
                em("pool", lambda e, h=h, ic2=ic2: e.dma_start(
                    out=rbc[:],
                    in_=d_r16.ap()[h, ic2 * 1024:(ic2 + 1) * 1024].partition_broadcast(128)),
                    waits=w(ss_r, trk["rbc_free"]), inc=(S("gRBC"), 16), drain=False,
                    is_dma=True)
                ss_rbc2 = (S("gRBC"), S("gRBC").n)
                em("dve", lambda e, hp=hp, s=s, ic2=ic2: e.tensor_tensor(
                    OG[hp][s * 64:(s + 1) * 64, ic2 * 1024:(ic2 + 1) * 1024],
                    OG[hp][s * 64:(s + 1) * 64, ic2 * 1024:(ic2 + 1) * 1024],
                    rbc[s * 64:(s + 1) * 64, :], ALU.mult),
                    waits=[ss_rbc2], inc=(S("vOG"), 1))
                trk["rbc_free"] = (S("vOG"), S("vOG").n)
            trk["vt_free"] = pv_done[15]
        trk["QK_free"] = (w(pv_done[15], (S("sDBQ"), 48))
                          if (KDBG and hp == 0) else pv_done[15])
        trk["mneg_free"] = exp_done[15]
        trk["psQ_free"] = [evac_done[14], evac_done[15]]

    og_done = (S("vOG"), S("vOG").n)
    if KDBG:
        em("pool", lambda e: e.dma_start(out=dbg["og0p"], in_=OG[0][:]),
           waits=[og_done], inc=(S("sDBG"), 16), drain=False, is_dma=True)
    for hp in range(8):
        g_g = stream_weight(wT["wg"][:, hp * 128:(hp + 1) * 128])
        pG = proj_fm(g_g, Y, "pG")
        em("act", lambda e, hp=hp: e.activation(gate_t[:], psQ[:], AF.Silu,
                                                bias=sb_b["bg"][:, hp:hp + 1]),
           waits=w(pG, trk["gate_free"] if "gate_free" in trk else None),
           inc=(S("aG"), 1))
        a_g = (S("aG"), S("aG").n)
        trk["psQ_free"] = a_g
        em("dve", lambda e, hp=hp: e.reduce_sum(gnst[:, 0:1], OG[hp][:], AX.X),
           waits=w(og_done if hp == 0 else None))
        em("act", lambda e, hp=hp: e.activation(t64a[:], OG[hp][0:64, :], AF.Square),
           waits=w(og_done if hp == 0 else None, trk["t64_free"]))
        em("act", lambda e, hp=hp: e.activation(t64b[:], OG[hp][64:128, :], AF.Square),
           inc=(S("aGSQ"), 1))
        aGN = (S("aGSQ"), S("aGSQ").n)
        em("dve", lambda e: e.reduce_sum(gnst[0:64, 1:2], t64a[:], AX.X), waits=[aGN])
        em("dve", lambda e: e.reduce_sum(gnst[64:128, 1:2], t64b[:], AX.X))
        em("dve", lambda e: e.tensor_copy(gnstb[:], gnst[:]), inc=(S("vGST"), 1))
        vGN = (S("vGST"), S("vGST").n)
        trk["t64_free"] = vGN
        em("pe", lambda e: e.matmul(psPV[:, 0:2], gnfold[:], gnstb[:],
                                    start=True, stop=True),
           waits=w(vGN, (S("cGNF"), 1), trk["psPV_free"]), inc=(S("pGN"), 1))
        em("dve", lambda e: e.tensor_copy(gn2[:, 0:2], psPV[:, 0:2]),
           waits=[(S("pGN"), S("pGN").n)], inc=(S("vGN2"), 1))
        trk["psPV_free"] = (S("vGN2"), S("vGN2").n)
        NL = float(T * HD)
        em("dve", lambda e: e.tensor_scalar(gn2[:, 0:2], gn2[:, 0:2], 1.0 / NL, None,
                                            ALU.mult))
        em("dve", lambda e: e.tensor_tensor(gn2[:, 2:3], gn2[:, 0:1], gn2[:, 0:1],
                                            ALU.mult))
        em("dve", lambda e: e.tensor_tensor(gn2[:, 3:4], gn2[:, 1:2], gn2[:, 2:3],
                                            ALU.subtract), inc=(S("vGVAR"), 1))
        em("act", lambda e: e.activation(gn2[:, 3:4], gn2[:, 3:4], AF.Sqrt, bias=ceps[:]),
           waits=[(S("vGVAR"), S("vGVAR").n), (S("cP"), 1)], inc=(S("aGSD"), 1))
        em("dve", lambda e: e.reciprocal(gn2[:, 3:4], gn2[:, 3:4]),
           waits=[(S("aGSD"), S("aGSD").n)])
        em("dve", lambda e: e.tensor_copy(gn2[:, 2:3], gn2[:, 0:1]), inc=(S("vGR"), 1))
        vRstd = (S("vGR"), S("vGR").n)
        em("pool", lambda e: e.dma_start(out=d_sc.ap()[0:1, 4:6], in_=gn2[0:1, 2:4]),
           waits=[vRstd], inc=(S("sGN"), 16), drain=False, is_dma=True)
        em("pool", lambda e: e.dma_start(out=d_sc.ap()[0:1, 6:8], in_=gn2[64:65, 2:4]),
           inc=(S("sGN"), 16), drain=False, is_dma=True)
        em("pool", lambda e: e.dma_start(
            out=gsc[0:64, 0:2], in_=d_sc.ap()[0, 4:6].partition_broadcast(64)),
           waits=[(S("sGN"), S("sGN").n)], inc=(S("sGN2"), 16),
           drain=False, is_dma=True)
        em("pool", lambda e: e.dma_start(
            out=gsc[64:128, 0:2], in_=d_sc.ap()[0, 6:8].partition_broadcast(64)),
           inc=(S("sGN2"), 16), drain=False, is_dma=True)
        em("dve", lambda e, hp=hp: e.tensor_scalar(gsc[:, 1:2],
                                                   sb_b["gamma"][:, hp:hp + 1],
                                                   gsc[:, 1:2], None, ALU.mult),
           waits=[(S("sGN2"), S("sGN2").n)])
        em("dve", lambda e: e.tensor_tensor(gsc[:, 0:1], gsc[:, 0:1], gsc[:, 1:2],
                                            ALU.mult))
        em("dve", lambda e, hp=hp: e.scalar_tensor_tensor(
            gsc[:, 0:1], gsc[:, 0:1], -1.0, sb_b["beta"][:, hp:hp + 1], ALU.mult, ALU.add),
            inc=(S("vGSC"), 1))
        em("dve", lambda e, hp=hp: e.tensor_scalar(OG[hp][:], OG[hp][:], gsc[:, 1:2],
                                                   gsc[:, 0:1], ALU.mult, ALU.add),
           waits=[(S("vGSC"), S("vGSC").n)] +
                 ([(S("sDBG"), 16)] if (KDBG and hp == 0) else []))
        em("dve", lambda e, hp=hp: e.tensor_tensor(OG[hp][:], OG[hp][:], gate_t[:],
                                                   ALU.mult), waits=[a_g],
           inc=(S("vAPP"), 1))
        trk["gate_free"] = (S("vAPP"), S("vAPP").n)
    v_att = (S("vAPP"), S("vAPP").n)

    vO = [None, None, None, None]
    for oc in range(NEC):
        g_o = stream_weight(wT["wo"][:, oc * 128:(oc + 1) * 128])
        for ic in range(4):
            for c in range(NEC):
                em("pe", lambda e, c=c, ic=ic: e.matmul(
                    psQ[:, ic * 512:(ic + 1) * 512],
                    wbf[:, c * 128:(c + 1) * 128],
                    OG[c][:, ic * 512:(ic + 1) * 512],
                    start=(c == 0), stop=(c == NEC - 1)),
                    waits=(w(g_o, v_att if oc == 0 else None,
                             trk["psQ_free"] if oc == 0 and ic == 0 else None,
                             vO[ic] if oc > 0 else None) if c == 0 else ()),
                    inc=(S(f"pO{ic}"), 1) if c == NEC - 1 else None)
            pO = (S(f"pO{ic}"), S(f"pO{ic}").n)
            em("dve", lambda e, oc=oc, ic=ic: e.scalar_tensor_tensor(
                X[oc][:, ic * 512:(ic + 1) * 512], psQ[:, ic * 512:(ic + 1) * 512],
                sb_b["bo"][0:128, oc:oc + 1], X[oc][:, ic * 512:(ic + 1) * 512],
                ALU.add, ALU.add), waits=[pO], inc=(S(f"vO{ic}"), 1))
            vO[ic] = (S(f"vO{ic}"), S(f"vO{ic}").n)
        trk["wbf_free"] = (S("pO3"), S("pO3").n)

    if KDBG:
        ddone = w(v_att, (S("vOG"), S("vOG").n))
        for nm, tile in (("y0", Y[0][:]), ("ld", ld16[:]), ("qf", Qf[:]), ("kf", Kf[:]),
                         ("vp", Vp[:]), ("og0", OG[0][:]), ("mneg", mneg[:]),
                         ("cos", cos64[:]), ("sin", sinS[:]), ("gate", gate_t[:]),
                         ("pf", pfsb[:]), ("pp", ppsb[:])) + tuple(
                             (f"yc{c}", Y[c][:]) for c in range(8)):
            em("pool", lambda e, nm=nm, tile=tile: e.dma_start(out=dbg[nm], in_=tile),
               waits=ddone, inc=(S("sDBG"), 16), drain=False, is_dma=True)
            ddone = ()

    if KSTOP == "attn":
        for c in range(NEC):
            em("sp", lambda e, c=c: e.dma_start(out=yout[c * 128:(c + 1) * 128, :],
                                                in_=X[c][:]),
               waits=[vO[0], vO[1], vO[2], vO[3]] if c == 0 else (), inc=(S("sO"), 16))
        em("sp", lambda e: e.nop(), waits=[(S("sO"), S("sO").n)] +
           ([(S("sDBG"), S("sDBG").n), (S("sDBQ"), S("sDBQ").n),
             (S("sDBP"), S("sDBP").n), (S("sDBP2"), S("sDBP2").n)] if KDBG else []),
           drain=False)
    else:
        ctxB.close()
        ctxFP = ExitStack()
        FP = ctxFP.enter_context
        psF1 = FP(PSL("psF1", [128, T], F32))
        psF2 = FP(PSL("psF2", [128, T], F32))
        ctxF2 = ExitStack()
        F2 = ctxF2.enter_context
        iA = F2(SBR("iA", [128, T], I32))
        iA2 = F2(SBR("iA2", [128, T], I32))
        iA3 = F2(SBR("iA3", [128, T], I32))
        scb0 = F2(SBR("scb0", [128, 2], F32))
        scb1 = F2(SBR("scb1", [128, 4], F32))
        scb2 = F2(SBR("scb2", [128, 2], F32))
        scb3 = F2(SBR("scb3", [128, 2], F32))
        mx0 = F2(SBR("mx0", [128, 1], F32))
        mxu = F2(SBR("mxu", [128, 1], F32))
        mxg = F2(SBR("mxg", [128, 1], F32))
        mx2 = F2(SBR("mx2", [128, 1], F32))
        mx3 = F2(SBR("mx3", [128, 1], F32))
        tmpr = F2(SBR("tmpr", [128, 1], F32))
        mrow = F2(SBR("mrow", [1, 128], F32))
        sscl = F2(SBR("sscl", [1, 20], F32))
        XW = w(vO[0], vO[1], vO[2], vO[3],
               (S("sDBG"), S("sDBG").n) if KDBG else None,
               (S("sDBQ"), S("sDBQ").n) if KDBG else None)

        for c in range(NEC):
            em("sp", lambda e, c=c: e.dma_start(
                out=d_x1.ap()[c * 128:(c + 1) * 128, :], in_=X[c][:]),
                waits=XW if c == 0 else (), inc=(S("sLx"), 16))
        sX1 = (S("sLx"), S("sLx").n)
        for m in (mx0, mxu, mxg, mx2, mx3):
            em("pool", lambda e, m=m: e.memset(m[:], 0.0),
               waits=XW if m is mx0 else ())
        em("pool", lambda e: e.nop(), inc=(S("cPM"), 1), drain=True)
        mZ = (S("cPM"), S("cPM").n)

        ctxDQ = ExitStack()
        DQ = ctxDQ.enter_context
        wtF = DQ(SBR("wtF", [128, FF], F32))
        iW = DQ(SBR("iW", [128, FF], I32))
        wqF = DQ(SBR("wqF", [128, FF], F32))
        wqT = DQ(SBR("wqTs", [128, FF], BF16))
        s4t = DQ(SBR("s4t", [128, 96], F32))
        rin4 = DQ(SBR("rin4", [128, 96], F32))
        sI = DQ(SBR("sI", [128, 128], F32))
        em("pool", lambda e: e.memset(sI[:], 1.0), waits=XW)
        em("pool", lambda e: e.affine_select(sI[:], sI[:], pattern=[[1, 128]],
                                             compare_op=ALU.is_equal, fill=0.0,
                                             base=0, channel_multiplier=-1),
           inc=(S("cIOTA"), 1))
        nID = S("cIOTA").n
        prev_wq = None
        prev_tr = None

        def dequant_chunk(src2d, ncols, tcol_of_j, first):
            nonlocal prev_wq, prev_tr
            global_es = {}
            nj = ncols // 128
            nb = ncols // 32
            em("sp", lambda e, src2d=src2d, ncols=ncols: e.dma_start(
                out=wtF[:, 0:ncols], in_=src2d),
               waits=w(XW if first else None, prev_wq, prev_tr),
               inc=(S("sLw"), 16))
            em("dve", lambda e, ncols=ncols, nb=nb: e.tensor_reduce(
                s4t[:, 0:nb].rearrange("p (b one) -> p b one", one=1),
                wtF[:, 0:ncols].rearrange("p (b k) -> p b k", k=32),
                AX.X, ALU.max, apply_absolute_value=True),
               waits=w((S("sLw"), S("sLw").n), prev_tr), inc=(S("vNRM"), 1))
            em("dve", lambda e, nb=nb: e.tensor_scalar(
                s4t[:, 0:nb], s4t[:, 0:nb], 1.0 / 7.0, None, ALU.mult))
            em("dve", lambda e, nb=nb: e.tensor_scalar(
                s4t[:, 0:nb], s4t[:, 0:nb], 1e-12, None, ALU.max))
            em("dve", lambda e, nb=nb: e.reciprocal(rin4[:, 0:nb], s4t[:, 0:nb]))
            em("dve", lambda e, ncols=ncols, nb=nb: e.tensor_tensor(
                iW[:, 0:ncols].rearrange("p (b k) -> p b k", k=32),
                wtF[:, 0:ncols].rearrange("p (b k) -> p b k", k=32),
                rin4[:, 0:nb].rearrange("p (b one) -> p b one", one=1)
                .broadcast_to((128, nb, 32)),
                ALU.mult))
            em("dve", lambda e, ncols=ncols, nb=nb: e.tensor_tensor(
                wqF[:, 0:ncols].rearrange("p (b k) -> p b k", k=32),
                iW[:, 0:ncols].rearrange("p (b k) -> p b k", k=32),
                s4t[:, 0:nb].rearrange("p (b one) -> p b one", one=1)
                .broadcast_to((128, nb, 32)),
                ALU.mult),
               inc=(S("vCOMB"), 1))
            prev_wq = (S("vCOMB"), S("vCOMB").n)
            for j in range(nj):
                pslice = (psF1[:, 0:128] if j % 2 == 0 else psF1[:, 512:640])
                em("pe", lambda e, j=j, ps=pslice: e.transpose(
                    ps, wqF[:, j * 128:(j + 1) * 128], sI[:]),
                    waits=w(prev_wq if j == 0 else None,
                            XW if (first and j <= 1) else None,
                            (S("cIOTA"), nID) if (first and j == 0) else None,
                            global_es.get(j - 2)),
                    inc=(S("pROT"), 1))
                em("act", lambda e, j=j, ps=pslice, tc=tcol_of_j(j): e.activation(
                    wqT[:, tc:tc + 128], ps, AF.Copy),
                    waits=[(S("pROT"), S("pROT").n)], inc=(S("aROT"), 1))
                global_es[j] = (S("aROT"), S("aROT").n)
            prev_tr = (S("aROT"), S("aROT").n)

        gW = {}
        for nm, src, ncols, nch in (("up", wupNs, E, 3), ("ga", wgaNs, E, 3),
                                    ("do", wdoNs, FF, 1)):
            for ch in range(nch):
                tcol = (lambda j, ch=ch: j * 384 + ch * 128) if nch == 3 else \
                       (lambda j: j * 128)
                dequant_chunk(src[ch * 128:(ch + 1) * 128, :], ncols, tcol,
                              prev_wq is None)
            em("sp", lambda e, nm=nm: e.dma_start(
                out=g_w_in[nm].ap()[0:1, :].rearrange("one (p o) -> p o", p=128),
                in_=wqT[:]),
               waits=[(S("aROT"), S("aROT").n)], inc=(S("sCB"), 16))
            em("pool", lambda e, nm=nm: e.collective_compute(
                "AllGather", ALU.bypass, replica_groups=[list(range(8))],
                ins=[g_w_in[nm].ap().opt()], outs=[g_w_out[nm].ap().opt()]),
                waits=[(S("sCB"), S("sCB").n)], inc=(S("gRBC"), 1), drain=False,
                is_dma=True)
            gW[nm] = S("gRBC").n
            prev_tr = w(prev_tr, (S("sCB"), S("sCB").n))
        ctxDQ.close()

        ctxLN = ExitStack()
        LN = ctxLN.enter_context
        tl2 = {"cb": [LN(SBR(f"fcb{k}", [128, T], BF16)) for k in range(2)],
               "sqb": [LN(SBR(f"fsq{k}", [128, T], BF16)) for k in range(2)],
               "selA": LN(SBR("fselA", [128, 65], BF16)),
               "selB": LN(SBR("fselB", [128, 65], BF16)),
               "Rbc": LN(SBR("fRbc", [128, T], F32)),
               "Nbc": LN(SBR("fNbc", [128, T], F32)),
               "tmpc": LN(SBR("ftmpc", [128, T], F32)),
               "row_m": LN(SBR("frm", [1, T], F32)),
               "row_v": LN(SBR("frv", [1, T], F32)),
               "psStat": psF2[0:65, :]}
        w_y2 = emit_ln("L1", X, Y, XW, XW, tl2)

        def emit_barrier(k, mxs, wait_extra):
            n = len(mxs)
            for i, mx in enumerate(mxs):
                em("sp", lambda e, k=k, i=i, mx=mx: e.dma_start(
                    out=d_red.ap()[2 * k + i:2 * k + i + 1, :], in_=mx[:]),
                    waits=wait_extra if i == 0 else (), inc=(S("sGN"), 16))
            sA = (S("sGN"), S("sGN").n)
            for i in range(n):
                em("sp", lambda e, k=k, i=i: e.dma_start(
                    out=mrow[:], in_=d_red.ap()[2 * k + i:2 * k + i + 1, :]),
                    waits=w(sA if i == 0 else None,
                            (S("vGR"), S("vGR").n) if i > 0 else None),
                    inc=(S("sGN"), 16))
                em("dve", lambda e, i=i: e.reduce_max(sscl[0:1, i:i + 1], mrow[:],
                                                      AX.X),
                   waits=[(S("sGN"), S("sGN").n)], inc=(S("vGR"), 1))
            em("sp", lambda e, k=k, n=n: e.dma_start(out=cc_in[k].ap()[0:1, 0:n],
                                                     in_=sscl[0:1, 0:n]),
               waits=[(S("vGR"), S("vGR").n)], inc=(S("sGN"), 16))
            em("pool", lambda e, k=k: e.collective_compute(
                "AllReduce", ALU.max, replica_groups=[list(range(8))],
                ins=[cc_in[k].ap().opt()], outs=[cc_out[k].ap().opt()]),
                waits=[(S("sGN"), S("sGN").n)], inc=(S("gRBC"), 1), drain=False,
                is_dma=True)
            em("sp", lambda e, k=k, n=n: e.dma_start(out=sscl[0:1, 4:4 + n],
                                                     in_=cc_out[k].ap()[0:1, 0:n]),
               waits=[(S("gRBC"), S("gRBC").n)], inc=(S("sGN"), 16))
            return (S("sGN"), S("sGN").n)

        def bscal(fn, waits=(), inc=None):
            em("dve", fn, waits=waits, inc=inc)

        def brd(k, src_lanes, dst, n):
            for ci, ln in enumerate(src_lanes):
                em("sp", lambda e, k=k, ci=ci, ln=ln: e.dma_start(
                    out=d_sc2.ap()[k:k + 1, ci:ci + 1], in_=sscl[0:1, ln:ln + 1]),
                    waits=[(S("vGR"), S("vGR").n)] if ci == 0 else (),
                    inc=(S("sGN2"), 16))
            em("sp", lambda e, k=k, n=n, dst=dst: e.dma_start(
                out=dst[:], in_=d_sc2.ap()[k, 0:n].partition_broadcast(128)),
               waits=[(S("sGN2"), S("sGN2").n)], inc=(S("sGN2"), 16))
            return (S("sGN2"), S("sGN2").n)

        for c in range(NEC):
            em("dve", lambda e, c=c: e.reduce_max(tmpr[:], Y[c][:], AX.X,
                                                  apply_absolute_value=True),
               waits=w(w_y2, mZ) if c == 0 else ())
            em("dve", lambda e: e.tensor_tensor(mx0[:], mx0[:], tmpr[:], ALU.max),
               inc=(S("vYM"), 1) if c == NEC - 1 else None)
        bB0 = emit_barrier(0, [mx0], [(S("vYM"), S("vYM").n)])
        bscal(lambda e: e.tensor_scalar(sscl[0:1, 8:9], sscl[0:1, 4:5],
                                        1.0 / 32767.0, None, ALU.mult),
              waits=[bB0])
        bscal(lambda e: e.tensor_scalar(sscl[0:1, 8:9], sscl[0:1, 8:9], 1e-12,
                                        None, ALU.max))
        bscal(lambda e: e.reciprocal(sscl[0:1, 9:10], sscl[0:1, 8:9]),
              inc=(S("vGR"), 1))
        s0done = brd(0, [8, 9], scb0, 2)
        for c in range(NEC):
            em("dve", lambda e, c=c: e.tensor_scalar(iA[:], Y[c][:], scb0[:, 1:2],
                                                     None, ALU.mult),
               waits=[s0done] if c == 0 else ())
            em("dve", lambda e, c=c: e.tensor_scalar(Y[c][:], iA[:], scb0[:, 0:1],
                                                     None, ALU.mult),
               inc=(S("vYM"), 1))
        xqDone = (S("vYM"), S("vYM").n)

        ctxLN.close()
        ctxS = ExitStack()
        FS_ = ctxS.enter_context
        wtu = [FS_(SBR(f"wtu{k}", [128, E], BF16)) for k in range(2)]
        wtg = [FS_(SBR(f"wtg{k}", [128, E], BF16)) for k in range(2)]
        upsg = [FS_(SBR(f"upsg{k}", [128, T], BF16)) for k in range(2)]
        gasg = [FS_(SBR(f"gasg{k}", [128, T], BF16)) for k in range(2)]
        gup3 = g_w_out["up"].ap().rearrange("c (p ec o) -> c p ec o", p=128, ec=8)
        gga3 = g_w_out["ga"].ap().rearrange("c (p ec o) -> c p ec o", p=128, ec=8)
        NOB = FF // 128
        upmm, gamm, upev, gaev, updm, gadm = {}, {}, {}, {}, {}, {}
        upmx, gamx = {}, {}
        for ob in range(NOB):
            sl = ob % 2
            em("sp", lambda e, ob=ob, sl=sl: e.dma_start(
                out=wtu[sl][:].rearrange("p (c o) -> p c o", o=128),
                in_=gup3[ob // 3, :, :, (ob % 3) * 128:(ob % 3) * 128 + 128]),
                waits=w((S("gRBC"), gW["up"]) if ob == 0 else None, upmm.get(ob - 2)),
                inc=(S("ex0" if sl == 0 else "ex1"), 16))
            wU = (S("ex0" if sl == 0 else "ex1"), S("ex0" if sl == 0 else "ex1").n)
            em("sp", lambda e, ob=ob, sl=sl: e.dma_start(
                out=wtg[sl][:].rearrange("p (c o) -> p c o", o=128),
                in_=gga3[ob // 3, :, :, (ob % 3) * 128:(ob % 3) * 128 + 128]),
                waits=w((S("gRBC"), gW["ga"]) if ob == 0 else None, gamm.get(ob - 2)),
                inc=(S("pv0" if sl == 0 else "pv1"), 16))
            wG = (S("pv0" if sl == 0 else "pv1"), S("pv0" if sl == 0 else "pv1").n)
            for ic in range(4):
                for c in range(NEC):
                    em("pe", lambda e, sl=sl, ic=ic, c=c: e.matmul(
                        psF1[:, ic * 512:(ic + 1) * 512],
                        wtu[sl][:, c * 128:(c + 1) * 128],
                        Y[c][:, ic * 512:(ic + 1) * 512],
                        start=(c == 0), stop=(c == NEC - 1)),
                        waits=(w(wU, xqDone if ob == 0 else None,
                                 (S("aROT"), S("aROT").n) if ob == 0 else None,
                                 upev.get(ob - 1))
                               if (ic == 0 and c == 0) else ()),
                        inc=(S("pQK"), 1) if (ic == 3 and c == NEC - 1) else None)
            upmm[ob] = (S("pQK"), S("pQK").n)
            for ic in range(4):
                for c in range(NEC):
                    em("pe", lambda e, sl=sl, ic=ic, c=c: e.matmul(
                        psF2[:, ic * 512:(ic + 1) * 512],
                        wtg[sl][:, c * 128:(c + 1) * 128],
                        Y[c][:, ic * 512:(ic + 1) * 512],
                        start=(c == 0), stop=(c == NEC - 1)),
                        waits=(w(wG, xqDone if ob == 0 else None,
                                 (S("L1ROWS"), S("L1ROWS").n) if ob == 0 else None,
                                 gaev.get(ob - 1))
                               if (ic == 0 and c == 0) else ()),
                        inc=(S("pG"), 1) if (ic == 3 and c == NEC - 1) else None)
            gamm[ob] = (S("pG"), S("pG").n)
            em("act", lambda e, sl=sl: e.activation(upsg[sl][:], psF1[:], AF.Copy),
               waits=w(upmm[ob], updm.get(ob - 2), upmx.get(ob - 2)),
               inc=(S("aQK"), 1))
            upev[ob] = (S("aQK"), S("aQK").n)
            em("act", lambda e, sl=sl: e.activation(gasg[sl][:], psF2[:], AF.Copy),
               waits=w(gamm[ob], gadm.get(ob - 2), gamx.get(ob - 2)),
               inc=(S("aPP"), 1))
            gaev[ob] = (S("aPP"), S("aPP").n)
            em("act", lambda e, ob=ob, sl=sl: e.dma_start(
                out=d_up.ap()[ob * 128:(ob + 1) * 128, :], in_=upsg[sl][:]),
                waits=[upev[ob]], inc=(S("sO"), 16), drain=False, is_dma=True)
            updm[ob] = (S("sO"), S("sO").n)
            em("act", lambda e, ob=ob, sl=sl: e.dma_start(
                out=d_ga.ap()[ob * 128:(ob + 1) * 128, :], in_=gasg[sl][:]),
                waits=[gaev[ob]], inc=(S("sR16"), 16), drain=False, is_dma=True)
            gadm[ob] = (S("sR16"), S("sR16").n)
            em("dve", lambda e, sl=sl: e.reduce_max(tmpr[:], upsg[sl][:], AX.X,
                                                    apply_absolute_value=True),
               waits=w(upev[ob], mZ if ob == 0 else None))
            em("dve", lambda e: e.tensor_tensor(mxu[:], mxu[:], tmpr[:], ALU.max),
               inc=(S("aNQ"), 1))
            upmx[ob] = (S("aNQ"), S("aNQ").n)
            em("dve", lambda e, sl=sl: e.reduce_max(tmpr[:], gasg[sl][:], AX.X,
                                                    apply_absolute_value=True),
               waits=[gaev[ob]])
            em("dve", lambda e: e.tensor_tensor(mxg[:], mxg[:], tmpr[:], ALU.max),
               inc=(S("vNM"), 1))
            gamx[ob] = (S("vNM"), S("vNM").n)

        bB1 = emit_barrier(1, [mxu, mxg],
                           [(S("aNQ"), S("aNQ").n), (S("vNM"), S("vNM").n)])
        bscal(lambda e: e.tensor_scalar(sscl[0:1, 8:10], sscl[0:1, 4:6],
                                        1.0 / 32767.0, None, ALU.mult),
              waits=[bB1])
        bscal(lambda e: e.tensor_scalar(sscl[0:1, 8:10], sscl[0:1, 8:10], 1e-12,
                                        None, ALU.max))
        bscal(lambda e: e.reciprocal(sscl[0:1, 10:12], sscl[0:1, 8:10]))
        bscal(lambda e: e.tensor_scalar(sscl[0:1, 12:13], sscl[0:1, 9:10],
                                        float(2.0 ** -16), None, ALU.mult),
              inc=(S("vGR"), 1))
        s1done = brd(1, [11, 9, 12, 10], scb1, 4)

        ctxS.close()
        ctxA.close()
        ctxM = ExitStack()
        M_ = ctxM.enter_context
        garaw = [M_(SBL(f"gar{k}", [128, T], BF16)) for k in range(NOB)]
        dwt = [M_(SBL(f"dwt{k}", [128, 512], BF16)) for k in range(2)]
        garld = {}
        phase5_done = w(upmm[NOB - 1], gamm[NOB - 1], upev[NOB - 1], gaev[NOB - 1],
                        (S("sR16"), S("sR16").n), (S("sO"), S("sO").n))
        g2p = {}
        ia_done, sg_done, ib_done = {}, {}, {}

        def g2_stageA(ob):
            iA_t = iA if ob % 2 == 0 else iA3
            Xp = X[1 + ob % 2]
            em("pool", lambda e, ob=ob: e.dma_start(
                out=garaw[ob][:], in_=d_ga.ap()[ob * 128:(ob + 1) * 128, :]),
                waits=(phase5_done + [s1done] if ob == 0 else []),
                inc=(S("vGST"), 16), drain=False, is_dma=True)
            garld[ob] = (S("vGST"), S("vGST").n)
            em("dve", lambda e, ob=ob, iA_t=iA_t: e.tensor_scalar(
                iA_t[:], garaw[ob][:], scb1[:, 0:1], None, ALU.mult),
               waits=w(garld[ob], sX1 if ob == 0 else None),
               inc=(S("qk0"), 1))
            ia_done[ob] = (S("qk0"), S("qk0").n)
            em("act", lambda e, iA_t=iA_t, Xp=Xp: e.activation(
                Xp[:], iA_t[:], AF.Sigmoid, scale=scb1[:, 1:2]),
               waits=w(ia_done[ob], ib_done.get(ob - 2)), inc=(S("aG"), 1))
            sg_done[ob] = (S("aG"), S("aG").n)

        def g2_stageB(ob):
            iA_t = iA if ob % 2 == 0 else iA3
            iB_t = iA2
            Xp = X[1 + ob % 2]
            em("dve", lambda e, iB_t=iB_t, Xp=Xp: e.tensor_scalar(
                iB_t[:], Xp[:], 65536.0, 65535.0, ALU.mult, ALU.min),
               waits=[sg_done[ob]], inc=(S("qk0"), 1))
            ib_done[ob] = (S("qk0"), S("qk0").n)
            em("dve", lambda e, ob=ob, iA_t=iA_t, iB_t=iB_t: e.scalar_tensor_tensor(
                garaw[ob][:], iA_t[:], scb1[:, 2:3], iB_t[:], ALU.mult, ALU.mult))
            em("dve", lambda e, ob=ob: e.reduce_max(tmpr[:], garaw[ob][:], AX.X,
                                                    apply_absolute_value=True))
            em("dve", lambda e: e.tensor_tensor(mx2[:], mx2[:], tmpr[:], ALU.max),
               inc=(S("vGVAR"), 1))
            g2p[ob] = (S("vGVAR"), S("vGVAR").n)

        for ob in range(NOB + 1):
            if ob < NOB:
                g2_stageA(ob)
            if ob >= 1:
                g2_stageB(ob - 1)

        bB2 = emit_barrier(2, [mx2], [(S("vGVAR"), S("vGVAR").n)])
        bscal(lambda e: e.tensor_scalar(sscl[0:1, 13:14], sscl[0:1, 4:5],
                                        1.0 / 32767.0, None, ALU.mult),
              waits=[bB2])
        bscal(lambda e: e.tensor_scalar(sscl[0:1, 13:14], sscl[0:1, 13:14], 1e-12,
                                        None, ALU.max))
        bscal(lambda e: e.reciprocal(sscl[0:1, 14:15], sscl[0:1, 13:14]))
        bscal(lambda e: e.tensor_tensor(sscl[0:1, 15:16], sscl[0:1, 13:14],
                                        sscl[0:1, 8:9], ALU.mult),
              inc=(S("vGR"), 1))
        s2done = brd(2, [14, 15], scb2, 2)

        odm, ow, op_ = {}, {}, {}
        for ob in range(NOB):
            sl = ob % 2
            em("pool", lambda e, ob=ob, sl=sl: e.dma_start(
                out=X[6 + sl][:], in_=d_up.ap()[ob * 128:(ob + 1) * 128, :]),
                waits=w((S("sO"), S("sO").n) if ob == 0 else None,
                        op_.get(ob - 2)),
                inc=(S("aLD" if sl == 0 else "vLDN"), 16), drain=False, is_dma=True)
            lU = (S("aLD" if sl == 0 else "vLDN"),
                  S("aLD" if sl == 0 else "vLDN").n)
            em("dve", lambda e, sl=sl: e.tensor_scalar(iA[:], X[6 + sl][:],
                                                       scb1[:, 3:4], None, ALU.mult),
               waits=w(lU, s2done if ob == 0 else None, g2p.get(NOB - 1) if ob == 0 else None),
               inc=(S("ev0"), 1))
            op_[ob] = (S("ev0"), S("ev0").n)
            em("dve", lambda e, ob=ob: e.tensor_scalar(iA2[:], garaw[ob][:],
                                                       scb2[:, 0:1], None, ALU.mult))
            em("dve", lambda e, ob=ob: e.scalar_tensor_tensor(
                garaw[ob][:], iA[:], scb2[:, 1:2], iA2[:], ALU.mult, ALU.mult),
               inc=(S("ev0"), 1))
            em("dve", lambda e, ob=ob: e.reduce_max(tmpr[:], garaw[ob][:], AX.X,
                                                    apply_absolute_value=True))
            em("dve", lambda e: e.tensor_tensor(mx3[:], mx3[:], tmpr[:], ALU.max),
               inc=(S("aGSD"), 1))

        bB3 = emit_barrier(3, [mx3], [(S("aGSD"), S("aGSD").n)])
        bscal(lambda e: e.tensor_scalar(sscl[0:1, 16:17], sscl[0:1, 4:5],
                                        1.0 / 32767.0, None, ALU.mult),
              waits=[bB3])
        bscal(lambda e: e.tensor_scalar(sscl[0:1, 16:17], sscl[0:1, 16:17], 1e-12,
                                        None, ALU.max))
        bscal(lambda e: e.reciprocal(sscl[0:1, 17:18], sscl[0:1, 16:17]),
              inc=(S("vGR"), 1))
        s3done = brd(3, [17, 16], scb3, 2)

        oq = {}
        for ob in range(NOB):
            em("dve", lambda e, ob=ob: e.tensor_scalar(iA[:], garaw[ob][:],
                                                       scb3[:, 0:1], None, ALU.mult),
               waits=w(s3done if ob == 0 else None,
                       (S("aGSD"), S("aGSD").n) if ob == 0 else None,
                       oq.get(ob - 1)),
               inc=(S("ev0"), 1))
            em("act", lambda e, ob=ob: e.activation(garaw[ob][:], iA[:],
                                                    AF.Identity, scale=scb3[:, 1:2]),
               waits=[(S("ev0"), S("ev0").n)], inc=(S("vGSC"), 1))
            oq[ob] = (S("vGSC"), S("vGSC").n)

        gdo3 = g_w_out["do"].ap().rearrange("c (p ic o) -> c p ic o", p=128, ic=24)
        dmm, dev = {}, {}
        for o2c in range(NEC):
            ps = psF1 if o2c % 2 == 0 else psF2
            for g in range(NOB // 4):
                sl = g % 2
                prev_use = (dmm.get((o2c, 4 * g - 5)) if g >= 2
                            else (dmm.get((o2c - 1, 4 * g + 19))
                                  if o2c > 0 else None))
                em("sp", lambda e, g=g, sl=sl, o2c=o2c: e.dma_start(
                    out=dwt[sl][:].rearrange("p (j o) -> p j o", o=128),
                    in_=gdo3[o2c, :, 4 * g:4 * g + 4, :]),
                    waits=w((S("gRBC"), gW["do"]) if (o2c == 0 and g == 0) else None,
                            prev_use),
                    inc=(S("aV" if sl == 0 else "pV"), 16))
                lW = (S("aV" if sl == 0 else "pV"),
                      S("aV" if sl == 0 else "pV").n)
                for j in range(4):
                    i24 = 4 * g + j
                    for ic in range(4):
                        em("pe", lambda e, i24=i24, j=j, sl=sl, ic=ic, ps=ps: e.matmul(
                            ps[:, ic * 512:(ic + 1) * 512],
                            dwt[sl][:, j * 128:(j + 1) * 128],
                            garaw[i24][:, ic * 512:(ic + 1) * 512],
                            start=(i24 == 0), stop=(i24 == NOB - 1)),
                            waits=(w(lW if j == 0 else None, oq[i24],
                                     dev.get(o2c - 2) if j == 0 else None,
                                     (S("aQK"), S("aQK").n) if (o2c == 0 and i24 == 0) else None,
                                     (S("aPP"), S("aPP").n) if (o2c == 1 and i24 == 0) else None,
                                     (S("L1ROWS"), S("L1ROWS").n) if (o2c == 1 and i24 == 0) else None)
                                   if ic == 0 else ()),
                            inc=(S("pNRM"), 1) if ic == 3 else None)
                    dmm[(o2c, i24)] = (S("pNRM"), S("pNRM").n)
            xsl = 5 + o2c % 2
            em("sp", lambda e, o2c=o2c, xsl=xsl: e.dma_start(
                out=X[xsl][:], in_=d_x1.ap()[o2c * 128:(o2c + 1) * 128, :]),
                waits=w(sX1 if o2c == 0 else None, dev.get(o2c - 2)),
                inc=(S("sLp"), 16))
            ysl = 3 + o2c % 2
            em("dve", lambda e, ps=ps, xsl=xsl, ysl=ysl: e.tensor_tensor(
                X[ysl][:], ps[:], X[xsl][:], ALU.add),
               waits=w(dmm[(o2c, NOB - 1)], (S("sLp"), S("sLp").n),
                       dev.get(o2c - 2)),
               inc=(S("vOG"), 1))
            dv = (S("vOG"), S("vOG").n)
            em("pool", lambda e, o2c=o2c, ysl=ysl: e.dma_start(
                out=yout[o2c * 128:(o2c + 1) * 128, :], in_=X[ysl][:]),
                waits=[dv], inc=(S("vAPP"), 16), drain=False, is_dma=True)
            dev[o2c] = (S("vAPP"), S("vAPP").n)
        em("sp", lambda e: e.nop(), waits=[(S("vAPP"), S("vAPP").n)] +
           ([(S("sDBG"), S("sDBG").n), (S("sDBQ"), S("sDBQ").n),
             (S("sDBP"), S("sDBP").n), (S("sDBP2"), S("sDBP2").n)] if KDBG else []),
           drain=False)
        ctxM.close()
        ctxF2.close()
        ctxFP.close()

    with ctx0, ctxA, ctxB:
        with nc.Block() as block:
            @block.sync
            def _(sp):
                EM.run("sp", sp)

            @block.tensor
            def _(pe):
                EM.run("pe", pe)

            @block.vector
            def _(dve):
                EM.run("dve", dve)

            @block.scalar
            def _(act):
                EM.run("act", act)

            @block.gpsimd
            def _(pool):
                EM.run("pool", pool)
    return nc


def kernel(**inputs):
    g = lambda k: np.ascontiguousarray(np.asarray(inputs[k], dtype=np.float32))
    x = g("L0_attn_x")
    TT_ = lambda a: np.ascontiguousarray(a.T)
    base = {
        "wqT": TT_(g("L0_attn_wq")), "wkT": TT_(g("L0_attn_wk")),
        "wvT": TT_(g("L0_attn_wv")), "wgT": TT_(g("L0_attn_wg")),
        "woT": TT_(g("L0_attn_wo")), "wdT": TT_(g("L0_attn_wd")),
        "bq": np.ascontiguousarray(g("L0_attn_bq").reshape(NEC, 128).T),
        "bk": np.ascontiguousarray(g("L0_attn_bk").reshape(NEC, 128).T),
        "bg": np.ascontiguousarray(g("L0_attn_bg").reshape(NEC, 128).T),
        "bo": np.ascontiguousarray(g("L0_attn_bo").reshape(NEC, 128).T),
        "gamma": np.ascontiguousarray(g("L0_attn_gamma").reshape(NEC, 128).T),
        "beta": np.ascontiguousarray(g("L0_attn_beta").reshape(NEC, 128).T),
        "bv": g("L0_attn_bv").reshape(1, E),
        "bd": g("L0_attn_bd").reshape(H, 1),
    }
    wupN = g("L0_ffn_w_up")
    wgaN = g("L0_ffn_w_gate")
    wdoN = g("L0_ffn_w_down")
    nc = build_nc()
    in_maps = []
    for b in range(8):
        m = dict(base)
        m["x"] = TT_(x[b])
        m["wupNs"] = np.ascontiguousarray(wupN[b * 384:(b + 1) * 384, :])
        m["wgaNs"] = np.ascontiguousarray(wgaN[b * 384:(b + 1) * 384, :])
        m["wdoNs"] = np.ascontiguousarray(wdoN[b * 128:(b + 1) * 128, :])
        in_maps.append(m)
    res = run_bass_kernel_spmd(nc, in_maps, core_ids=list(range(8)))
    out = np.stack([np.asarray(res.results[b]["y"]).T for b in range(8)])
    return out.astype(np.float32)


# version 23
# speedup vs baseline: 1.0715x; 1.0043x over previous
import os
import numpy as np
from contextlib import ExitStack

import concourse.bass as bass
import concourse.mybir as mybir
from concourse.bass_utils import run_bass_kernel_spmd

F32 = mybir.dt.float32
BF16 = mybir.dt.bfloat16
I32 = mybir.dt.int32
FP16 = mybir.dt.float16
AF = mybir.ActivationFunctionType
ALU = mybir.AluOpType
AX = mybir.AxisListType

T, E, H, HD, FF = 2048, 1024, 16, 64, 3072
NEC = E // 128
NTB = T // 128
TWOPI = float(2.0 * np.pi)
KSTOP = os.environ.get("KSTOP", "full")


class Sem:
    def __init__(self, ctx, nc, name):
        self.h = ctx.enter_context(nc.semaphore(name))
        self.n = 0


class Emitter:
    ENGS = ("pe", "dve", "act", "pool", "sp")

    def __init__(self):
        self.ops = {e: [] for e in self.ENGS}

    def em(self, eng, fn, waits=(), inc=None, drain=None, is_dma=False):
        if drain is None:
            drain = eng in ("dve", "act", "pool")
        w = tuple((s, v) for (s, v) in waits if v > 0)
        r = None
        if inc is not None:
            s, k = inc
            s.n += k
            r = s.n
        self.ops[eng].append((w, fn, inc, r, drain, is_dma))
        return r

    def run(self, eng, engobj):
        compute = eng in ("dve", "act", "pool")
        for w, fn, inc, _r, drain, is_dma in self.ops[eng]:
            for s, v in w:
                engobj.wait_ge(s.h, v)
            if drain:
                engobj.drain()
            inst = fn(engobj)
            if inc is not None:
                if compute and not is_dma:
                    engobj.drain().then_inc(inc[0].h, inc[1])
                else:
                    inst.then_inc(inc[0].h, inc[1])


def build_nc():
    nc = bass.Bass("TRN2", target_bir_lowering=False)

    xT = nc.dram_tensor("x", [E, T], F32, kind="ExternalInput").ap()
    wT = {nm: nc.dram_tensor(nm + "T", [E, E], F32, kind="ExternalInput").ap()
          for nm in ("wq", "wk", "wv", "wg", "wo")}
    wdT = nc.dram_tensor("wdT", [E, H], F32, kind="ExternalInput").ap()
    bfm_names = ("bq", "bk", "bg", "bo", "gamma", "beta")
    bfm = {nm: nc.dram_tensor(nm, [128, NEC], F32, kind="ExternalInput").ap()
           for nm in bfm_names}
    bv_in = nc.dram_tensor("bv", [1, E], F32, kind="ExternalInput").ap()
    bd_in = nc.dram_tensor("bd", [H, 1], F32, kind="ExternalInput").ap()
    wupNs = nc.dram_tensor("wupNs", [384, E], F32, kind="ExternalInput").ap()
    wgaNs = nc.dram_tensor("wgaNs", [384, E], F32, kind="ExternalInput").ap()
    wdoNs = nc.dram_tensor("wdoNs", [128, FF], F32, kind="ExternalInput").ap()
    yout = nc.dram_tensor("y", [E, T], F32, kind="ExternalOutput").ap()
    KDBG = os.environ.get("KDBG")
    if KDBG:
        dbg = {nm: nc.dram_tensor("dbg_" + nm, shp, F32, kind="ExternalOutput").ap()
               for nm, shp in (("y0", [128, T]), ("ld", [H, 1]), ("qf", [64, 2 * T]),
                               ("kf", [64, 2 * T]), ("vp", [128, T]), ("og0", [128, T]),
                               ("mneg", [128, 2]), ("cos", [64, T]), ("sin", [64, T]),
                               ("gate", [128, T]), ("og0p", [128, T]),
                               ("pf", [65, 1024]), ("pp", [65, 1024]),
                               ("qf0", [64, 2 * T]), ("kf0", [64, 2 * T]),
                               ("vp0", [128, T]), ("psq0", [128, T]),
                               ("qpre0", [64, 2 * T]), ("wbfq0", [128, E]))}
        dbg.update({f"yc{c}": nc.dram_tensor(f"dbg_yc{c}", [128, T], F32,
                                             kind="ExternalOutput").ap()
                    for c in range(8)})

    d_row = nc.dram_tensor("d_row", [4, T], F32)
    d_ld = nc.dram_tensor("d_ld", [H, 1], F32)
    d_c16 = nc.dram_tensor("d_c16", [H, T], F32)
    d_r16 = nc.dram_tensor("d_r16", [H, T], F32)
    d_sc = nc.dram_tensor("d_sc", [1, 64], F32)
    d_x1 = nc.dram_tensor("d_x1", [E, T], F32)
    d_up = nc.dram_tensor("d_up", [FF, T], BF16)
    d_ga = nc.dram_tensor("d_ga", [FF, T], BF16)
    d_ss = nc.dram_tensor("d_ss", [1, 12288], F32)
    d_red = nc.dram_tensor("d_red", [8, 128], F32)
    d_sc2 = nc.dram_tensor("d_sc2", [4, 8], F32)
    g_w_in = {nm: nc.dram_tensor(f"g_{nm}_in", [1, 128 * FF], BF16)
              for nm in ("up", "ga", "do")}
    g_w_out = {nm: nc.dram_tensor(f"g_{nm}_out", [8, 128 * FF], BF16)
               for nm in ("up", "ga", "do")}
    cc_in = [nc.dram_tensor(f"cc_in{k}", [1, 512], F32) for k in range(4)]
    cc_out = [nc.dram_tensor(f"cc_out{k}", [1, 512], F32) for k in range(4)]

    ctx0 = ExitStack()

    def P(t):
        return ctx0.enter_context(t)

    def SBL(name, shape, dt):
        return nc.sbuf_tensor(name, shape, dt, side="left")

    def SBR(name, shape, dt):
        return nc.sbuf_tensor(name, shape, dt, side="right")

    def PSL(name, shape, dt):
        return nc.psum_tensor(name, shape, dt, side="left")

    def PSR(name, shape, dt):
        return nc.psum_tensor(name, shape, dt, side="right")

    SEMS = {}

    def S(name):
        if name not in SEMS:
            SEMS[name] = Sem(ctx0, nc, name)
        return SEMS[name]

    X = [P(SBL(f"X{c}", [128, T], F32)) for c in range(NEC)]
    sb_b = {nm: P(SBL("s_" + nm, [128, NEC], F32)) for nm in bfm_names}
    bv_bf = P(SBL("bv_bf", [1, E], BF16))
    wd_sb = P(SBL("wd_sb", [128, NEC * H], BF16))
    bd_sb = P(SBL("bd_sb", [H, 1], F32))
    ones_row = P(SBL("ones_row", [1, 128], BF16))
    ones_col = P(SBL("ones_col", [128, 1], BF16))
    Drelu0 = P(SBL("Drelu0", [128, 128], BF16))
    ldI = P(SBL("ldI", [128, H * 128], BF16))
    lam = P(SBL("lam", [128, H * 16], F32))
    cos64 = P(SBL("cos64", [64, T], BF16))
    sinS = P(SBL("sinS", [64, T], BF16))
    ld16 = P(SBL("ld16", [H, 1], F32))
    sc2 = P(SBL("sc2", [128, 8], F32))
    Pi64 = P(SBL("Pi64", [64, 64], BF16))
    ceps = P(SBL("ceps", [65, 1], F32))
    cb5 = P(SBL("cb5", [16, 1], F32))
    z65 = P(SBL("z65", [1, 65], BF16))
    bk8 = P(SBL("bk8", [128, NEC], F32))

    EM = Emitter()
    em = EM.em

    ctxP0 = ExitStack()
    P0 = ctxP0.enter_context
    it_t = P0(SBR("it_t", [128, T], F32))
    it_jp = P0(SBR("it_jp", [128, 16], F32))
    I128 = P0(SBR("I128", [128, 128], F32))
    ldbc = P0(SBR("ldbc", [128, H], F32))
    ldn16 = P0(SBR("ldn16", [H, 1], F32))
    wd_f32 = P0(SBR("wd_f32", [128, NEC * H], F32))
    bv_f32 = P0(SBR("bv_f32", [1, E], F32))

    for c in range(NEC):
        em("sp", lambda e, c=c: e.dma_start(
            out=X[c][:], in_=xT[c * 128:(c + 1) * 128, :]), inc=(S("sLx"), 16))
    LX = S("sLx").n
    for nm in bfm_names:
        em("sp", lambda e, nm=nm: e.dma_start(out=sb_b[nm][:], in_=bfm[nm]),
           inc=(S("sLp"), 16))
    em("sp", lambda e: e.dma_start(out=bd_sb[:], in_=bd_in), inc=(S("sLp"), 16))
    em("sp", lambda e: e.dma_start(
        out=wd_f32[:].rearrange("p (c h) -> p c h", h=H),
        in_=wdT.rearrange("(c p) h -> p c h", p=128)), inc=(S("sLp"), 16))
    em("sp", lambda e: e.dma_start(out=bv_f32[:], in_=bv_in), inc=(S("sLp"), 16))
    LP = S("sLp").n

    em("pool", lambda e: e.memset(z65[:], 0.0))
    em("pool", lambda e: e.memset(ceps[:], 1e-5))
    em("pool", lambda e: e.memset(cb5[:], float(-5.0 * np.log(2.0))))
    em("pool", lambda e: e.iota(it_t[:], pattern=[[1, T]], base=0, channel_multiplier=0,
                                allow_small_or_imprecise_dtypes=True))
    em("pool", lambda e: e.iota(it_jp[:], pattern=[[128, 16]], base=0, channel_multiplier=1,
                                allow_small_or_imprecise_dtypes=True))
    em("pool", lambda e: e.memset(I128[:], 1.0))
    em("pool", lambda e: e.affine_select(I128[:], I128[:], pattern=[[1, 128]],
                                         compare_op=ALU.is_equal, fill=0.0,
                                         base=0, channel_multiplier=-1))
    em("pool", lambda e: e.tensor_copy(wd_sb[:], wd_f32[:]),
       waits=[(S("sLp"), LP)], inc=(S("cP"), 1))

    ctxT = ExitStack()
    PT = ctxT.enter_context
    it_tmp = PT(SBR("it_tmp", [64, T], F32))
    it_i32 = PT(SBR("it_i32", [64, T], I32))
    frac = PT(SBR("frac", [64, T], F32))
    invf = PT(SBR("invf", [64, 1], F32))
    idv = PT(SBR("idv", [64, 1], I32))
    par = PT(SBR("par", [64, 1], F32))
    emask = PT(SBR("emask", [64, 1], F32))
    pm0 = PT(SBR("pm0", [64, 64], F32))
    pm1 = PT(SBR("pm1", [64, 64], F32))
    dscr = PT(SBR("dscr", [128, 128], F32))

    em("pool", lambda e: e.iota(dscr[:], pattern=[[-1, 128]], base=0,
                                channel_multiplier=1, allow_small_or_imprecise_dtypes=True))
    em("pool", lambda e: e.tensor_scalar(Drelu0[:], dscr[:], 0.0, None, ALU.max),
       inc=(S("cDRELU"), 1))
    em("pool", lambda e: e.iota(idv[:], pattern=[[1, 1]], base=0, channel_multiplier=1),
       inc=(S("cIOTA"), 1))
    em("dve", lambda e: e.tensor_scalar(idv[:], idv[:], 1, None, ALU.bitwise_and),
       waits=[(S("cIOTA"), 1)], inc=(S("cPAR"), 1))
    em("dve", lambda e: e.tensor_copy(par[:], idv[:]), waits=[(S("cPAR"), 1)])
    em("dve", lambda e: e.tensor_scalar(emask[:], par[:], -1.0, 1.0, ALU.mult, ALU.add),
       inc=(S("vMASK"), 1))
    em("pool", lambda e: e.iota(idv[:], pattern=[[1, 1]], base=0, channel_multiplier=1),
       waits=[(S("vMASK"), 1)], inc=(S("cIOTA"), 1))
    em("dve", lambda e: e.tensor_scalar(idv[:], idv[:], 31, None, ALU.bitwise_and),
       waits=[(S("cIOTA"), 2)], inc=(S("cD32"), 1))
    em("dve", lambda e: e.tensor_copy(invf[:], idv[:]), waits=[(S("cD32"), 1)],
       inc=(S("vINVF"), 1))
    em("act", lambda e: e.activation(invf[:], invf[:], AF.Exp,
                                     scale=float(-np.log(10000.0) / 32.0)),
       waits=[(S("vINVF"), 1)], inc=(S("aINVF"), 1))
    em("pool", lambda e: e.memset(pm0[:], 1.0))
    em("pool", lambda e: e.affine_select(pm0[:], pm0[:], pattern=[[1, 64]],
                                         compare_op=ALU.is_equal, fill=0.0,
                                         base=-1, channel_multiplier=-1))
    em("pool", lambda e: e.memset(pm1[:], 1.0))
    em("pool", lambda e: e.affine_select(pm1[:], pm1[:], pattern=[[1, 64]],
                                         compare_op=ALU.is_equal, fill=0.0,
                                         base=1, channel_multiplier=-1), inc=(S("cPM"), 1))
    em("dve", lambda e: e.tensor_scalar(pm0[:], pm0[:], emask[:], None, ALU.mult),
       waits=[(S("cPM"), 1)])
    em("dve", lambda e: e.tensor_scalar(pm1[:], pm1[:], par[:], None, ALU.mult))
    em("dve", lambda e: e.tensor_tensor(Pi64[:], pm0[:], pm1[:], ALU.add),
       inc=(S("vPI"), 1))

    tcor = PT(SBR("tcor", [64, T], F32))

    def emit_trig(dst, shift, idx):
        w0 = [(S("aINVF"), 1), (S("cP"), 1)]
        if idx == 1:
            w0.append((S("aTRG"), 1))
        em("dve", lambda e: e.tensor_scalar(it_tmp[:], it_t[0:64, :], invf[:, 0:1],
                                            None, ALU.mult), waits=w0)
        em("dve", lambda e, sh=shift: e.tensor_scalar(it_tmp[:], it_tmp[:],
                                                      float(1.0 / TWOPI), sh + 1.0,
                                                      ALU.mult, ALU.add))
        em("dve", lambda e: e.tensor_copy(it_i32[:], it_tmp[:]))
        em("dve", lambda e: e.tensor_copy(frac[:], it_i32[:]))
        em("dve", lambda e: e.tensor_tensor(tcor[:], frac[:], it_tmp[:], ALU.is_gt))
        em("dve", lambda e: e.tensor_tensor(frac[:], frac[:], tcor[:], ALU.subtract))
        em("dve", lambda e: e.scalar_tensor_tensor(frac[:], it_tmp[:], -0.5, frac[:],
                                                   ALU.add, ALU.subtract),
           inc=(S("vTRGx"), 1))
        em("act", lambda e, d=dst: e.activation(d, frac[:], AF.Sin, scale=TWOPI),
           waits=[(S("vTRGx"), idx + 1)], inc=(S("aTRG"), 1))
    emit_trig(cos64[:], 0.25, 0)
    emit_trig(sinS[:], 0.0, 1)
    em("dve", lambda e: e.tensor_scalar(cos64[:], cos64[:], -1.0, None, ALU.mult),
       waits=[(S("aTRG"), 1)], inc=(S("vTRIG"), 1))
    em("dve", lambda e: e.tensor_scalar(par[:], par[:], -2.0, 1.0, ALU.mult, ALU.add),
       waits=[(S("aTRG"), 2)])
    em("dve", lambda e: e.tensor_scalar(sinS[:], sinS[:], par[:, 0:1], None, ALU.mult),
       inc=(S("vTRIG"), 1))
    em("dve", lambda e: e.memset(ones_row[:], 1.0))
    em("dve", lambda e: e.memset(ones_col[:], 1.0), inc=(S("vONES"), 1))
    em("dve", lambda e: e.tensor_copy(bv_bf[:], bv_f32[:]), waits=[(S("sLp"), LP)],
       inc=(S("vBV"), 1))
    ctxT.close()

    ctxA = ExitStack()
    A = ctxA.enter_context
    Y = [A(SBL(f"Y{c}", [128, T], BF16)) for c in range(NEC)]

    ctxL = ExitStack()
    PL = ctxL.enter_context
    cb = [PL(SBR(f"cb{k}", [128, T], BF16)) for k in range(2)]
    sqb = [PL(SBR(f"sqb{k}", [128, T], BF16)) for k in range(2)]
    Rbc = PL(SBR("Rbc", [128, T], F32))
    Nbc = PL(SBR("Nbc", [128, T], F32))
    tmpc = PL(SBR("tmpc", [128, T], F32))
    row_m = PL(SBR("row_m", [1, T], F32))
    row_v = PL(SBR("row_v", [1, T], F32))
    sc2bf = PL(SBR("sc2bf", [128, 8], BF16))
    c16t = PL(SBR("c16t", [H, T], F32))
    psStat = PL(PSR("psStat", [65, T], F32))
    psPD = PL(PSR("psPD", [H, 1], F32))
    selA = PL(SBR("selA", [128, 65], BF16))
    selB = PL(SBR("selB", [128, 65], BF16))

    def emit_ln(pref, src_chunks, dst_chunks, x_wait, sel_wait, tl):
        em("pool", lambda e: e.memset(tl["selA"][:], 0.0), waits=sel_wait)
        em("pool", lambda e: e.memset(tl["selA"][:, 0:1], 1.0))
        em("pool", lambda e: e.memset(tl["selB"][:], 0.0))
        em("pool", lambda e: e.memset(tl["selB"][:, 64:65], 1.0),
           inc=(S(pref + "SEL"), 1))
        nsel = S(pref + "SEL").n
        psStat = tl["psStat"]
        mm_done = [0, 0]
        for c in range(NEC):
            b = c % 2
            em("dve", lambda e, c=c, b=b: e.tensor_copy(tl["cb"][b][:], src_chunks[c][:]),
               waits=list(x_wait) + ([(S(pref + f"MM{b}"), mm_done[b])] if mm_done[b] else []),
               inc=(S(pref + f"CB{b}"), 1))
            vc = S(pref + f"CB{b}").n
            em("act", lambda e, c=c, b=b: e.activation(tl["sqb"][b][:], src_chunks[c][:],
                                                       AF.Square),
               waits=list(x_wait) + ([(S(pref + f"MM{b}"), mm_done[b])] if mm_done[b] else []),
               inc=(S(pref + f"SQ{b}"), 1))
            ac = S(pref + f"SQ{b}").n
            for ic in range(4):
                em("pe", lambda e, c=c, b=b, ic=ic: e.matmul(
                    psStat[:, ic * 512:(ic + 1) * 512], tl["selA"][:],
                    tl["cb"][b][:, ic * 512:(ic + 1) * 512],
                    start=(c == 0), stop=False),
                    waits=[(S(pref + f"CB{b}"), vc), (S(pref + "SEL"), nsel),
                           (S("vONES"), 1)] if ic == 0 else ())
            for ic in range(4):
                em("pe", lambda e, c=c, b=b, ic=ic: e.matmul(
                    psStat[:, ic * 512:(ic + 1) * 512], tl["selB"][:],
                    tl["sqb"][b][:, ic * 512:(ic + 1) * 512],
                    start=False, stop=(c == NEC - 1)),
                    waits=[(S(pref + f"SQ{b}"), ac)] if ic == 0 else (),
                    inc=(S(pref + f"MM{b}"), 1) if ic == 3 else None)
            mm_done[b] = S(pref + f"MM{b}").n
        em("act", lambda e: e.activation(tl["row_m"][:], psStat[0:1, :], AF.Copy,
                                         scale=1.0 / E),
           waits=[(S(pref + "MM0"), mm_done[0]), (S(pref + "MM1"), mm_done[1])])
        em("act", lambda e: e.activation(tl["row_v"][:], psStat[64:65, :], AF.Copy,
                                         scale=1.0 / E),
           inc=(S(pref + "ROWS"), 1))
        nrows = S(pref + "ROWS").n
        em("dve", lambda e: e.tensor_tensor(tl["tmpc"][0:1, :], tl["row_m"][:],
                                            tl["row_m"][:], ALU.mult),
           waits=[(S(pref + "ROWS"), nrows)])
        em("dve", lambda e: e.tensor_tensor(tl["row_v"][:], tl["row_v"][:],
                                            tl["tmpc"][0:1, :], ALU.subtract),
           inc=(S(pref + "VAR"), 1))
        em("act", lambda e: e.activation(tl["row_v"][:], tl["row_v"][:], AF.Sqrt,
                                         bias=ceps[0:1, :]),
           waits=[(S(pref + "VAR"), S(pref + "VAR").n), (S("cP"), 1)],
           inc=(S(pref + "SD"), 1))
        em("dve", lambda e: e.reciprocal(tl["row_v"][:], tl["row_v"][:]),
           waits=[(S(pref + "SD"), S(pref + "SD").n)])
        em("dve", lambda e: e.tensor_tensor(tl["row_m"][:], tl["row_m"][:],
                                            tl["row_v"][:], ALU.mult))
        em("dve", lambda e: e.tensor_scalar(tl["row_m"][:], tl["row_m"][:], -1.0, None,
                                            ALU.mult),
           inc=(S(pref + "RW"), 1))
        em("sp", lambda e: e.dma_start(out=d_row.ap()[0:1, :], in_=tl["row_v"][:]),
           waits=[(S(pref + "RW"), S(pref + "RW").n)], inc=(S(pref + "DW"), 16))
        em("sp", lambda e: e.dma_start(out=d_row.ap()[1:2, :], in_=tl["row_m"][:]),
           inc=(S(pref + "DW"), 16))
        em("sp", lambda e: e.dma_start(out=tl["Rbc"][:],
                                       in_=d_row.ap()[0, :].partition_broadcast(128)),
           waits=[(S(pref + "DW"), S(pref + "DW").n)], inc=(S(pref + "DB"), 16))
        em("sp", lambda e: e.dma_start(out=tl["Nbc"][:],
                                       in_=d_row.ap()[1, :].partition_broadcast(128)),
           inc=(S(pref + "DB"), 16))
        ndb = S(pref + "DB").n
        for c in range(NEC):
            em("dve", lambda e, c=c: e.tensor_tensor(tl["tmpc"][:], src_chunks[c][:],
                                                     tl["Rbc"][:], ALU.mult),
               waits=[(S(pref + "DB"), ndb)] if c == 0 else ())
            em("dve", lambda e, c=c: e.tensor_tensor(dst_chunks[c][:], tl["tmpc"][:],
                                                     tl["Nbc"][:], ALU.add),
               inc=(S(pref + "Y"), 1) if c == NEC - 1 else None)
        return (S(pref + "Y"), S(pref + "Y").n)

    em("dve", lambda e: e.tensor_scalar(bk8[:], sb_b["bk"][:], 0.125, None, ALU.mult),
       waits=[(S("sLp"), LP)], inc=(S("vBK8"), 1))
    tl1 = {"cb": cb, "sqb": sqb, "selA": selA, "selB": selB, "psStat": psStat,
           "row_m": row_m, "row_v": row_v, "tmpc": tmpc, "Rbc": Rbc, "Nbc": Nbc}
    w_y = emit_ln("L1", X, Y, [(S("sLx"), LX)], [], tl1)

    for c in range(NEC):
        em("dve", lambda e, c=c: e.reduce_sum(sc2[:, c:c + 1], Y[c][:], AX.X),
           waits=[w_y] if c == 0 else ())
    em("dve", lambda e: e.tensor_copy(sc2bf[:], sc2[:]), inc=(S("vYM"), 1))
    for c in range(NEC):
        em("pe", lambda e, c=c: e.matmul(
            psPD[:], wd_sb[:, c * H:(c + 1) * H], sc2bf[:, c:c + 1],
            start=(c == 0), stop=(c == NEC - 1)),
            waits=[(S("vYM"), 1), (S("cP"), 1)] if c == 0 else (),
            inc=(S("pPD"), 1) if c == NEC - 1 else None)
    em("dve", lambda e: e.tensor_copy(ld16[:], psPD[:]), waits=[(S("pPD"), 1)],
       inc=(S("vPD"), 1))
    em("act", lambda e: e.activation(ld16[:], ld16[:], AF.Sigmoid, scale=1.0 / T,
                                     bias=bd_sb[:]),
       waits=[(S("vPD"), 1), (S("sLp"), LP), (S("cP"), 1)])
    em("act", lambda e: e.activation(ld16[:], ld16[:], AF.Exp,
                                     scale=float(-8.0 * np.log(2.0)), bias=cb5[:]))
    em("act", lambda e: e.activation(ld16[:], ld16[:], AF.Ln, scale=-1.0, bias=1.0),
       inc=(S("aLD"), 1))
    em("sp", lambda e: e.dma_start(out=d_ld.ap(), in_=ld16[:]), waits=[(S("aLD"), 1)],
       inc=(S("sLD"), 16))
    em("sp", lambda e: e.dma_start(
        out=ldbc[:],
        in_=d_ld.ap().rearrange("h one -> one h")[0, :].partition_broadcast(128)),
        waits=[(S("sLD"), 16)], inc=(S("sLD"), 16))
    for h in range(H):
        em("act", lambda e, h=h: e.activation(lam[:, h * 16:(h + 1) * 16], it_jp[:],
                                              AF.Exp, scale=ldbc[:, h:h + 1]),
           waits=[(S("sLD"), 32), (S("cP"), 1)] if h == 0 else (),
           inc=(S("aLAM"), 1) if h == H - 1 else None)
    em("dve", lambda e: e.tensor_scalar(ldn16[:], ld16[:], -1.0, None, ALU.mult),
       waits=[(S("aLD"), 1)], inc=(S("vLDN"), 1))
    em("act", lambda e: e.activation(c16t[:], it_t[0:H, :], AF.Exp, scale=ldn16[:]),
       waits=[(S("vLDN"), 1), (S("cP"), 1)], inc=(S("aC16"), 1))
    em("sp", lambda e: e.dma_start(out=d_c16.ap(), in_=c16t[:]), waits=[(S("aC16"), 1)],
       inc=(S("sC16"), 16))
    for h in range(H):
        em("dve", lambda e, h=h: e.tensor_scalar(ldI[:, h * 128:(h + 1) * 128], I128[:],
                                                 ldbc[:, h:h + 1], None, ALU.mult),
           waits=[(S("sLD"), 32), (S("cP"), 1)] if h == 0 else (),
           inc=(S("vLDI"), 1) if h == H - 1 else None)

    ctxL.close()
    ctxP0.close()

    ctxB = ExitStack()
    B = ctxB.enter_context
    OG = [B(SBL(f"OG{c}", [128, T], BF16)) for c in range(NEC)]
    Qf = B(SBL("Qf", [64, 2 * T], BF16))
    Kf = B(SBL("Kf", [64, 2 * T], BF16))
    Vp = B(SBL("Vp", [128, T], BF16))
    vfut = B(SBL("vfut", [128, 16 * 65], BF16))
    vpast = B(SBL("vpast", [128, 16 * 65], BF16))
    wsl = B(SBL("wsl", [128, E], F32))
    wbf = B(SBL("wbf", [128, E], BF16))
    t64a = B(SBL("t64a", [64, T], BF16))
    t64b = B(SBL("t64b", [64, T], BF16))
    scoresb = [B(SBL(f"scoresb{k}", [128, 1024], BF16)) for k in range(2)]
    pfsb = B(SBL("pfsb", [65, 1024], F32))
    ppsb = B(SBL("ppsb", [65, 1024], F32))
    cbc = B(SBL("cbc", [65, 1024], F32))
    rbc = B(SBL("rbc", [128, 1024], BF16))
    mneg = B(SBL("mneg", [128, 2], F32))
    nqk = B(SBL("nqk", [1, 8], F32))
    gnst = B(SBL("gnst", [128, 2], F32))
    gnstb = B(SBL("gnstb", [128, 2], BF16))
    gnfold = B(SBL("gnfold", [128, 65], BF16))
    gn2 = B(SBL("gn2", [65, 4], F32))
    gsc = B(SBL("gsc", [128, 2], F32))
    gate_t = B(SBL("gate_t", [128, T], BF16))
    psQ = B(PSL("psQ", [128, T], F32))
    psPV = B(PSL("psPV", [65, T], F32))

    P0W = [w_y, (S("aC16"), 1), (S("aLAM"), 1), (S("vLDI"), 1), (S("vPD"), 1),
           (S("sC16"), 16)]
    em("pool", lambda e: e.memset(gnfold[:], 0.0), waits=P0W)
    em("pool", lambda e: e.memset(gnfold[0:64, 0:1], 1.0))
    em("pool", lambda e: e.memset(gnfold[64:128, 64:65], 1.0), inc=(S("cGNF"), 1))

    trk = {
        "wsl_free": P0W, "wbf_free": P0W, "psQ_free": P0W, "psPV_free": P0W,
        "QK_free": P0W, "t64_free": P0W, "Vp_free": P0W, "vt_free": P0W,
        "pfpp_free": P0W, "cbc_free": P0W, "rbc_free": P0W, "mneg_free": P0W,
        "qkbuf_free": [P0W, P0W], "sbuf_free": [P0W, P0W],
    }

    def w(*items):
        out = []
        for t in items:
            if t is None:
                continue
            if isinstance(t, list):
                out.extend(x for x in t if x is not None)
            else:
                out.append(t)
        return out

    def stream_weight(dram_col_slice):
        em("pool", lambda e, d=dram_col_slice: e.dma_start(
            out=wbf[:].rearrange("p (c o) -> p c o", o=128),
            in_=d.rearrange("(c p) o -> p c o", p=128)),
            waits=w(trk["wbf_free"]), inc=(S("cWS"), 16), drain=False, is_dma=True)
        return (S("cWS"), S("cWS").n)

    def proj_fm(g_w, rhs_chunks, psem, extra_first=()):
        for ic in range(4):
            for c in range(NEC):
                em("pe", lambda e, c=c, ic=ic: e.matmul(
                    psQ[:, ic * 512:(ic + 1) * 512],
                    wbf[:, c * 128:(c + 1) * 128],
                    rhs_chunks[c][:, ic * 512:(ic + 1) * 512],
                    start=(c == 0), stop=(c == NEC - 1)),
                    waits=(w(g_w, trk["psQ_free"]) + list(extra_first)
                           if (c == 0 and ic == 0) else ()),
                    inc=(S(psem), 1) if (ic == 3 and c == NEC - 1) else None)
        trk["wbf_free"] = (S(psem), S(psem).n)
        return (S(psem), S(psem).n)

    for hp in range(8):
        for which, (wnm, bnm, dst) in enumerate((("wq", "bq", Qf), ("wk", "bk", Kf))):
            g_w = stream_weight(wT[wnm][:, hp * 128:(hp + 1) * 128])
            pQ = proj_fm(g_w, Y, "pQK", extra_first=[w_y] + w(trk["QK_free"]))
            if KDBG and hp == 0 and which == 0:
                em("pool", lambda e: e.dma_start(out=dbg["wbfq0"], in_=wbf[:]),
                   waits=[g_w], inc=(S("sDBP"), 16), drain=False, is_dma=True)
                trk["wbf_free"] = w(trk["wbf_free"], (S("sDBP"), 16))
            bsrc = sb_b[bnm] if which == 0 else bk8
            ksc = 1.0 if which == 0 else 0.125
            em("act", lambda e, d=dst, b=bsrc, sc=ksc, hp=hp: e.activation(
                d[:, 0:T], psQ[0:64, :], AF.Identity, scale=sc, bias=b[0:64, hp:hp + 1]),
                waits=w(pQ, [(S("sLp"), LP), (S("vBK8"), 1)]
                        if (hp == 0 and which == 0) else None))
            em("act", lambda e, d=dst, b=bsrc, sc=ksc, hp=hp: e.activation(
                d[:, T:2 * T], psQ[64:128, :], AF.Identity, scale=sc,
                bias=b[64:128, hp:hp + 1]),
                inc=(S("aQK"), 1))
            aQ = (S("aQK"), S("aQK").n)
            trk["psQ_free"] = aQ
            if KDBG and hp == 0 and which == 0:
                em("pool", lambda e: e.dma_start(out=dbg["qpre0"], in_=Qf[:]),
                   waits=[aQ], inc=(S("sDBP2"), 16), drain=False, is_dma=True)
            last_comb = None
            pRs = {}
            for s in range(2):
                rps = psQ if s == 0 else psPV
                for ic in range(4):
                    em("pe", lambda e, s=s, ic=ic, d=dst, rps=rps: e.matmul(
                        rps[0:64, ic * 512:(ic + 1) * 512],
                        Pi64[:], d[:, s * T + ic * 512: s * T + (ic + 1) * 512],
                        start=True, stop=True),
                        waits=(w(aQ, (S("vPI"), 1),
                                 trk["psPV_free"] if s == 1 else None)
                               if ic == 0 else ()),
                        inc=(S("pROT"), 1) if ic == 3 else None)
                pRs[s] = (S("pROT"), S("pROT").n)
            for s in range(2):
                rps = psQ if s == 0 else psPV
                em("act", lambda e, rps=rps: e.activation(t64b[:], rps[0:64, :],
                                                          AF.Copy),
                   waits=w(pRs[s], trk["t64_free"]), inc=(S("aROT"), 1))
                aRot = (S("aROT"), S("aROT").n)
                if s == 0:
                    trk["psQ_free"] = w(aQ, aRot)
                else:
                    trk["psPV_free"] = aRot
                em("dve", lambda e, s=s, d=dst: e.tensor_tensor(
                    t64a[:], d[:, s * T:(s + 1) * T], cos64[:], ALU.mult),
                    waits=w((S("vTRIG"), 2), aRot,
                            (S("sDBP2"), 16) if (KDBG and hp == 0 and which == 0
                                                 and s == 0) else None))
                em("dve", lambda e: e.tensor_tensor(t64b[:], t64b[:], sinS[:], ALU.mult))
                em("dve", lambda e, s=s, d=dst: e.tensor_tensor(
                    d[:, s * T:(s + 1) * T], t64a[:], t64b[:], ALU.add),
                    inc=(S("vCOMB"), 1))
                last_comb = (S("vCOMB"), S("vCOMB").n)
                trk["t64_free"] = last_comb
        v_qk = (S("vCOMB"), S("vCOMB").n)
        if KDBG and hp == 0:
            em("pool", lambda e: e.dma_start(out=dbg["qf0"], in_=Qf[:]),
               waits=[v_qk], inc=(S("sDBQ"), 16), drain=False, is_dma=True)
            em("pool", lambda e: e.dma_start(out=dbg["kf0"], in_=Kf[:]),
               inc=(S("sDBQ"), 16), drain=False, is_dma=True)

        g_v = stream_weight(wT["wv"][:, hp * 128:(hp + 1) * 128])
        psV_free = [trk["psQ_free"]] * 4
        for tb in range(NTB):
            bk = tb % 4
            for c in range(NEC):
                em("pe", lambda e, tb=tb, c=c, bk=bk: e.matmul(
                    psQ[:, bk * 512: bk * 512 + 128],
                    Y[c][:, tb * 128:(tb + 1) * 128],
                    wbf[:, c * 128:(c + 1) * 128],
                    start=(c == 0), stop=False),
                    waits=(w(g_v, psV_free[bk], trk["Vp_free"]) if c == 0 else ()))
            em("pe", lambda e, bk=bk, hp=hp: e.matmul(
                psQ[:, bk * 512: bk * 512 + 128],
                ones_row[0:1, 0:128],
                bv_bf[0:1, hp * 128:(hp + 1) * 128],
                start=False, stop=True),
                waits=[(S("vBV"), 1), (S("vONES"), 1)] if (hp == 0 and tb == 0) else (),
                inc=(S("pV"), 1))
            pV = (S("pV"), S("pV").n)
            em("act", lambda e, tb=tb, bk=bk: e.activation(
                Vp[:, tb * 128:(tb + 1) * 128], psQ[:, bk * 512: bk * 512 + 128], AF.Copy),
                waits=[pV], inc=(S("aV"), 1))
            psV_free[bk] = (S("aV"), S("aV").n)
        a_v = (S("aV"), S("aV").n)
        if KDBG and hp == 0:
            em("pool", lambda e: e.dma_start(out=dbg["vp0"], in_=Vp[:]),
               waits=[a_v], inc=(S("sDBQ"), 16), drain=False, is_dma=True)
        trk["psQ_free"] = a_v
        trk["wbf_free"] = (S("pV"), S("pV").n)

        for w2, src in enumerate((Qf, Kf)):
            em("act", lambda e, s=src: e.activation(t64a[:], s[:, 0:T], AF.Square),
               waits=(w(v_qk, trk["t64_free"]) if w2 == 0
                      else [(S("pNRM"), S("pNRM").n)]))
            em("act", lambda e, s=src: e.activation(t64b[:], s[:, T:2 * T], AF.Square),
               inc=(S("aNSQ"), 1))
            a_sq = (S("aNSQ"), S("aNSQ").n)
            for half, tt in ((0, t64a), (1, t64b)):
                for ic in range(4):
                    em("pe", lambda e, tt=tt, ic=ic: e.matmul(
                        psQ[0:1, ic * 512:(ic + 1) * 512],
                        ones_col[0:64, :], tt[:, ic * 512:(ic + 1) * 512],
                        start=True, stop=True),
                        waits=w(a_sq, trk["psQ_free"]) if ic == 0 else (),
                        inc=(S("pNRM"), 1) if ic == 3 else None)
                pN = (S("pNRM"), S("pNRM").n)
                em("dve", lambda e: e.tensor_copy(pfsb[0:1, :], psQ[0:1, 0:1024]),
                   waits=w(pN, trk["pfpp_free"]))
                em("dve", lambda e: e.tensor_copy(ppsb[0:1, :], psQ[0:1, 1024:2048]))
                em("dve", lambda e, k=w2 * 2 + half: e.reduce_max(
                    nqk[0:1, k:k + 1], pfsb[0:1, :], AX.X))
                em("dve", lambda e, k=w2 * 2 + half: e.reduce_max(
                    nqk[0:1, 4 + k:5 + k], ppsb[0:1, :], AX.X))
                em("dve", lambda e, k=w2 * 2 + half: e.tensor_tensor(
                    nqk[0:1, k:k + 1], nqk[0:1, k:k + 1], nqk[0:1, 4 + k:5 + k], ALU.max),
                    inc=(S("vNRM"), 1))
                trk["psQ_free"] = (S("vNRM"), S("vNRM").n)
                trk["pfpp_free"] = (S("vNRM"), S("vNRM").n)
        em("dve", lambda e: e.tensor_tensor(nqk[0:1, 4:6], nqk[0:1, 0:2], nqk[0:1, 2:4],
                                            ALU.mult), inc=(S("vNQ"), 1))
        em("act", lambda e: e.activation(nqk[0:1, 4:6], nqk[0:1, 4:6], AF.Sqrt),
           waits=[(S("vNQ"), S("vNQ").n)], inc=(S("aNQ"), 1))
        em("dve", lambda e: e.tensor_scalar(nqk[0:1, 4:6], nqk[0:1, 4:6], -1.0,
                                            None, ALU.mult),
           waits=[(S("aNQ"), S("aNQ").n)], inc=(S("vNM"), 1))
        em("pool", lambda e, hp=hp: e.dma_start(out=d_sc.ap()[0:1, 2 * hp:2 * hp + 2],
                                               in_=nqk[0:1, 4:6]),
           waits=[(S("vNM"), S("vNM").n)], inc=(S("sNQ"), 16),
           drain=False, is_dma=True)
        em("pool", lambda e, hp=hp: e.dma_start(
            out=mneg[:], in_=d_sc.ap()[0, 2 * hp:2 * hp + 2].partition_broadcast(128)),
           waits=w((S("sNQ"), S("sNQ").n), trk["mneg_free"]), inc=(S("sNQ"), 16),
           drain=False, is_dma=True)
        ss_mneg = (S("sNQ"), S("sNQ").n)

        vfut3 = vfut[:].rearrange("p (j d) -> p j d", d=65)
        vpast3 = vpast[:].rearrange("p (j d) -> p j d", d=65)
        Vp3 = Vp[:].rearrange("p (j d) -> p j d", d=128)
        for s in range(2):
            h = 2 * hp + s
            lam3 = lam[:, h * 16:(h + 1) * 16].rearrange("p j -> p j ()")
            em("dve", lambda e, s=s, lam3=lam3: e.tensor_tensor(
                vfut3[:, :, 0:64], Vp3[:, :, s * 64:(s + 1) * 64],
                lam3.broadcast_to((128, 16, 64)), ALU.mult),
                waits=w(a_v, (S("aLAM"), 1), trk["vt_free"]))
            em("dve", lambda e, lam3=lam3: e.tensor_copy(vfut3[:, :, 64:65], lam3))
            em("dve", lambda e, s=s: e.tensor_copy(
                vpast3[:, :, 0:64], Vp3[:, :, s * 64:(s + 1) * 64]))
            em("dve", lambda e: e.memset(vpast3[:, :, 64:65], 1.0), inc=(S("vVT"), 1))
            v_vt = (S("vVT"), S("vVT").n)
            trk["Vp_free"] = (w(v_vt, (S("sDBQ"), 48))
                              if (KDBG and hp == 0) else v_vt)

            for ic2 in range(2):
                def _wf(jb):
                    return min(max(128 * jb - 1024 * ic2, 0), 1024)
                segs_by_jb = {}
                bank_events = {0: [], 1: [], 2: [], 3: []}
                for jb in range(16):
                    wf = _wf(jb)
                    wfp = _wf(jb - 1) if jb > 0 else 0
                    segs = []
                    cuts = sorted({0, wf, min(wfp, wf), 512, 1024} & set(range(0, wf + 1))
                                  | {0, min(wfp, wf), wf}
                                  | {c for c in (512, 1024) if c < wf})
                    cuts = sorted(set([0, min(wfp, wf), wf] +
                                      [c for c in (512, 1024) if 0 < c < wf]))
                    for a2, b2 in zip(cuts, cuts[1:]):
                        if b2 > a2:
                            segs.append(("f", 1024 + a2, 1024 + b2))
                    a2 = wf
                    while a2 < 1024:
                        b2 = min((a2 // 512 + 1) * 512, 1024)
                        segs.append(("p", a2, b2))
                        a2 = b2
                    segs_by_jb[jb] = segs
                    for k, (kind, ca, cb2) in enumerate(segs):
                        for bk in range(ca // 512, (cb2 - 1) // 512 + 1):
                            bank_events[bk].append((jb, k))
                first_ev = {bk: (ev[0] if ev else None) for bk, ev in bank_events.items()}
                last_ev = {bk: (ev[-1] if ev else None) for bk, ev in bank_events.items()}
                wfmax = _wf(15)
                if wfmax < 1024:
                    a2 = wfmax
                    while a2 < 1024:
                        b2 = min((a2 // 512 + 1) * 512, 1024)
                        em("pe", lambda e, a=a2, b=b2: e.matmul(
                            psPV[:, 1024 + a:1024 + b], z65[:], ones_row[0:1, 0:(b - a)],
                            start=True, stop=True),
                            waits=w(trk["psPV_free"]))
                        a2 = b2
                exp_done = {}
                pv_done = {}
                evac_done = {}

                def emit_qk(jb):
                    off = 128 * jb - 1024 * ic2
                    wf = min(max(off, 0), 1024)
                    buf = jb % 2
                    bs_a, bs_b = wf, min(off + 128, 1024)
                    has_bias = bs_b > bs_a
                    if jb >= 2:
                        wqk = [evac_done[jb - 2]]
                    else:
                        wqk = w(trk["qkbuf_free"][buf], trk["psQ_free"],
                                trk["QK_free"])
                    segh = {0: has_bias and min(bs_b, 512) > max(bs_a, 0),
                            1: has_bias and min(bs_b, 1024) > max(bs_a, 512)}
                    for half in range(2):
                        ha, hb = half * 512, (half + 1) * 512
                        seg_a, seg_b = max(bs_a, ha), min(bs_b, hb)
                        has_seg = segh[half]
                        is_last_qk = (half == 1 and not has_seg)
                        em("pe", lambda e, jb=jb, half=half, buf=buf, s=s, ic2=ic2,
                           st=not has_seg: e.matmul(
                            psQ[:, buf * 1024 + half * 512: buf * 1024 + (half + 1) * 512],
                            Kf[:, s * T + jb * 128: s * T + (jb + 1) * 128],
                            Qf[:, s * T + ic2 * 1024 + half * 512:
                               s * T + ic2 * 1024 + (half + 1) * 512],
                            start=True, stop=st),
                            waits=wqk if half == 0 else (),
                            inc=(S(f"qk{buf}"), 1) if is_last_qk else None)
                        wqk[:] = []
                        if has_seg:
                            u0 = seg_a - off
                            em("pe", lambda e, buf=buf, h=h, a=seg_a, b=seg_b, u0=u0:
                               e.matmul(
                                psQ[:, buf * 1024 + a: buf * 1024 + b],
                                ldI[:, h * 128:(h + 1) * 128],
                                Drelu0[:, u0:u0 + (b - a)],
                                start=False, stop=True),
                                waits=[(S("vLDI"), 1), (S("cDRELU"), 1)]
                                if (hp == 0 and s == 0 and ic2 == 0) else (),
                                inc=(S(f"qk{buf}"), 1) if half == 1 else None)
                    pQK = (S(f"qk{buf}"), S(f"qk{buf}").n)
                    wev = w(pQK, trk["sbuf_free"][buf])
                    em("act", lambda e, buf=buf: e.activation(
                        scoresb[buf][:], psQ[:, buf * 1024: buf * 1024 + 1024],
                        AF.Copy),
                        waits=wev, inc=(S(f"ev{buf}"), 1))
                    evac_done[jb] = (S(f"ev{buf}"), S(f"ev{buf}").n)
                    trk["qkbuf_free"][buf] = evac_done[jb]
                    em("act", lambda e, buf=buf, s=s: e.activation(
                        scoresb[buf][:], scoresb[buf][:], AF.Exp, bias=mneg[:, s:s + 1]),
                        waits=w(ss_mneg if (s == 0 and ic2 == 0 and jb == 0)
                                else None),
                        inc=(S(f"ex{buf}"), 1))
                    exp_done[jb] = (S(f"ex{buf}"), S(f"ex{buf}").n)

                def emit_pv(jb):
                    buf = jb % 2
                    first_mm = True
                    for k, (kind, ca, cb2) in enumerate(segs_by_jb[jb]):
                        start = first_ev[ca // 512] == (jb, k)
                        stop = last_ev[ca // 512] == (jb, k)
                        wpv = []
                        if first_mm:
                            wpv = [exp_done[jb]]
                            if jb == 0:
                                wpv += w(trk["psPV_free"], v_vt)
                        vt = vfut3 if kind == "f" else vpast3
                        ea = ca - 1024 if kind == "f" else ca
                        eb = cb2 - 1024 if kind == "f" else cb2
                        em("pe", lambda e, jb=jb, buf=buf, ca=ca, cb2=cb2, ea=ea, eb=eb,
                           vt=vt, start=start, stop=stop: e.matmul(
                            psPV[:, ca:cb2], vt[:, jb, :], scoresb[buf][:, ea:eb],
                            start=start, stop=stop), waits=wpv,
                            inc=(S(f"pv{buf}"), 1) if k == len(segs_by_jb[jb]) - 1 else None)
                        first_mm = False
                    pv_done[jb] = (S(f"pv{buf}"), S(f"pv{buf}").n)
                    trk["sbuf_free"][buf] = pv_done[jb]

                for jb in range(17):
                    if jb < 16:
                        emit_qk(jb)
                    if jb >= 1:
                        emit_pv(jb - 1)
                em("act", lambda e: e.activation(ppsb[:], psPV[:, 0:1024], AF.Copy),
                   waits=w(pv_done[15], pv_done[14], trk["pfpp_free"]))
                em("act", lambda e: e.activation(pfsb[:], psPV[:, 1024:2048], AF.Copy),
                   inc=(S("aPP"), 1))
                aPV = (S("aPP"), S("aPP").n)
                trk["psPV_free"] = aPV
                em("pool", lambda e, h=h, ic2=ic2: e.dma_start(
                    out=cbc[:],
                    in_=d_c16.ap()[h, ic2 * 1024:(ic2 + 1) * 1024].partition_broadcast(65)),
                    waits=w((S("sC16"), 16) if (hp == 0 and s == 0 and ic2 == 0)
                            else None, trk["cbc_free"]),
                    inc=(S("sCB"), 16), drain=False, is_dma=True)
                ss_cbc = (S("sCB"), S("sCB").n)
                em("dve", lambda e: e.tensor_tensor(pfsb[64:65, :], cbc[64:65, :],
                                                    pfsb[64:65, :], ALU.mult),
                   waits=w(aPV, ss_cbc))
                em("dve", lambda e: e.tensor_tensor(ppsb[64:65, :], pfsb[64:65, :],
                                                    ppsb[64:65, :], ALU.add))
                em("dve", lambda e: e.reciprocal(pfsb[64:65, :], ppsb[64:65, :]))
                em("dve", lambda e: e.tensor_tensor(cbc[0:64, :], cbc[0:64, :],
                                                    pfsb[0:64, :], ALU.mult))
                em("dve", lambda e, hp=hp, s=s, ic2=ic2: e.tensor_tensor(
                    OG[hp][s * 64:(s + 1) * 64, ic2 * 1024:(ic2 + 1) * 1024],
                    cbc[0:64, :], ppsb[0:64, :], ALU.add), inc=(S("vEPI"), 1))
                vDen = (S("vEPI"), S("vEPI").n)
                trk["cbc_free"] = vDen
                em("pool", lambda e, h=h, ic2=ic2: e.dma_start(
                    out=d_r16.ap()[h:h + 1, ic2 * 1024:(ic2 + 1) * 1024],
                    in_=pfsb[64:65, :]), waits=[vDen], inc=(S("sR16"), 16),
                    drain=False, is_dma=True)
                ss_r = (S("sR16"), S("sR16").n)
                trk["pfpp_free"] = w(vDen, ss_r)
                em("pool", lambda e, h=h, ic2=ic2: e.dma_start(
                    out=rbc[:],
                    in_=d_r16.ap()[h, ic2 * 1024:(ic2 + 1) * 1024].partition_broadcast(128)),
                    waits=w(ss_r, trk["rbc_free"]), inc=(S("gRBC"), 16), drain=False,
                    is_dma=True)
                ss_rbc2 = (S("gRBC"), S("gRBC").n)
                em("dve", lambda e, hp=hp, s=s, ic2=ic2: e.tensor_tensor(
                    OG[hp][s * 64:(s + 1) * 64, ic2 * 1024:(ic2 + 1) * 1024],
                    OG[hp][s * 64:(s + 1) * 64, ic2 * 1024:(ic2 + 1) * 1024],
                    rbc[s * 64:(s + 1) * 64, :], ALU.mult),
                    waits=[ss_rbc2], inc=(S("vOG"), 1))
                trk["rbc_free"] = (S("vOG"), S("vOG").n)
            trk["vt_free"] = pv_done[15]
        trk["QK_free"] = (w(pv_done[15], (S("sDBQ"), 48))
                          if (KDBG and hp == 0) else pv_done[15])
        trk["mneg_free"] = exp_done[15]
        trk["psQ_free"] = [evac_done[14], evac_done[15]]

    og_done = (S("vOG"), S("vOG").n)
    if KDBG:
        em("pool", lambda e: e.dma_start(out=dbg["og0p"], in_=OG[0][:]),
           waits=[og_done], inc=(S("sDBG"), 16), drain=False, is_dma=True)
    for hp in range(8):
        g_g = stream_weight(wT["wg"][:, hp * 128:(hp + 1) * 128])
        pG = proj_fm(g_g, Y, "pG")
        em("act", lambda e, hp=hp: e.activation(gate_t[:], psQ[:], AF.Silu,
                                                bias=sb_b["bg"][:, hp:hp + 1]),
           waits=w(pG, trk["gate_free"] if "gate_free" in trk else None),
           inc=(S("aG"), 1))
        a_g = (S("aG"), S("aG").n)
        trk["psQ_free"] = a_g
        em("dve", lambda e, hp=hp: e.reduce_sum(gnst[:, 0:1], OG[hp][:], AX.X),
           waits=w(og_done if hp == 0 else None))
        em("act", lambda e, hp=hp: e.activation(t64a[:], OG[hp][0:64, :], AF.Square),
           waits=w(og_done if hp == 0 else None, trk["t64_free"]))
        em("act", lambda e, hp=hp: e.activation(t64b[:], OG[hp][64:128, :], AF.Square),
           inc=(S("aGSQ"), 1))
        aGN = (S("aGSQ"), S("aGSQ").n)
        em("dve", lambda e: e.reduce_sum(gnst[0:64, 1:2], t64a[:], AX.X), waits=[aGN])
        em("dve", lambda e: e.reduce_sum(gnst[64:128, 1:2], t64b[:], AX.X))
        em("dve", lambda e: e.tensor_copy(gnstb[:], gnst[:]), inc=(S("vGST"), 1))
        vGN = (S("vGST"), S("vGST").n)
        trk["t64_free"] = vGN
        em("pe", lambda e: e.matmul(psPV[:, 0:2], gnfold[:], gnstb[:],
                                    start=True, stop=True),
           waits=w(vGN, (S("cGNF"), 1), trk["psPV_free"]), inc=(S("pGN"), 1))
        em("dve", lambda e: e.tensor_copy(gn2[:, 0:2], psPV[:, 0:2]),
           waits=[(S("pGN"), S("pGN").n)], inc=(S("vGN2"), 1))
        trk["psPV_free"] = (S("vGN2"), S("vGN2").n)
        NL = float(T * HD)
        em("dve", lambda e: e.tensor_scalar(gn2[:, 0:2], gn2[:, 0:2], 1.0 / NL, None,
                                            ALU.mult))
        em("dve", lambda e: e.tensor_tensor(gn2[:, 2:3], gn2[:, 0:1], gn2[:, 0:1],
                                            ALU.mult))
        em("dve", lambda e: e.tensor_tensor(gn2[:, 3:4], gn2[:, 1:2], gn2[:, 2:3],
                                            ALU.subtract), inc=(S("vGVAR"), 1))
        em("act", lambda e: e.activation(gn2[:, 3:4], gn2[:, 3:4], AF.Sqrt, bias=ceps[:]),
           waits=[(S("vGVAR"), S("vGVAR").n), (S("cP"), 1)], inc=(S("aGSD"), 1))
        em("dve", lambda e: e.reciprocal(gn2[:, 3:4], gn2[:, 3:4]),
           waits=[(S("aGSD"), S("aGSD").n)])
        em("dve", lambda e: e.tensor_copy(gn2[:, 2:3], gn2[:, 0:1]), inc=(S("vGR"), 1))
        vRstd = (S("vGR"), S("vGR").n)
        em("pool", lambda e: e.dma_start(out=d_sc.ap()[0:1, 4:6], in_=gn2[0:1, 2:4]),
           waits=[vRstd], inc=(S("sGN"), 16), drain=False, is_dma=True)
        em("pool", lambda e: e.dma_start(out=d_sc.ap()[0:1, 6:8], in_=gn2[64:65, 2:4]),
           inc=(S("sGN"), 16), drain=False, is_dma=True)
        em("pool", lambda e: e.dma_start(
            out=gsc[0:64, 0:2], in_=d_sc.ap()[0, 4:6].partition_broadcast(64)),
           waits=[(S("sGN"), S("sGN").n)], inc=(S("sGN2"), 16),
           drain=False, is_dma=True)
        em("pool", lambda e: e.dma_start(
            out=gsc[64:128, 0:2], in_=d_sc.ap()[0, 6:8].partition_broadcast(64)),
           inc=(S("sGN2"), 16), drain=False, is_dma=True)
        em("dve", lambda e, hp=hp: e.tensor_scalar(gsc[:, 1:2],
                                                   sb_b["gamma"][:, hp:hp + 1],
                                                   gsc[:, 1:2], None, ALU.mult),
           waits=[(S("sGN2"), S("sGN2").n)])
        em("dve", lambda e: e.tensor_tensor(gsc[:, 0:1], gsc[:, 0:1], gsc[:, 1:2],
                                            ALU.mult))
        em("dve", lambda e, hp=hp: e.scalar_tensor_tensor(
            gsc[:, 0:1], gsc[:, 0:1], -1.0, sb_b["beta"][:, hp:hp + 1], ALU.mult, ALU.add),
            inc=(S("vGSC"), 1))
        em("dve", lambda e, hp=hp: e.tensor_scalar(OG[hp][:], OG[hp][:], gsc[:, 1:2],
                                                   gsc[:, 0:1], ALU.mult, ALU.add),
           waits=[(S("vGSC"), S("vGSC").n)] +
                 ([(S("sDBG"), 16)] if (KDBG and hp == 0) else []))
        em("dve", lambda e, hp=hp: e.tensor_tensor(OG[hp][:], OG[hp][:], gate_t[:],
                                                   ALU.mult), waits=[a_g],
           inc=(S("vAPP"), 1))
        trk["gate_free"] = (S("vAPP"), S("vAPP").n)
    v_att = (S("vAPP"), S("vAPP").n)

    vO = [None, None, None, None]
    for oc in range(NEC):
        g_o = stream_weight(wT["wo"][:, oc * 128:(oc + 1) * 128])
        for ic in range(4):
            for c in range(NEC):
                em("pe", lambda e, c=c, ic=ic: e.matmul(
                    psQ[:, ic * 512:(ic + 1) * 512],
                    wbf[:, c * 128:(c + 1) * 128],
                    OG[c][:, ic * 512:(ic + 1) * 512],
                    start=(c == 0), stop=(c == NEC - 1)),
                    waits=(w(g_o, v_att if oc == 0 else None,
                             trk["psQ_free"] if oc == 0 and ic == 0 else None,
                             vO[ic] if oc > 0 else None) if c == 0 else ()),
                    inc=(S(f"pO{ic}"), 1) if c == NEC - 1 else None)
            pO = (S(f"pO{ic}"), S(f"pO{ic}").n)
            em("dve", lambda e, oc=oc, ic=ic: e.scalar_tensor_tensor(
                X[oc][:, ic * 512:(ic + 1) * 512], psQ[:, ic * 512:(ic + 1) * 512],
                sb_b["bo"][0:128, oc:oc + 1], X[oc][:, ic * 512:(ic + 1) * 512],
                ALU.add, ALU.add), waits=[pO], inc=(S(f"vO{ic}"), 1))
            vO[ic] = (S(f"vO{ic}"), S(f"vO{ic}").n)
        trk["wbf_free"] = (S("pO3"), S("pO3").n)

    if KDBG:
        ddone = w(v_att, (S("vOG"), S("vOG").n))
        for nm, tile in (("y0", Y[0][:]), ("ld", ld16[:]), ("qf", Qf[:]), ("kf", Kf[:]),
                         ("vp", Vp[:]), ("og0", OG[0][:]), ("mneg", mneg[:]),
                         ("cos", cos64[:]), ("sin", sinS[:]), ("gate", gate_t[:]),
                         ("pf", pfsb[:]), ("pp", ppsb[:])) + tuple(
                             (f"yc{c}", Y[c][:]) for c in range(8)):
            em("pool", lambda e, nm=nm, tile=tile: e.dma_start(out=dbg[nm], in_=tile),
               waits=ddone, inc=(S("sDBG"), 16), drain=False, is_dma=True)
            ddone = ()

    if KSTOP == "attn":
        for c in range(NEC):
            em("sp", lambda e, c=c: e.dma_start(out=yout[c * 128:(c + 1) * 128, :],
                                                in_=X[c][:]),
               waits=[vO[0], vO[1], vO[2], vO[3]] if c == 0 else (), inc=(S("sO"), 16))
        em("sp", lambda e: e.nop(), waits=[(S("sO"), S("sO").n)] +
           ([(S("sDBG"), S("sDBG").n), (S("sDBQ"), S("sDBQ").n),
             (S("sDBP"), S("sDBP").n), (S("sDBP2"), S("sDBP2").n)] if KDBG else []),
           drain=False)
    else:
        ctxB.close()
        ctxFP = ExitStack()
        FP = ctxFP.enter_context
        psF1 = FP(PSL("psF1", [128, T], F32))
        psF2 = FP(PSL("psF2", [128, T], F32))
        ctxF2 = ExitStack()
        F2 = ctxF2.enter_context
        iA = F2(SBR("iA", [128, T], I32))
        iA2 = F2(SBR("iA2", [128, T], I32))
        iA3 = F2(SBR("iA3", [128, T], I32))
        scb0 = F2(SBR("scb0", [128, 2], F32))
        scb1 = F2(SBR("scb1", [128, 4], F32))
        scb2 = F2(SBR("scb2", [128, 2], F32))
        scb3 = F2(SBR("scb3", [128, 2], F32))
        mx0 = F2(SBR("mx0", [128, 1], F32))
        mxu = F2(SBR("mxu", [128, 1], F32))
        mxg = F2(SBR("mxg", [128, 1], F32))
        mx2 = F2(SBR("mx2", [128, 1], F32))
        mx3 = F2(SBR("mx3", [128, 1], F32))
        tmpr = F2(SBR("tmpr", [128, 1], F32))
        mrow = F2(SBR("mrow", [1, 128], F32))
        sscl = F2(SBR("sscl", [1, 20], F32))
        XW = w(vO[0], vO[1], vO[2], vO[3],
               (S("sDBG"), S("sDBG").n) if KDBG else None,
               (S("sDBQ"), S("sDBQ").n) if KDBG else None)

        for c in range(NEC):
            em("sp", lambda e, c=c: e.dma_start(
                out=d_x1.ap()[c * 128:(c + 1) * 128, :], in_=X[c][:]),
                waits=XW if c == 0 else (), inc=(S("sLx"), 16))
        sX1 = (S("sLx"), S("sLx").n)
        for m in (mx0, mxu, mxg, mx2, mx3):
            em("pool", lambda e, m=m: e.memset(m[:], 0.0),
               waits=XW if m is mx0 else ())
        em("pool", lambda e: e.nop(), inc=(S("cPM"), 1), drain=True)
        mZ = (S("cPM"), S("cPM").n)

        ctxDQ = ExitStack()
        DQ = ctxDQ.enter_context
        wtF = DQ(SBR("wtF", [128, FF], F32))
        iW = DQ(SBR("iW", [128, FF], I32))
        wqF = DQ(SBR("wqF", [128, FF], F32))
        wqT = DQ(SBR("wqTs", [128, FF], BF16))
        s4t = DQ(SBR("s4t", [128, 96], F32))
        rin4 = DQ(SBR("rin4", [128, 96], F32))
        sI = DQ(SBR("sI", [128, 128], F32))
        em("pool", lambda e: e.memset(sI[:], 1.0), waits=XW)
        em("pool", lambda e: e.affine_select(sI[:], sI[:], pattern=[[1, 128]],
                                             compare_op=ALU.is_equal, fill=0.0,
                                             base=0, channel_multiplier=-1),
           inc=(S("cIOTA"), 1))
        nID = S("cIOTA").n
        prev_wq = None
        prev_tr = None

        def dequant_chunk(src2d, ncols, tcol_of_j, first):
            nonlocal prev_wq, prev_tr
            global_es = {}
            nj = ncols // 128
            nb = ncols // 32
            em("sp", lambda e, src2d=src2d, ncols=ncols: e.dma_start(
                out=wtF[:, 0:ncols], in_=src2d),
               waits=w(XW if first else None, prev_wq, prev_tr),
               inc=(S("sLw"), 16))
            em("dve", lambda e, ncols=ncols, nb=nb: e.tensor_reduce(
                s4t[:, 0:nb].rearrange("p (b one) -> p b one", one=1),
                wtF[:, 0:ncols].rearrange("p (b k) -> p b k", k=32),
                AX.X, ALU.max, apply_absolute_value=True),
               waits=w((S("sLw"), S("sLw").n), prev_tr), inc=(S("vNRM"), 1))
            em("dve", lambda e, nb=nb: e.tensor_scalar(
                s4t[:, 0:nb], s4t[:, 0:nb], 1.0 / 7.0, None, ALU.mult))
            em("dve", lambda e, nb=nb: e.tensor_scalar(
                s4t[:, 0:nb], s4t[:, 0:nb], 1e-12, None, ALU.max))
            em("dve", lambda e, nb=nb: e.reciprocal(rin4[:, 0:nb], s4t[:, 0:nb]))
            em("dve", lambda e, ncols=ncols, nb=nb: e.tensor_tensor(
                iW[:, 0:ncols].rearrange("p (b k) -> p b k", k=32),
                wtF[:, 0:ncols].rearrange("p (b k) -> p b k", k=32),
                rin4[:, 0:nb].rearrange("p (b one) -> p b one", one=1)
                .broadcast_to((128, nb, 32)),
                ALU.mult))
            em("dve", lambda e, ncols=ncols, nb=nb: e.tensor_tensor(
                wqF[:, 0:ncols].rearrange("p (b k) -> p b k", k=32),
                iW[:, 0:ncols].rearrange("p (b k) -> p b k", k=32),
                s4t[:, 0:nb].rearrange("p (b one) -> p b one", one=1)
                .broadcast_to((128, nb, 32)),
                ALU.mult),
               inc=(S("vCOMB"), 1))
            prev_wq = (S("vCOMB"), S("vCOMB").n)
            for j in range(nj):
                pslice = (psF1[:, 0:128] if j % 2 == 0 else psF1[:, 512:640])
                em("pe", lambda e, j=j, ps=pslice: e.transpose(
                    ps, wqF[:, j * 128:(j + 1) * 128], sI[:]),
                    waits=w(prev_wq if j == 0 else None,
                            XW if (first and j <= 1) else None,
                            (S("cIOTA"), nID) if (first and j == 0) else None,
                            global_es.get(j - 2)),
                    inc=(S("pROT"), 1))
                em("act", lambda e, j=j, ps=pslice, tc=tcol_of_j(j): e.activation(
                    wqT[:, tc:tc + 128], ps, AF.Copy),
                    waits=[(S("pROT"), S("pROT").n)], inc=(S("aROT"), 1))
                global_es[j] = (S("aROT"), S("aROT").n)
            prev_tr = (S("aROT"), S("aROT").n)

        gW = {}
        for nm, src, ncols, nch in (("up", wupNs, E, 3), ("ga", wgaNs, E, 3),
                                    ("do", wdoNs, FF, 1)):
            for ch in range(nch):
                tcol = (lambda j, ch=ch: j * 384 + ch * 128) if nch == 3 else \
                       (lambda j: j * 128)
                dequant_chunk(src[ch * 128:(ch + 1) * 128, :], ncols, tcol,
                              prev_wq is None)
            em("sp", lambda e, nm=nm: e.dma_start(
                out=g_w_in[nm].ap()[0:1, :].rearrange("one (p o) -> p o", p=128),
                in_=wqT[:]),
               waits=[(S("aROT"), S("aROT").n)], inc=(S("sCB"), 16))
            em("pool", lambda e, nm=nm: e.collective_compute(
                "AllGather", ALU.bypass, replica_groups=[list(range(8))],
                ins=[g_w_in[nm].ap().opt()], outs=[g_w_out[nm].ap().opt()]),
                waits=[(S("sCB"), S("sCB").n)], inc=(S("gRBC"), 1), drain=False,
                is_dma=True)
            gW[nm] = S("gRBC").n
            prev_tr = w(prev_tr, (S("sCB"), S("sCB").n))
        ctxDQ.close()

        ctxLN = ExitStack()
        LN = ctxLN.enter_context
        tl2 = {"cb": [LN(SBR(f"fcb{k}", [128, T], BF16)) for k in range(2)],
               "sqb": [LN(SBR(f"fsq{k}", [128, T], BF16)) for k in range(2)],
               "selA": LN(SBR("fselA", [128, 65], BF16)),
               "selB": LN(SBR("fselB", [128, 65], BF16)),
               "Rbc": LN(SBR("fRbc", [128, T], F32)),
               "Nbc": LN(SBR("fNbc", [128, T], F32)),
               "tmpc": LN(SBR("ftmpc", [128, T], F32)),
               "row_m": LN(SBR("frm", [1, T], F32)),
               "row_v": LN(SBR("frv", [1, T], F32)),
               "psStat": psF2[0:65, :]}
        w_y2 = emit_ln("L1", X, Y, XW, XW, tl2)

        def emit_barrier(k, mxs, wait_extra):
            n = len(mxs)
            for i, mx in enumerate(mxs):
                em("sp", lambda e, k=k, i=i, mx=mx: e.dma_start(
                    out=d_red.ap()[2 * k + i:2 * k + i + 1, :], in_=mx[:]),
                    waits=wait_extra if i == 0 else (), inc=(S("sGN"), 16))
            sA = (S("sGN"), S("sGN").n)
            for i in range(n):
                em("sp", lambda e, k=k, i=i: e.dma_start(
                    out=mrow[:], in_=d_red.ap()[2 * k + i:2 * k + i + 1, :]),
                    waits=w(sA if i == 0 else None,
                            (S("vGR"), S("vGR").n) if i > 0 else None),
                    inc=(S("sGN"), 16))
                em("dve", lambda e, i=i: e.reduce_max(sscl[0:1, i:i + 1], mrow[:],
                                                      AX.X),
                   waits=[(S("sGN"), S("sGN").n)], inc=(S("vGR"), 1))
            em("sp", lambda e, k=k, n=n: e.dma_start(out=cc_in[k].ap()[0:1, 0:n],
                                                     in_=sscl[0:1, 0:n]),
               waits=[(S("vGR"), S("vGR").n)], inc=(S("sGN"), 16))
            em("pool", lambda e, k=k: e.collective_compute(
                "AllReduce", ALU.max, replica_groups=[list(range(8))],
                ins=[cc_in[k].ap().opt()], outs=[cc_out[k].ap().opt()]),
                waits=[(S("sGN"), S("sGN").n)], inc=(S("gRBC"), 1), drain=False,
                is_dma=True)
            em("sp", lambda e, k=k, n=n: e.dma_start(out=sscl[0:1, 4:4 + n],
                                                     in_=cc_out[k].ap()[0:1, 0:n]),
               waits=[(S("gRBC"), S("gRBC").n)], inc=(S("sGN"), 16))
            return (S("sGN"), S("sGN").n)

        def bscal(fn, waits=(), inc=None):
            em("dve", fn, waits=waits, inc=inc)

        def brd(k, src_lanes, dst, n):
            for ci, ln in enumerate(src_lanes):
                em("sp", lambda e, k=k, ci=ci, ln=ln: e.dma_start(
                    out=d_sc2.ap()[k:k + 1, ci:ci + 1], in_=sscl[0:1, ln:ln + 1]),
                    waits=[(S("vGR"), S("vGR").n)] if ci == 0 else (),
                    inc=(S("sGN2"), 16))
            em("sp", lambda e, k=k, n=n, dst=dst: e.dma_start(
                out=dst[:], in_=d_sc2.ap()[k, 0:n].partition_broadcast(128)),
               waits=[(S("sGN2"), S("sGN2").n)], inc=(S("sGN2"), 16))
            return (S("sGN2"), S("sGN2").n)

        for c in range(NEC):
            em("dve", lambda e, c=c: e.reduce_max(tmpr[:], Y[c][:], AX.X,
                                                  apply_absolute_value=True),
               waits=w(w_y2, mZ) if c == 0 else ())
            em("dve", lambda e: e.tensor_tensor(mx0[:], mx0[:], tmpr[:], ALU.max),
               inc=(S("vYM"), 1) if c == NEC - 1 else None)
        bB0 = emit_barrier(0, [mx0], [(S("vYM"), S("vYM").n)])
        bscal(lambda e: e.tensor_scalar(sscl[0:1, 8:9], sscl[0:1, 4:5],
                                        1.0 / 32767.0, None, ALU.mult),
              waits=[bB0])
        bscal(lambda e: e.tensor_scalar(sscl[0:1, 8:9], sscl[0:1, 8:9], 1e-12,
                                        None, ALU.max))
        bscal(lambda e: e.reciprocal(sscl[0:1, 9:10], sscl[0:1, 8:9]),
              inc=(S("vGR"), 1))
        s0done = brd(0, [8, 9], scb0, 2)
        for c in range(NEC):
            em("dve", lambda e, c=c: e.tensor_scalar(iA[:], Y[c][:], scb0[:, 1:2],
                                                     None, ALU.mult),
               waits=[s0done] if c == 0 else ())
            em("dve", lambda e, c=c: e.tensor_scalar(Y[c][:], iA[:], scb0[:, 0:1],
                                                     None, ALU.mult),
               inc=(S("vYM"), 1))
        xqDone = (S("vYM"), S("vYM").n)

        ctxLN.close()
        ctxS = ExitStack()
        FS_ = ctxS.enter_context
        wtu = [FS_(SBR(f"wtu{k}", [128, E], BF16)) for k in range(2)]
        wtg = [FS_(SBR(f"wtg{k}", [128, E], BF16)) for k in range(2)]
        upsg = [FS_(SBR(f"upsg{k}", [128, T], BF16)) for k in range(2)]
        gasg = [FS_(SBR(f"gasg{k}", [128, T], BF16)) for k in range(2)]
        gup3 = g_w_out["up"].ap().rearrange("c (p ec o) -> c p ec o", p=128, ec=8)
        gga3 = g_w_out["ga"].ap().rearrange("c (p ec o) -> c p ec o", p=128, ec=8)
        NOB = FF // 128
        upmm, gamm, upev, gaev, updm, gadm = {}, {}, {}, {}, {}, {}
        upmx, gamx = {}, {}
        for ob in range(NOB):
            sl = ob % 2
            em("sp", lambda e, ob=ob, sl=sl: e.dma_start(
                out=wtu[sl][:].rearrange("p (c o) -> p c o", o=128),
                in_=gup3[ob // 3, :, :, (ob % 3) * 128:(ob % 3) * 128 + 128]),
                waits=w((S("gRBC"), gW["up"]) if ob == 0 else None, upmm.get(ob - 2)),
                inc=(S("ex0" if sl == 0 else "ex1"), 16))
            wU = (S("ex0" if sl == 0 else "ex1"), S("ex0" if sl == 0 else "ex1").n)
            em("sp", lambda e, ob=ob, sl=sl: e.dma_start(
                out=wtg[sl][:].rearrange("p (c o) -> p c o", o=128),
                in_=gga3[ob // 3, :, :, (ob % 3) * 128:(ob % 3) * 128 + 128]),
                waits=w((S("gRBC"), gW["ga"]) if ob == 0 else None, gamm.get(ob - 2)),
                inc=(S("pv0" if sl == 0 else "pv1"), 16))
            wG = (S("pv0" if sl == 0 else "pv1"), S("pv0" if sl == 0 else "pv1").n)
            for ic in range(4):
                for c in range(NEC):
                    em("pe", lambda e, sl=sl, ic=ic, c=c: e.matmul(
                        psF1[:, ic * 512:(ic + 1) * 512],
                        wtu[sl][:, c * 128:(c + 1) * 128],
                        Y[c][:, ic * 512:(ic + 1) * 512],
                        start=(c == 0), stop=(c == NEC - 1)),
                        waits=(w(wU, xqDone if ob == 0 else None,
                                 (S("aROT"), S("aROT").n) if ob == 0 else None,
                                 upev.get(ob - 1))
                               if (ic == 0 and c == 0) else ()),
                        inc=(S("pQK"), 1) if (ic == 3 and c == NEC - 1) else None)
            upmm[ob] = (S("pQK"), S("pQK").n)
            for ic in range(4):
                for c in range(NEC):
                    em("pe", lambda e, sl=sl, ic=ic, c=c: e.matmul(
                        psF2[:, ic * 512:(ic + 1) * 512],
                        wtg[sl][:, c * 128:(c + 1) * 128],
                        Y[c][:, ic * 512:(ic + 1) * 512],
                        start=(c == 0), stop=(c == NEC - 1)),
                        waits=(w(wG, xqDone if ob == 0 else None,
                                 (S("L1ROWS"), S("L1ROWS").n) if ob == 0 else None,
                                 gaev.get(ob - 1))
                               if (ic == 0 and c == 0) else ()),
                        inc=(S("pG"), 1) if (ic == 3 and c == NEC - 1) else None)
            gamm[ob] = (S("pG"), S("pG").n)
            em("act", lambda e, sl=sl: e.activation(upsg[sl][:], psF1[:], AF.Copy),
               waits=w(upmm[ob], updm.get(ob - 2), upmx.get(ob - 2)),
               inc=(S("aQK"), 1))
            upev[ob] = (S("aQK"), S("aQK").n)
            em("act", lambda e, sl=sl: e.activation(gasg[sl][:], psF2[:], AF.Copy),
               waits=w(gamm[ob], gadm.get(ob - 2), gamx.get(ob - 2)),
               inc=(S("aPP"), 1))
            gaev[ob] = (S("aPP"), S("aPP").n)
            em("act", lambda e, ob=ob, sl=sl: e.dma_start(
                out=d_up.ap()[ob * 128:(ob + 1) * 128, :], in_=upsg[sl][:]),
                waits=[upev[ob]], inc=(S("sO"), 16), drain=False, is_dma=True)
            updm[ob] = (S("sO"), S("sO").n)
            em("act", lambda e, ob=ob, sl=sl: e.dma_start(
                out=d_ga.ap()[ob * 128:(ob + 1) * 128, :], in_=gasg[sl][:]),
                waits=[gaev[ob]], inc=(S("sR16"), 16), drain=False, is_dma=True)
            gadm[ob] = (S("sR16"), S("sR16").n)
            em("dve", lambda e, sl=sl: e.reduce_max(tmpr[:], upsg[sl][:], AX.X,
                                                    apply_absolute_value=True),
               waits=w(upev[ob], mZ if ob == 0 else None))
            em("dve", lambda e: e.tensor_tensor(mxu[:], mxu[:], tmpr[:], ALU.max),
               inc=(S("aNQ"), 1))
            upmx[ob] = (S("aNQ"), S("aNQ").n)
            em("dve", lambda e, sl=sl: e.reduce_max(tmpr[:], gasg[sl][:], AX.X,
                                                    apply_absolute_value=True),
               waits=[gaev[ob]])
            em("dve", lambda e: e.tensor_tensor(mxg[:], mxg[:], tmpr[:], ALU.max),
               inc=(S("vNM"), 1))
            gamx[ob] = (S("vNM"), S("vNM").n)

        bB1 = emit_barrier(1, [mxu, mxg],
                           [(S("aNQ"), S("aNQ").n), (S("vNM"), S("vNM").n)])
        bscal(lambda e: e.tensor_scalar(sscl[0:1, 8:10], sscl[0:1, 4:6],
                                        1.0 / 32767.0, None, ALU.mult),
              waits=[bB1])
        bscal(lambda e: e.tensor_scalar(sscl[0:1, 8:10], sscl[0:1, 8:10], 1e-12,
                                        None, ALU.max))
        bscal(lambda e: e.reciprocal(sscl[0:1, 10:12], sscl[0:1, 8:10]))
        bscal(lambda e: e.tensor_scalar(sscl[0:1, 12:13], sscl[0:1, 9:10],
                                        float(2.0 ** -16), None, ALU.mult),
              inc=(S("vGR"), 1))
        s1done = brd(1, [11, 9, 12, 10], scb1, 4)

        ctxS.close()
        ctxA.close()
        ctxM = ExitStack()
        M_ = ctxM.enter_context
        garaw = [M_(SBL(f"gar{k}", [128, T], BF16)) for k in range(NOB)]
        dwt = [M_(SBL(f"dwt{k}", [128, 512], BF16)) for k in range(2)]
        garld = {}
        phase5_done = w(upmm[NOB - 1], gamm[NOB - 1], upev[NOB - 1], gaev[NOB - 1],
                        (S("sR16"), S("sR16").n), (S("sO"), S("sO").n))
        g2p = {}
        ia_done, sg_done, ib_done = {}, {}, {}

        def g2_stageA(ob):
            iA_t = iA if ob % 2 == 0 else iA3
            Xp = X[1 + ob % 2]
            em("pool", lambda e, ob=ob: e.dma_start(
                out=garaw[ob][:], in_=d_ga.ap()[ob * 128:(ob + 1) * 128, :]),
                waits=(phase5_done + [s1done] if ob == 0 else []),
                inc=(S("vGST"), 16), drain=False, is_dma=True)
            garld[ob] = (S("vGST"), S("vGST").n)
            em("dve", lambda e, ob=ob, iA_t=iA_t: e.tensor_scalar(
                iA_t[:], garaw[ob][:], scb1[:, 0:1], None, ALU.mult),
               waits=w(garld[ob], sX1 if ob == 0 else None),
               inc=(S("qk0"), 1))
            ia_done[ob] = (S("qk0"), S("qk0").n)
            em("act", lambda e, iA_t=iA_t, Xp=Xp: e.activation(
                Xp[:], iA_t[:], AF.Sigmoid, scale=scb1[:, 1:2]),
               waits=w(ia_done[ob], ib_done.get(ob - 2)), inc=(S("aG"), 1))
            sg_done[ob] = (S("aG"), S("aG").n)

        def g2_stageB(ob):
            iA_t = iA if ob % 2 == 0 else iA3
            iB_t = iA2
            Xp = X[1 + ob % 2]
            em("dve", lambda e, iB_t=iB_t, Xp=Xp: e.tensor_scalar(
                iB_t[:], Xp[:], 65536.0, 65535.0, ALU.mult, ALU.min),
               waits=[sg_done[ob]], inc=(S("qk0"), 1))
            ib_done[ob] = (S("qk0"), S("qk0").n)
            em("dve", lambda e, ob=ob, iA_t=iA_t, iB_t=iB_t: e.scalar_tensor_tensor(
                garaw[ob][:], iA_t[:], scb1[:, 2:3], iB_t[:], ALU.mult, ALU.mult))
            em("dve", lambda e, ob=ob: e.reduce_max(tmpr[:], garaw[ob][:], AX.X,
                                                    apply_absolute_value=True))
            em("dve", lambda e: e.tensor_tensor(mx2[:], mx2[:], tmpr[:], ALU.max),
               inc=(S("vGVAR"), 1))
            g2p[ob] = (S("vGVAR"), S("vGVAR").n)

        for ob in range(NOB + 1):
            if ob < NOB:
                g2_stageA(ob)
            if ob >= 1:
                g2_stageB(ob - 1)

        bB2 = emit_barrier(2, [mx2], [(S("vGVAR"), S("vGVAR").n)])
        bscal(lambda e: e.tensor_scalar(sscl[0:1, 13:14], sscl[0:1, 4:5],
                                        1.0 / 32767.0, None, ALU.mult),
              waits=[bB2])
        bscal(lambda e: e.tensor_scalar(sscl[0:1, 13:14], sscl[0:1, 13:14], 1e-12,
                                        None, ALU.max))
        bscal(lambda e: e.reciprocal(sscl[0:1, 14:15], sscl[0:1, 13:14]))
        bscal(lambda e: e.tensor_tensor(sscl[0:1, 15:16], sscl[0:1, 13:14],
                                        sscl[0:1, 8:9], ALU.mult),
              inc=(S("vGR"), 1))
        s2done = brd(2, [14, 15], scb2, 2)

        odm, ow, op_ = {}, {}, {}
        for ob in range(NOB):
            sl = ob % 2
            em("pool", lambda e, ob=ob, sl=sl: e.dma_start(
                out=X[6 + sl][:], in_=d_up.ap()[ob * 128:(ob + 1) * 128, :]),
                waits=w((S("sO"), S("sO").n) if ob == 0 else None,
                        op_.get(ob - 2)),
                inc=(S("aLD" if sl == 0 else "vLDN"), 16), drain=False, is_dma=True)
            lU = (S("aLD" if sl == 0 else "vLDN"),
                  S("aLD" if sl == 0 else "vLDN").n)
            em("dve", lambda e, sl=sl: e.tensor_scalar(iA[:], X[6 + sl][:],
                                                       scb1[:, 3:4], None, ALU.mult),
               waits=w(lU, s2done if ob == 0 else None, g2p.get(NOB - 1) if ob == 0 else None),
               inc=(S("ev0"), 1))
            op_[ob] = (S("ev0"), S("ev0").n)
            em("dve", lambda e, ob=ob: e.tensor_scalar(iA2[:], garaw[ob][:],
                                                       scb2[:, 0:1], None, ALU.mult))
            em("dve", lambda e, ob=ob: e.scalar_tensor_tensor(
                garaw[ob][:], iA[:], scb2[:, 1:2], iA2[:], ALU.mult, ALU.mult),
               inc=(S("ev0"), 1))
            em("dve", lambda e, ob=ob: e.reduce_max(tmpr[:], garaw[ob][:], AX.X,
                                                    apply_absolute_value=True))
            em("dve", lambda e: e.tensor_tensor(mx3[:], mx3[:], tmpr[:], ALU.max),
               inc=(S("aGSD"), 1))

        bB3 = emit_barrier(3, [mx3], [(S("aGSD"), S("aGSD").n)])
        bscal(lambda e: e.tensor_scalar(sscl[0:1, 16:17], sscl[0:1, 4:5],
                                        1.0 / 32767.0, None, ALU.mult),
              waits=[bB3])
        bscal(lambda e: e.tensor_scalar(sscl[0:1, 16:17], sscl[0:1, 16:17], 1e-12,
                                        None, ALU.max))
        bscal(lambda e: e.reciprocal(sscl[0:1, 17:18], sscl[0:1, 16:17]),
              inc=(S("vGR"), 1))
        s3done = brd(3, [17, 16], scb3, 2)

        oq = {}
        for ob in range(NOB):
            em("dve", lambda e, ob=ob: e.tensor_scalar(iA[:], garaw[ob][:],
                                                       scb3[:, 0:1], None, ALU.mult),
               waits=w(s3done if ob == 0 else None,
                       (S("aGSD"), S("aGSD").n) if ob == 0 else None,
                       oq.get(ob - 1)),
               inc=(S("ev0"), 1))
            em("act", lambda e, ob=ob: e.activation(garaw[ob][:], iA[:],
                                                    AF.Identity, scale=scb3[:, 1:2]),
               waits=[(S("ev0"), S("ev0").n)], inc=(S("vGSC"), 1))
            oq[ob] = (S("vGSC"), S("vGSC").n)

        gdo3 = g_w_out["do"].ap().rearrange("c (p ic o) -> c p ic o", p=128, ic=24)
        dmm, dev = {}, {}
        for o2c in range(NEC):
            ps = psF1 if o2c % 2 == 0 else psF2
            for g in range(NOB // 4):
                sl = g % 2
                prev_use = (dmm.get((o2c, 4 * g - 5)) if g >= 2
                            else (dmm.get((o2c - 1, 4 * g + 19))
                                  if o2c > 0 else None))
                em("sp", lambda e, g=g, sl=sl, o2c=o2c: e.dma_start(
                    out=dwt[sl][:].rearrange("p (j o) -> p j o", o=128),
                    in_=gdo3[o2c, :, 4 * g:4 * g + 4, :]),
                    waits=w((S("gRBC"), gW["do"]) if (o2c == 0 and g == 0) else None,
                            prev_use),
                    inc=(S("aV" if sl == 0 else "pV"), 16))
                lW = (S("aV" if sl == 0 else "pV"),
                      S("aV" if sl == 0 else "pV").n)
                for j in range(4):
                    i24 = 4 * g + j
                    for ic in range(4):
                        em("pe", lambda e, i24=i24, j=j, sl=sl, ic=ic, ps=ps: e.matmul(
                            ps[:, ic * 512:(ic + 1) * 512],
                            dwt[sl][:, j * 128:(j + 1) * 128],
                            garaw[i24][:, ic * 512:(ic + 1) * 512],
                            start=(i24 == 0), stop=(i24 == NOB - 1)),
                            waits=(w(lW if j == 0 else None,
                                     oq[i24] if o2c == 0 else None,
                                     dev.get(o2c - 2) if j == 0 else None,
                                     (S("aQK"), S("aQK").n) if (o2c == 0 and i24 == 0) else None,
                                     (S("aPP"), S("aPP").n) if (o2c == 1 and i24 == 0) else None,
                                     (S("L1ROWS"), S("L1ROWS").n) if (o2c == 1 and i24 == 0) else None)
                                   if ic == 0 else ()),
                            inc=(S("pNRM"), 1) if ic == 3 else None)
                    dmm[(o2c, i24)] = (S("pNRM"), S("pNRM").n)
            xsl = 5 + o2c % 2
            em("sp", lambda e, o2c=o2c, xsl=xsl: e.dma_start(
                out=X[xsl][:], in_=d_x1.ap()[o2c * 128:(o2c + 1) * 128, :]),
                waits=w(sX1 if o2c == 0 else None, dev.get(o2c - 2)),
                inc=(S("sLp"), 16))
            ysl = 3 + o2c % 2
            em("dve", lambda e, ps=ps, xsl=xsl, ysl=ysl: e.tensor_tensor(
                X[ysl][:], ps[:], X[xsl][:], ALU.add),
               waits=w(dmm[(o2c, NOB - 1)], (S("sLp"), S("sLp").n),
                       dev.get(o2c - 2)),
               inc=(S("vOG"), 1))
            dv = (S("vOG"), S("vOG").n)
            em("pool", lambda e, o2c=o2c, ysl=ysl: e.dma_start(
                out=yout[o2c * 128:(o2c + 1) * 128, :], in_=X[ysl][:]),
                waits=[dv], inc=(S("vAPP"), 16), drain=False, is_dma=True)
            dev[o2c] = (S("vAPP"), S("vAPP").n)
        em("sp", lambda e: e.nop(), waits=[(S("vAPP"), S("vAPP").n)] +
           ([(S("sDBG"), S("sDBG").n), (S("sDBQ"), S("sDBQ").n),
             (S("sDBP"), S("sDBP").n), (S("sDBP2"), S("sDBP2").n)] if KDBG else []),
           drain=False)
        ctxM.close()
        ctxF2.close()
        ctxFP.close()

    with ctx0, ctxA, ctxB:
        with nc.Block() as block:
            @block.sync
            def _(sp):
                EM.run("sp", sp)

            @block.tensor
            def _(pe):
                EM.run("pe", pe)

            @block.vector
            def _(dve):
                EM.run("dve", dve)

            @block.scalar
            def _(act):
                EM.run("act", act)

            @block.gpsimd
            def _(pool):
                EM.run("pool", pool)
    return nc


def kernel(**inputs):
    g = lambda k: np.ascontiguousarray(np.asarray(inputs[k], dtype=np.float32))
    x = g("L0_attn_x")
    TT_ = lambda a: np.ascontiguousarray(a.T)
    base = {
        "wqT": TT_(g("L0_attn_wq")), "wkT": TT_(g("L0_attn_wk")),
        "wvT": TT_(g("L0_attn_wv")), "wgT": TT_(g("L0_attn_wg")),
        "woT": TT_(g("L0_attn_wo")), "wdT": TT_(g("L0_attn_wd")),
        "bq": np.ascontiguousarray(g("L0_attn_bq").reshape(NEC, 128).T),
        "bk": np.ascontiguousarray(g("L0_attn_bk").reshape(NEC, 128).T),
        "bg": np.ascontiguousarray(g("L0_attn_bg").reshape(NEC, 128).T),
        "bo": np.ascontiguousarray(g("L0_attn_bo").reshape(NEC, 128).T),
        "gamma": np.ascontiguousarray(g("L0_attn_gamma").reshape(NEC, 128).T),
        "beta": np.ascontiguousarray(g("L0_attn_beta").reshape(NEC, 128).T),
        "bv": g("L0_attn_bv").reshape(1, E),
        "bd": g("L0_attn_bd").reshape(H, 1),
    }
    wupN = g("L0_ffn_w_up")
    wgaN = g("L0_ffn_w_gate")
    wdoN = g("L0_ffn_w_down")
    nc = build_nc()
    in_maps = []
    for b in range(8):
        m = dict(base)
        m["x"] = TT_(x[b])
        m["wupNs"] = np.ascontiguousarray(wupN[b * 384:(b + 1) * 384, :])
        m["wgaNs"] = np.ascontiguousarray(wgaN[b * 384:(b + 1) * 384, :])
        m["wdoNs"] = np.ascontiguousarray(wdoN[b * 128:(b + 1) * 128, :])
        in_maps.append(m)
    res = run_bass_kernel_spmd(nc, in_maps, core_ids=list(range(8)))
    out = np.stack([np.asarray(res.results[b]["y"]).T for b in range(8)])
    return out.astype(np.float32)
